# version 16
# baseline (speedup 1.0000x reference)
import os
import numpy as np
import ml_dtypes
from contextlib import ExitStack
import concourse.bass as bass
import concourse.mybir as mybir
from concourse.bass_utils import run_bass_kernel_spmd

F32 = mybir.dt.float32
BF16 = mybir.dt.bfloat16
AF = mybir.ActivationFunctionType
ALU = mybir.AluOpType
AX = mybir.AxisListType

L = 2048
D = 1024
ENGINES = ("sync", "vector", "scalar", "tensor", "gpsimd")


class Planner:

    def __init__(self):
        self.nc = bass.Bass("TRN2", target_bir_lowering=False)
        self.ops = {e: [] for e in ENGINES}
        self.ctx = ExitStack()
        self._uid = 0
        self.barrier_tile = None
        self.dma_count = {e: 0 for e in ENGINES}

    def name(self, base):
        self._uid += 1
        return f"{base}_{self._uid}"

    def sb(self, base, shape, dtype=F32):
        return self.ctx.enter_context(self.nc.sbuf_tensor(self.name(base), shape, dtype))

    def scope(self):
        return ExitStack()

    def sbs(self, scope, base, shape, dtype=F32):
        return scope.enter_context(self.nc.sbuf_tensor(self.name(base), shape, dtype))

    def psum(self, base, shape, dtype=F32):
        return self.ctx.enter_context(self.nc.psum_tensor(self.name(base), shape, dtype))

    def emit(self, eng, fn, deps=(), dma=False, sdeps=()):
        ds = []
        stack = list(deps)
        while stack:
            d = stack.pop()
            if d is None:
                continue
            if isinstance(d, list):
                stack.extend(d)
            else:
                ds.append(d)
        sd = [d for d in sdeps if d is not None]
        h = (eng, len(self.ops[eng]))
        self.ops[eng].append({"fn": fn, "deps": ds, "dma": dma, "need": dma,
                              "sdeps": sd})
        return h

    def dma(self, eng, out, in_, deps=()):
        h = self.emit(eng, lambda e, o=out, i=in_: e.dma_start(out=o, in_=i),
                      deps, dma=True)
        self.dma_count[eng] += 16
        return h

    def barrier(self):
        tails = {e: (e, len(self.ops[e]) - 1) for e in ENGINES if self.ops[e]}
        snap = dict(self.dma_count)
        hs = {}
        for e in ENGINES:
            deps = [t for te, t in tails.items() if te != e]
            if e == "scalar" and e in tails:
                deps = deps + [tails[e]]
            if e in ("vector", "gpsimd"):
                hs[e] = self.emit(e, lambda en: en.engine_nop(), deps)
            elif e == "scalar" and self.barrier_tile is not None:
                hs[e] = self.emit(e, lambda en: en.copy(
                    self.barrier_tile[0:1, 0:1], self.barrier_tile[0:1, 0:1]), deps)
            else:
                hs[e] = self.emit(e, lambda en: en.nop(), deps)
            self.ops[e][hs[e][1]]["dsnap"] = snap
        return hs

    def finalize(self):
        nc = self.nc
        for e in ENGINES:
            for op in self.ops[e]:
                for (de, di) in op["deps"]:
                    self.ops[de][di]["need"] = True
                for (de, di) in op.get("sdeps", []):
                    self.ops[de][di]["need"] = True
        for e in ENGINES:
            cc = 0
            cd = 0
            for op in self.ops[e]:
                if op["need"]:
                    if op["dma"]:
                        op["inc"] = 16
                        cd += 16
                    else:
                        op["inc"] = 1
                        cc += 1
                else:
                    op["inc"] = 0
                op["cum"] = cd if op["dma"] else cc
        sems = {e: self.ctx.enter_context(self.nc.semaphore(f"sem_{e}"))
                for e in ENGINES}
        dsems = {e: self.ctx.enter_context(self.nc.semaphore(f"dsem_{e}"))
                 for e in ENGINES}
        block = self.ctx.enter_context(nc.Block())

        def runner(e):
            def run(eng):
                waited = {x: 0 for x in ENGINES}
                dwaited = {x: 0 for x in ENGINES}
                my_dma = 0
                for op in self.ops[e]:
                    if op["dma"] and my_dma > 0:
                        if my_dma > dwaited[e]:
                            eng.wait_ge(dsems[e], my_dma)
                            dwaited[e] = my_dma
                    for de2, v2 in op.get("dsnap", {}).items():
                        if v2 > dwaited[de2]:
                            eng.wait_ge(dsems[de2], v2)
                            dwaited[de2] = v2
                    for (de, di) in list(op["deps"]) + list(op.get("sdeps", [])):
                        p = self.ops[de][di]
                        if (de == e == "tensor" and not p["dma"]
                                and (de, di) not in op.get("sdeps", [])):
                            continue
                        v = p["cum"]
                        if p["dma"]:
                            if v > dwaited[de]:
                                eng.wait_ge(dsems[de], v)
                                dwaited[de] = v
                        else:
                            if v > waited[de]:
                                eng.wait_ge(sems[de], v)
                                waited[de] = v
                    inst = op["fn"](eng)
                    if op["inc"]:
                        inst.then_inc(dsems[e] if op["dma"] else sems[e], op["inc"])
                    if op["dma"]:
                        my_dma += 16
            return run

        block.sync(runner("sync"))
        block.vector(runner("vector"))
        block.scalar(runner("scalar"))
        block.tensor(runner("tensor"))
        block.gpsimd(runner("gpsimd"))
        return {e: len(self.ops[e]) for e in ENGINES}


class PsumPool:
    def __init__(self, tensors):
        self.free = [[t, []] for t in tensors]

    def acquire(self):
        t, dep = self.free.pop(0)
        return t, dep

    def release(self, t, deps):
        if not isinstance(deps, list):
            deps = [deps]
        self.free.append([t, deps])


class Builder:
    def __init__(self):
        self.P = Planner()
        nc = self.P.nc
        P = self.P
        self.inputs = {}

        self.xT_d = self.din("xT", [D, L])
        self.out_d = nc.dram_tensor("outT", [D, L], F32, kind="ExternalOutput").ap()

        self.pf = PsumPool([P.psum("pf", [128, 512]) for _ in range(5)])
        self.pb = PsumPool([P.psum("pb", [128, 1024], BF16) for _ in range(2)])
        self.pr = PsumPool([P.psum("pr", [1, 512])])

        self.ones_col = P.sb("ones_col", [128, 1])
        self.ones_col_bf = P.sb("ones_col_bf", [128, 1], BF16)
        self.ones_row = P.sb("ones_row", [1, 128])
        self.ones_row_bf = P.sb("ones_row_bf", [1, 128], BF16)
        self.ones_chunk = P.sb("ones_chunk", [128, 128])
        self.eps_col = P.sb("eps_col", [1, 1])
        P.barrier_tile = self.eps_col
        cr = []
        cr.append(P.emit("gpsimd", lambda e: e.memset(self.ones_col[:], 1.0)))
        cr.append(P.emit("gpsimd", lambda e: e.memset(self.ones_col_bf[:], 1.0)))
        cr.append(P.emit("gpsimd", lambda e: e.memset(self.ones_row[:], 1.0)))
        cr.append(P.emit("gpsimd",
                         lambda e: e.memset(self.ones_row_bf[:], 1.0)))
        cr.append(P.emit("gpsimd", lambda e: e.memset(self.ones_chunk[:], 1.0)))
        cr.append(P.emit("gpsimd", lambda e: e.memset(self.eps_col[:], 1e-5)))
        self.idty_d = self.din("c_idty", [128, 128])
        self.idty_bf = P.sb("idty_bf", [128, 128], BF16)
        cr.append(P.dma("gpsimd", self.idty_bf[:], self.idty_d[:]))
        self.cr = cr

        self.x = [P.sb("x", [128, L]) for _ in range(8)]
        self.last_x = [P.dma("sync", self.x[i][:], self.xT_d[128 * i:128 * (i + 1), :])
                       for i in range(8)]

    def din(self, name2, shape):
        ap = self.P.nc.dram_tensor(name2, shape, F32, kind="ExternalInput").ap()
        self.inputs[name2] = shape
        return ap

    def load_w_bf(self, dram_ap, rows, cols, deps=(), scope=None):
        if scope is None:
            t = self.P.sb("w", [128, rows // 128, cols], BF16)
        else:
            t = self.P.sbs(scope, "w", [128, rows // 128, cols], BF16)
        h = self.P.dma("gpsimd", t.ap(),
                       dram_ap.rearrange("(c p) n -> p c n", p=128), deps)
        return t, h

    def finish(self):
        self.P.barrier()
        sts = [self.P.dma("sync", self.out_d[128 * i:128 * (i + 1), :], self.x[i][:])
               for i in range(8)]
        self.P.emit("sync", lambda e: e.nop(), sts)
        return self.P.finalize()

    def cast_bf(self, dst_tiles, cs, deps_per_tile, tag=""):
        hs = []
        for i in range(8):
            eng = "vector" if i % 2 else "scalar"
            if eng == "vector":
                fn = (lambda e, o=dst_tiles[i], s=self.x[i], c=cs:
                      e.tensor_copy(o[:], s[:, c] if c is not None else s[:]))
            else:
                fn = (lambda e, o=dst_tiles[i], s=self.x[i], c=cs:
                      e.copy(o[:], s[:, c] if c is not None else s[:]))
            hs.append(self.P.emit(eng, fn, [deps_per_tile[i]]))
        return hs

    def proj(self, weight, w_dep, rhs_tiles, rhs_dep, cs, ncols, out_cb):
        hs = []
        for oc in range(8):
            pt, pd = self.pf.acquire()
            mm = None
            for k in range(8):
                mm = self.P.emit(
                    "tensor",
                    lambda e, p=pt, w=weight, kk=k, o=oc, r=rhs_tiles[k], c=cs, n=ncols:
                    e.matmul(p[:, 0:n], lhsT=w[:, kk, 128 * o:128 * (o + 1)],
                             rhs=(r[:, c] if c is not None else r[:]),
                             start=(kk == 0), stop=(kk == 7)),
                    [w_dep, rhs_dep[k]] + ([pd] if k == 0 else [mm]))
            ev = out_cb(oc, pt, mm)
            self.pf.release(pt, ev)
            hs.append(ev if not isinstance(ev, list) else ev[-1])
        return hs

    def bcast_row(self, row_ap, n, dst_ap, deps, f32=False):
        pt, pd = self.pf.acquire()
        ones = self.ones_row if f32 else self.ones_row_bf
        mm = self.P.emit("tensor", lambda e, p=pt, o1=ones: e.matmul(
            p[:, 0:n], lhsT=o1[0:1, :], rhs=row_ap,
            start=True, stop=True), deps + [pd, self.cr[2], self.cr[-1]])
        ev = self.P.emit("vector", lambda e, p=pt: e.tensor_copy(dst_ap, p[:, 0:n]),
                         [mm])
        self.pf.release(pt, ev)
        return ev

    def emit_l0_attn(self):
        P = self.P
        x = self.x
        C = 128
        NS = int(os.environ.get("DEV_NS", L // C))
        STAGE = int(os.environ.get("DEV_STAGE", 99))
        H, DK, DV = 16, 64, 64
        SCALE = DK ** -0.5
        LV = ("qs8s", "qs8e", "qs16", "qs32", "qs64", "ks8e", "ks16", "ks32", "ks64")
        sc = P.scope()

        wq, wq_h = self.load_w_bf(self.din("l0_wq", [D, D]), D, D, scope=sc)
        wf, wf_h = self.load_w_bf(self.din("l0_wf", [D, D]), D, D, scope=sc)
        wi, wi_h = self.load_w_bf(self.din("l0_wi", [D, D]), D, D, scope=sc)
        wo, wo_h = self.load_w_bf(self.din("l0_wo", [D, D]), D, D, scope=sc)
        gm_d = self.din("c_gmask", [3 * 128, 128])
        masks = P.sbs(sc, "gmask", [128, 3, 128])
        mask_h = P.dma("sync", masks.ap(), gm_d.rearrange("(m p) n -> p m n", p=128))
        m8d, m8o, m16 = masks[:, 0, :], masks[:, 1, :], masks[:, 2, :]

        Ft = [P.sbs(sc, "l0_F", [128, C]) for _ in range(8)]
        Bt = [P.sbs(sc, "l0_B", [128, 132]) for _ in range(8)]
        Gt = [P.sbs(sc, "l0_G", [128, C]) for _ in range(8)]
        Kt = [P.sbs(sc, "l0_LNK", [128, C]) for _ in range(8)]
        eb = [P.sbs(sc, "l0_eb", [128, C], BF16) for _ in range(8)]
        xbf = [P.sbs(sc, "l0_xbf", [128, C], BF16) for _ in range(8)]
        vtm = P.sbs(sc, "l0_vtm", [128, D], BF16)
        ksw = [P.sbs(sc, "l0_ksw", [128, 5, 128], BF16) for _ in range(8)]
        qsw = [P.sbs(sc, "l0_qsw", [128, 6, 128], BF16) for _ in range(8)]
        wtmp = [P.sbs(sc, "l0_wtmp", [128, 6, 128]) for _ in range(3)]
        wtmp_h = [None] * 3

        class _Lane:
            def __init__(self, t, lane):
                self.t, self.lane = t, lane

            def __getitem__(self, idx):
                if isinstance(idx, tuple):
                    rw, cc = idx
                else:
                    rw, cc = slice(None), slice(None)
                return self.t[rw, self.lane, cc]
        KLANE = {"ks8e": 0, "ks16": 1, "ks32": 2, "ks64": 3}
        QLANE = {"qs8s": 0, "qs8e": 1, "qs16": 2, "qs32": 3, "qs64": 4}
        lv = {n: [_Lane(ksw[i], KLANE[n]) for i in range(8)] for n in KLANE}
        lv.update({n: [_Lane(qsw[i], QLANE[n]) for i in range(8)] for n in QLANE})
        Zq = [_Lane(qsw[i], 5) for i in range(8)]
        KH = [_Lane(ksw[i], 4) for i in range(8)]
        KHtm = P.sbs(sc, "l0_khtm", [128, D], BF16)
        Sf = [P.sbs(sc, "l0_S", [128, DV]) for _ in range(8)]
        Sbf = [P.sbs(sc, "l0_Sbf", [128, DV], BF16) for _ in range(8)]
        wcol = [P.sbs(sc, "l0_wcol", [128, 1]) for _ in range(8)]
        osb = [P.sbs(sc, "l0_o", [128, C], BF16) for _ in range(8)]
        a_sb = {n: P.sbs(sc, f"l0_{n}", [128, 128], BF16)
                for n in ("a8d", "a8o", "a16", "a64", "a32")}
        rqb = P.sbs(sc, "l0_rqb", [128, C])
        rowf = P.sbs(sc, "l0_rowf", [1, C])
        rowq = P.sbs(sc, "l0_rowq", [1, C])
        rowqb = P.sbs(sc, "l0_rowqb", [1, C], BF16)

        s_init = []
        b_zero = []
        for i in range(8):
            s_init.append(P.emit("gpsimd", lambda e, s=Sf[i]: e.memset(s[:], 0.0)))
            s_init.append(P.emit("gpsimd", lambda e, s=Sbf[i]: e.memset(s[:], 0.0)))
            b_zero.append(P.emit("gpsimd", lambda e, b=Bt[i]:
                                 e.memset(b[:, 0:1], 0.0)))
        s_init = s_init + b_zero
        sbf_h = {i: s_init for i in range(8)}
        sf_h = {i: s_init for i in range(8)}
        entry_x = list(self.last_x)
        war = {}
        ar_last = {n: [mask_h] for n in a_sb}

        xbf_pending = None
        for sl in range(NS):
            cs = slice(C * sl, C * (sl + 1))
            if xbf_pending is not None:
                xbf_h = xbf_pending
            else:
                xbf_h = self.cast_bf(
                    xbf, cs,
                    [[entry_x[i]] + war.get("xbf", []) for i in range(8)])
            if STAGE < 1:
                bars = self.P.barrier()
                self.last_x = [bars["vector"]] * 8
                continue

            F_h = [None] * 8
            ef_h = [None] * 8

            def f_evac(oc, pt, mm):
                h1 = P.emit("vector", lambda e, p=pt, o=Ft[oc]:
                            e.tensor_copy(o[:], p[:, 0:C]),
                            [mm, war.get(("ft", oc))])
                h2 = P.emit("scalar", lambda e, p=pt, o=eb[oc]:
                            e.activation(out=o[:], in_=p[:, 0:C], func=AF.Exp),
                            [mm, h1, war.get("eb")])
                F_h[oc] = h1
                ef_h[oc] = h2
                return [h1, h2]
            self.proj(wf, wf_h, xbf, xbf_h, None, C, f_evac)

            if STAGE < 12:
                bars = self.P.barrier()
                self.last_x = [bars["vector"]] * 8
                continue
            pt, pd = self.pr.acquire()
            mm = None
            for k in range(8):
                mm = P.emit("tensor", lambda e, p=pt, r=eb[k], kk=k:
                            e.matmul(p[0:1, 0:C], lhsT=self.ones_col_bf[:, 0:1],
                                     rhs=r[:], start=(kk == 0), stop=(kk == 7)),
                            [ef_h[k], self.cr[1]] + ([pd] if k == 0 else [mm]))
            sumf_mm = mm
            lsf_h = P.emit("scalar", lambda e, p=pt: e.activation(
                out=rowf[:], in_=p[0:1, 0:C], func=AF.Ln),
                [mm, war.get("rowf")])
            self.pr.release(pt, lsf_h)
            pt2, pd2 = self.pf.acquire()
            mmb = P.emit("tensor", lambda e, p=pt2: e.matmul(
                p[:, 0:C], lhsT=self.ones_row[0:1, :], rhs=rowf[0:1, :],
                start=True, stop=True), [lsf_h, pd2, self.cr[2]])
            g_h = [P.emit("vector", lambda e, p=pt2, f=Ft[i]:
                          e.tensor_sub(f[:], f[:], p[:, 0:C]), [mmb, F_h[i]])
                   for i in range(8)]
            self.pf.release(pt2, g_h)

            if STAGE < 13:
                bars = self.P.barrier()
                self.last_x = [bars["vector"]] * 8
                continue
            B_h = {}
            for i in range(8):
                B_h[i] = P.emit(
                    "vector", lambda e, f=Ft[i], b=Bt[i]:
                    e.tensor_tensor_scan(
                        out=b[:, 1:129], data0=self.ones_chunk[:, 0:128],
                        data1=f[:], initial=0.0, op0=ALU.mult, op1=ALU.add),
                    [g_h[i], self.cr[3], war.get(("bt", i))])
            H_h = [None] * 8
            ft_read = [None] * 8
            for i in range(8):
                h1 = P.emit("scalar", lambda e, f=Ft[i], o=eb[i]:
                            e.activation(out=o[:], in_=f[:], func=AF.Exp),
                            [g_h[i], sumf_mm])
                ft_read[i] = h1
                h2 = P.emit("scalar", lambda e, s=eb[i], o=Kt[i]:
                            e.activation(out=o[:], in_=s[:], func=AF.Ln,
                                         bias=self.ones_col[:, 0:1], scale=-1.0),
                            [h1, self.cr[0], war.get(("kt", i))])
                H_h[i] = P.emit("vector", lambda e, kk=Kt[i], b=Bt[i]:
                                e.tensor_sub(kk[:], kk[:], b[:, 1:129]),
                                [h2, B_h[i]])

            ksw_h = [None] * 8
            kh1s = [None] * 8
            for i in range(8):
                slot = i % 3
                t = wtmp[slot]
                h1s = []
                for lane, ssz in enumerate((8, 16, 32, 64)):
                    nblk = 128 // ssz
                    bb = Bt[i][:, ssz::ssz][:, 0:nblk].unsqueeze(2) \
                        .broadcast_to([128, nblk, ssz])
                    eng2 = "vector" if (i + lane) % 2 == 0 else "gpsimd"
                    h1s.append(P.emit(
                        eng2,
                        lambda e, t2=t, a=Kt[i], b=bb, ln2=lane, ss2=ssz:
                        e.tensor_tensor(
                            out=t2[:, ln2, :].rearrange("p (b s) -> p b s", s=ss2),
                            in0=a.ap().rearrange("p (b s) -> p b s", s=ss2),
                            in1=b, op=ALU.add),
                        [H_h[i], B_h[i], b_zero[i], wtmp_h[slot]]))
                h1s.append(P.emit(
                    "vector" if i % 2 else "gpsimd",
                    lambda e, t2=t, k2=Kt[i], b=Bt[i]:
                    e.tensor_scalar(out=t2[:, 4, :], in0=k2[:],
                                    scalar1=b[:, 128:129], scalar2=None,
                                    op0=ALU.add),
                    [H_h[i], B_h[i], wtmp_h[slot]]))
                ksw_h[i] = P.emit("scalar", lambda e, t2=t, o=ksw[i]:
                                  e.activation(out=o[:, 0:5, :], in_=t2[:, 0:5, :],
                                               func=AF.Exp),
                                  h1s + [war.get(("lvk", i))])
                wtmp_h[slot] = ksw_h[i]
                kh1s[i] = list(h1s)
            kh_h = ksw_h

            if STAGE < 14:
                bars = self.P.barrier()
                self.last_x = [bars["vector"]] * 8
                continue
            G_h = [None] * 8
            eq_h = [None] * 8

            def q_evac(oc, pt_, mmh):
                h1 = P.emit("vector", lambda e, p=pt_, o=Gt[oc], b=Bt[oc]:
                            e.tensor_add(o[:], p[:, 0:C], b[:, 1:129]),
                            [mmh, B_h[oc], war.get(("gt", oc))])
                h2 = P.emit("scalar", lambda e, p=pt_, o=eb[oc]:
                            e.activation(out=o[:], in_=p[:, 0:C], func=AF.Exp),
                            [mmh, H_h[oc], h1])
                G_h[oc] = h1
                eq_h[oc] = h2
                return [h1, h2]
            self.proj(wq, wq_h, xbf, xbf_h, None, C, q_evac)

            qsw_h = [None] * 8
            qh1s = [None] * 8
            for i in range(8):
                slot = i % 3
                t = wtmp[slot]
                h1s = []
                for lane, (ssz, off) in enumerate(
                        ((8, 0), (8, 8), (16, 0), (32, 0), (64, 0))):
                    nblk = 128 // ssz
                    bb = Bt[i][:, off::ssz][:, 0:nblk].unsqueeze(2) \
                        .broadcast_to([128, nblk, ssz])
                    eng2 = "vector" if (i + lane) % 2 == 0 else "gpsimd"
                    h1s.append(P.emit(
                        eng2,
                        lambda e, t2=t, a=Gt[i], b=bb, ln2=lane, ss2=ssz:
                        e.tensor_tensor(
                            out=t2[:, ln2, :].rearrange("p (b s) -> p b s", s=ss2),
                            in0=a.ap().rearrange("p (b s) -> p b s", s=ss2),
                            in1=b, op=ALU.subtract),
                        [G_h[i], B_h[i], b_zero[i], wtmp_h[slot]]))
                h1s.append(P.emit("vector" if i % 2 else "gpsimd",
                                  lambda e, t2=t, g2=Gt[i]:
                                  e.tensor_copy(t2[:, 5, :], g2[:]),
                                  [G_h[i], wtmp_h[slot]]))
                qsw_h[i] = P.emit("scalar", lambda e, t2=t, o=qsw[i]:
                                  e.activation(out=o[:], in_=t2[:], func=AF.Exp),
                                  h1s + [war.get(("lvq", i))])
                wtmp_h[slot] = qsw_h[i]
                qh1s[i] = list(h1s)
            lv_h = {}
            for i in range(8):
                for n in ("ks8e", "ks16", "ks32", "ks64"):
                    lv_h[(i, n)] = ksw_h[i]
                for n in ("qs8s", "qs8e", "qs16", "qs32", "qs64"):
                    lv_h[(i, n)] = qsw_h[i]

            pt, pd = self.pr.acquire()
            mmq = None
            for k in range(8):
                mmq = P.emit("tensor", lambda e, p=pt, r=eb[k], kk=k:
                             e.matmul(p[0:1, 0:C], lhsT=self.ones_col_bf[:, 0:1],
                                      rhs=r[:], start=(kk == 0), stop=(kk == 7)),
                             [eq_h[k], self.cr[1]] + ([pd] if k == 0 else [mmq]))
            rq1 = P.emit("vector", lambda e, p=pt:
                         e.reciprocal(rowq[0:1, :], p[0:1, 0:C]),
                         [mmq, war.get("rowq")])
            self.pr.release(pt, rq1)
            rq2 = P.emit("scalar", lambda e: e.mul(rowqb[:], rowq[:], SCALE),
                         [rq1, war.get("rowqb")])

            if STAGE < 15:
                bars = self.P.barrier()
                self.last_x = [bars["vector"]] * 8
                continue
            vtm_h = []
            for hf in range(2):
                pt, pd = self.pf.acquire()
                mm = None
                for k in range(8):
                    mm = P.emit(
                        "tensor",
                        lambda e, p=pt, kk=k, h=hf, c=cs:
                        e.matmul(p[:, 0:512], lhsT=xbf[kk][:],
                                 rhs=wi[:, kk, 512 * h:512 * (h + 1)],
                                 start=(kk == 0), stop=(kk == 7)),
                        [wi_h, xbf_h[k]] + ([pd] if k == 0 else [mm]))
                ev = P.emit("vector", lambda e, p=pt, h=hf:
                            e.tensor_copy(vtm[:, 512 * h:512 * (h + 1)],
                                          p[:, 0:512]), [mm, war.get("vtm")])
                self.pf.release(pt, ev)
                vtm_h.append(ev)

            if STAGE < 20:
                bars = self.P.barrier()
                self.last_x = [bars["vector"]] * 8
                continue
            ktm_h = []
            for i in range(8):
                ptb, pdb = self.pb.acquire()
                t1 = P.emit("tensor", lambda e, p=ptb, s=KH[i]:
                            e.transpose(p[:, 0:128], in_=s[:],
                                        identity=self.idty_bf[:]),
                            [kh_h[i], pdb, self.cr[5]])
                t2 = P.emit("vector", lambda e, p=ptb, i2=i:
                            e.tensor_copy(KHtm[:, 128 * i2:128 * (i2 + 1)],
                                          p[:, 0:128]),
                            [t1, war.get(("khtm", i))])
                self.pb.release(ptb, t2)
                ktm_h.append(t2)
            rqb_h = self.bcast_row(rowqb[0:1, :], C, rqb[:],
                                   [rq2, war.get("rqb")])
            if sl + 1 < NS and STAGE >= 99:
                xbf_pending = self.cast_bf(
                    xbf, slice(C * (sl + 1), C * (sl + 2)),
                    [[entry_x[i]] + list(vtm_h) for i in range(8)])

            if STAGE < 30:
                bars = self.P.barrier()
                self.last_x = [bars["vector"]] * 8
                continue
            o_h = {i: [] for i in range(8)}
            o_by_tile = {}
            for h in range(H):
                ti, half = h // 2, h % 2
                rows = slice(64 * half, 64 * (half + 1))
                tp = (64 * half, 0)
                hcols = slice(64 * h, 64 * (h + 1))
                pa = {}
                for nm2, lt, rt in (("a8d", "ks8e", "qs8e"),
                                    ("a8o", "ks8e", "qs8s"),
                                    ("a16", "ks16", "qs16")):
                    pt, pd = self.pf.acquire()
                    mm = P.emit(
                        "tensor",
                        lambda e, p=pt, l=lv[lt][ti], r=lv[rt][ti], rw=rows, tpp=tp:
                        e.matmul(p[:, 0:128], lhsT=l[rw, :], rhs=r[rw, :],
                                 start=True, stop=True, tile_position=tpp),
                        [lv_h[(ti, lt)], lv_h[(ti, rt)], pd])
                    pa[nm2] = (pt, mm)
                evs = {}
                for nm2, mk in (("a8d", m8d), ("a8o", m8o), ("a16", m16)):
                    evs[nm2] = P.emit(
                        "vector", lambda e, p=pa[nm2][0], m=mk, o=a_sb[nm2]:
                        e.tensor_tensor(out=o[:], in0=p[:, 0:128], in1=m,
                                        op=ALU.mult),
                        [pa[nm2][1]] + ar_last[nm2])
                    self.pf.release(pa[nm2][0], evs[nm2])
                pt64, pd64 = self.pf.acquire()
                mm64 = P.emit("tensor",
                              lambda e, p=pt64, l=lv["ks64"][ti], r=lv["qs64"][ti],
                              rw=rows, tpp=tp:
                              e.matmul(p[0:64, 64:128], lhsT=l[rw, 0:64],
                                       rhs=r[rw, 64:128],
                                       start=True, stop=True, tile_position=tpp),
                              [lv_h[(ti, "ks64")], lv_h[(ti, "qs64")], pd64])
                evs["a64"] = P.emit("scalar", lambda e, p=pt64:
                                    e.copy(a_sb["a64"][0:64, 64:128], p[0:64, 64:128]),
                                    [mm64] + ar_last["a64"])
                self.pf.release(pt64, evs["a64"])
                pt32, pd32 = self.pf.acquire()
                mm32a = P.emit("tensor",
                               lambda e, p=pt32, l=lv["ks32"][ti], r=lv["qs32"][ti],
                               rw=rows, tpp=tp:
                               e.matmul(p[0:32, 32:64], lhsT=l[rw, 0:32],
                                        rhs=r[rw, 32:64],
                                        start=True, stop=True, tile_position=tpp),
                               [lv_h[(ti, "ks32")], lv_h[(ti, "qs32")], pd32])
                mm32b = P.emit("tensor",
                               lambda e, p=pt32, l=lv["ks32"][ti], r=lv["qs32"][ti],
                               rw=rows, tpp=(64 * half, 64):
                               e.matmul(p[64:96, 96:128], lhsT=l[rw, 64:96],
                                        rhs=r[rw, 96:128],
                                        start=True, stop=True, tile_position=tpp),
                               [lv_h[(ti, "ks32")], lv_h[(ti, "qs32")], mm32a])
                ev32a = P.emit("scalar", lambda e, p=pt32:
                               e.copy(a_sb["a32"][0:32, 32:64], p[0:32, 32:64]),
                               [mm32a] + ar_last["a32"])
                ev32b = P.emit("scalar", lambda e, p=pt32:
                               e.copy(a_sb["a32"][64:96, 96:128], p[64:96, 96:128]),
                               [mm32b, ev32a])
                evs["a32"] = ev32b
                self.pf.release(pt32, [ev32a, ev32b])
                po, po_d = self.pf.acquire()
                mo = P.emit("tensor",
                            lambda e, p=po, s=Sbf[ti], z=Zq[ti], rw=rows,
                            tpp=(64 * half, 64 * half):
                            e.matmul(p[rw, 0:128], lhsT=s[rw, :], rhs=z[rw, :],
                                     start=True, stop=False, tile_position=tpp),
                            [sbf_h[ti], qsw_h[ti], po_d])
                intra = [("a8d", None, None), ("a8o", None, None),
                         ("a16", None, None),
                         ("a64", (0, 64), (64, 128)),
                         ("a32", (0, 32), (32, 64)),
                         ("a32b", (64, 96), (96, 128))]
                for idx2, (nm2, jr, ir) in enumerate(intra):
                    key = "a32" if nm2 == "a32b" else nm2
                    last = (idx2 == len(intra) - 1)
                    if jr is None:
                        fn = (lambda e, p=po, v=vtm, hc=hcols, rw=rows, o=a_sb[key],
                              lst=last, tpp=(0, 64 * half):
                              e.matmul(p[rw, 0:128], lhsT=v[:, hc], rhs=o[:],
                                       start=False, stop=lst, tile_position=tpp))
                    else:
                        j0, j1 = jr
                        i0, i1 = ir
                        fn = (lambda e, p=po, v=vtm, hc=hcols, rw=rows, o=a_sb[key],
                              j0=j0, j1=j1, i0=i0, i1=i1, lst=last,
                              tpp=(j0, 64 * half):
                              e.matmul(p[rw, i0:i1],
                                       lhsT=v[j0:j1, hc], rhs=o[j0:j1, i0:i1],
                                       start=False, stop=lst, tile_position=tpp))
                    mo_new = P.emit("tensor", fn, [vtm_h, evs[key]],
                                    sdeps=[mo] if mo else [])
                    mo = mo_new
                oev = P.emit("vector",
                             lambda e, p=po, o=osb[ti], rw=rows:
                             e.tensor_tensor(out=o[rw, :], in0=p[rw, 0:128],
                                             in1=rqb[rw, :], op=ALU.mult),
                             [mo, rqb_h, war.get(("osb", ti))] + o_h[ti])
                self.pf.release(po, oev)
                o_h[ti].append(oev)
                for nm2 in ("a8d", "a8o", "a16", "a64", "a32"):
                    ar_last[nm2] = [mo]
                o_by_tile.setdefault(ti, []).append(oev)

            def wo_evac(oc, pt_, mmh):
                ev = P.emit("vector", lambda e, p=pt_, s=x[oc], c=cs:
                            e.tensor_add(s[:, c], s[:, c], p[:, 0:C]),
                            [mmh, self.last_x[oc]])
                self.last_x[oc] = ev
                return ev
            wo_hs = self.proj(wo, wo_h, osb, [o_h[i] for i in range(8)],
                              None, C, wo_evac)

            wc_h = [None] * 8
            dec_h = [None] * 8
            for ti in range(8 if STAGE >= 40 else 0):
                wc = P.emit("scalar", lambda e, b=Bt[ti], w=wcol[ti]:
                            e.activation(out=w[:], in_=b[:, 128:129], func=AF.Exp),
                            [B_h[ti], war.get(("wcol", ti))])
                wc_h[ti] = wc
                dec = P.emit("scalar", lambda e, s=Sf[ti], w=wcol[ti]:
                             e.mul(s[:], s[:], w[:, 0:1]),
                             [wc] + sf_h[ti] + o_by_tile[ti])
                dec_h[ti] = dec
                sus = []
                for half in range(2):
                    h = 2 * ti + half
                    rows = slice(64 * half, 64 * (half + 1))
                    ps2, ps2d = self.pf.acquire()
                    msu = P.emit("tensor",
                                 lambda e, p=ps2, kt=KHtm, v=vtm, rw=rows,
                                 hc=slice(64 * h, 64 * (h + 1)),
                                 tpp=(0, 64 * half):
                                 e.matmul(p[rw, 0:64], lhsT=kt[:, hc],
                                          rhs=v[:, hc],
                                          start=True, stop=True, tile_position=tpp),
                                 [ktm_h[ti], vtm_h, ps2d])
                    su = P.emit("vector", lambda e, p=ps2, s=Sf[ti], rw=rows:
                                e.tensor_add(s[rw, :], s[rw, :], p[rw, 0:64]),
                                [msu, dec])
                    self.pf.release(ps2, su)
                    sus.append(su)
                sf_h[ti] = sus
                sbf_h[ti] = [P.emit("scalar", lambda e, s=Sf[ti], d=Sbf[ti]:
                                    e.copy(d[:], s[:]), sus)]

            if STAGE < 50:
                bars = self.P.barrier()
                self.last_x = [bars["vector"]] * 8
                for i in range(8):
                    sf_h[i] = [bars["vector"]]
                    sbf_h[i] = [bars["vector"]]
                continue
            war = {"xbf": list(vtm_h), "eb": [mmq], "rowf": [g_h[0]],
                   "rowq": [rq2], "rowqb": [rqb_h],
                   "rqb": [o_by_tile[ti][-1] for ti in range(8)],
                   "vtm": [o_by_tile[ti][-1] for ti in range(8)]
                   + [h for ti in range(8) for h in sf_h[ti]]}
            for i in range(8):
                war[("ft", i)] = [B_h[i], ft_read[i]]
                war[("bt", i)] = kh1s[i] + qh1s[i] + [wc_h[i]] + [G_h[i]]
                war[("kt", i)] = kh1s[i]
                war[("gt", i)] = qh1s[i]
                war[("lvk", i)] = o_by_tile[i] + [ktm_h[i]]
                war[("lvq", i)] = list(o_by_tile[i])
                war[("khtm", i)] = list(sf_h[i])
                war[("osb", i)] = [wo_hs[-1]]
                war[("wcol", i)] = [dec_h[i]]
        bars = self.P.barrier()
        self.last_x = [bars["vector"]] * 8
        sc.close()

    def emit_postln(self, tag):
        P = self.P
        x = self.x
        sc = P.scope()
        xbf = [P.sbs(sc, f"ln_xbf_{tag}", [128, 512], BF16) for _ in range(8)]
        sq = [P.sbs(sc, f"ln_sq_{tag}", [128, 512], BF16) for _ in range(2)]
        srow = P.sbs(sc, f"ln_srow_{tag}", [1, L])
        ssrow = P.sbs(sc, f"ln_ssrow_{tag}", [1, L])
        trow = P.sbs(sc, f"ln_t_{tag}", [1, L])
        srow_b = P.sbs(sc, f"ln_srow_b_{tag}", [1, L], BF16)
        ssrow_b = P.sbs(sc, f"ln_ssrow_b_{tag}", [1, L], BF16)
        rbs = [P.sbs(sc, f"ln_rb_{tag}", [128, 512]) for _ in range(2)]
        mbs = [P.sbs(sc, f"ln_mb_{tag}", [128, 512]) for _ in range(2)]
        seg_h = []
        prev_cast = {i: None for i in range(8)}
        prev_sq = {}
        for ct in range(4):
            cseg = slice(512 * ct, 512 * (ct + 1))
            cast_h = []
            for i in range(8):
                eng = "vector" if i % 2 else "scalar"
                if eng == "vector":
                    fn = (lambda e, o=xbf[i], s2=x[i], c=cseg:
                          e.tensor_copy(o[:], s2[:, c]))
                else:
                    fn = (lambda e, o=xbf[i], s2=x[i], c=cseg:
                          e.copy(o[:], s2[:, c]))
                cast_h.append(P.emit(eng, fn, [self.last_x[i], prev_cast[i]]))
            pt, pd = self.pr.acquire()
            pt2, pd2 = self.pf.acquire()
            mm1 = mm2 = None
            for i in range(8):
                mm1 = P.emit("tensor",
                             lambda e, p=pt, r=xbf[i], kk=i:
                             e.matmul(p[0:1, 0:512], lhsT=self.ones_col_bf[:, 0:1],
                                      rhs=r[:], start=(kk == 0), stop=(kk == 7)),
                             [cast_h[i], self.cr[1]] + ([pd] if i == 0 else [mm1]))
                sqt = sq[i % 2]
                sh = P.emit("scalar",
                            lambda e, o=sqt, s2=xbf[i]:
                            e.activation(out=o[:], in_=s2[:], func=AF.Square),
                            [cast_h[i], prev_sq.get(i % 2)])
                mm2 = P.emit("tensor",
                             lambda e, p=pt2, r=sqt, kk=i:
                             e.matmul(p[0:1, 0:512], lhsT=self.ones_col_bf[:, 0:1],
                                      rhs=r[:], start=(kk == 0), stop=(kk == 7)),
                             [sh, self.cr[1]] + ([pd2] if i == 0 else [mm2]))
                prev_sq[i % 2] = mm2
                prev_cast[i] = [mm1, sh]
            e1 = P.emit("vector", lambda e, p=pt, c=cseg:
                        e.tensor_copy(srow[0:1, c], p[0:1, 0:512]), [mm1])
            e2 = P.emit("vector", lambda e, p=pt2, c=cseg:
                        e.tensor_copy(ssrow[0:1, c], p[0:1, 0:512]), [mm2])
            self.pr.release(pt, e1)
            self.pf.release(pt2, e2)
            seg_h.append((e1, e2))
        newx = list(self.last_x)
        war = {0: [], 1: []}
        seg_hh = {}
        for ct in range(4):
            cseg = slice(512 * ct, 512 * (ct + 1))
            e1, e2 = seg_h[ct]
            h1 = P.emit("scalar", lambda e, c=cseg:
                        e.mul(srow[0:1, c], srow[0:1, c], 1.0 / D), [e1])
            h2 = P.emit("scalar", lambda e, c=cseg:
                        e.activation(out=trow[0:1, c], in_=srow[0:1, c],
                                     func=AF.Square), [h1])
            h3 = P.emit("vector", lambda e, c=cseg: e.scalar_tensor_tensor(
                out=ssrow[0:1, c], in0=ssrow[0:1, c], scalar=1.0 / D,
                in1=trow[0:1, c], op0=ALU.mult, op1=ALU.subtract), [h2, e2])
            h5 = P.emit("scalar", lambda e, c=cseg:
                        e.activation(out=ssrow[0:1, c], in_=ssrow[0:1, c],
                                     func=AF.Ln, bias=self.eps_col[0:1, 0:1]),
                        [h3, self.cr[4]])
            h6 = P.emit("scalar", lambda e, c=cseg:
                        e.activation(out=ssrow[0:1, c], in_=ssrow[0:1, c],
                                     func=AF.Exp, scale=-0.5), [h5])
            h78 = P.emit("vector", lambda e, c=cseg: e.scalar_tensor_tensor(
                out=srow[0:1, c], in0=srow[0:1, c], scalar=-1.0,
                in1=ssrow[0:1, c], op0=ALU.mult, op1=ALU.mult), [h6, h1])
            h9 = P.emit("scalar", lambda e, c=cseg:
                        e.copy(ssrow_b[0:1, c], ssrow[0:1, c]), [h6])
            h10 = P.emit("scalar", lambda e, c=cseg:
                         e.copy(srow_b[0:1, c], srow[0:1, c]), [h78])
            seg_hh[ct] = (h9, h10)
        for ct in range(4):
            cseg = slice(512 * ct, 512 * (ct + 1))
            h9, h10 = seg_hh[ct]
            b1 = self.bcast_row(ssrow_b[0:1, cseg], 512, rbs[ct % 2][:],
                                [h9] + war[ct % 2])
            b2 = self.bcast_row(srow_b[0:1, cseg], 512, mbs[ct % 2][:],
                                [h10] + war[ct % 2])
            applies = []
            for i in range(8):
                eng = "gpsimd" if i % 4 == 3 else "vector"
                a = P.emit(eng, lambda e, s2=x[i], c=cseg, r=rbs[ct % 2]:
                           e.tensor_mul(s2[:, c], s2[:, c], r[:]),
                           [b1, newx[i]])
                n2 = P.emit(eng, lambda e, s2=x[i], c=cseg, m=mbs[ct % 2]:
                            e.tensor_add(s2[:, c], s2[:, c], m[:]), [a, b2])
                applies.append(n2)
                newx[i] = n2
            war[ct % 2] = applies
        self.last_x = newx
        bars = self.P.barrier()
        self.last_x = [bars["vector"]] * 8
        sc.close()

    def emit_l0_ffn(self):
        P = self.P
        x = self.x
        nc = P.nc
        sc = P.scope()
        E = 8
        CAP = 2048 * 8 // 8

        wg_d = self.din("l0f_wg", [D, E])
        we_d = self.din("l0f_we", [D, D])
        be_d = self.din("l0f_be", [1, D])
        ps_d = self.din("presel", [E, 1])

        wg = P.sbs(sc, "wg", [128, 8, E])
        wg_h = P.dma("sync", wg.ap(), wg_d.rearrange("(c p) n -> p c n", p=128))
        we, we_h = self.load_w_bf(we_d, D, D, scope=sc)
        be_bf = P.sbs(sc, "be", [1, D], BF16)
        be_h = P.dma("gpsimd", be_bf[:], be_d[:])
        presel = P.sbs(sc, "presel", [E, 1])
        psel_h = P.dma("sync", presel[:], ps_d[:])
        idf = P.sbs(sc, "idf", [128, 128])
        idf_h = P.dma("sync", idf[:], self.idty_d[:])
        ones512_bf = P.sbs(sc, "ones512b", [1, 512], BF16)
        o5_h = P.emit("gpsimd", lambda e: e.memset(ones512_bf[:], 1.0))
        tri = P.sbs(sc, "tri", [128, 128])
        t1 = P.emit("gpsimd", lambda e: e.memset(tri[:], 1.0))
        tri_h = P.emit("gpsimd", lambda e: e.affine_select(
            out=tri[:], in_=tri[:], compare_op=ALU.is_ge, fill=0.0,
            base=0, pattern=[[1, 128]], channel_multiplier=-1), [t1])

        mask_sb = [P.sbs(sc, "moemask", [128, E]) for _ in range(16)]
        gate_sb = P.sbs(sc, "moegate", [128, 16])
        gf_sb = P.sbs(sc, "moegf", [128, 16])
        counts = P.sbs(sc, "cnt", [E, 1])
        crow = P.sbs(sc, "crow", [1, E])
        gath = P.sbs(sc, "gath", [E, E])
        offrow = P.sbs(sc, "offrow", [1, E])
        offb = P.sbs(sc, "offb", [128, E])
        tmp8 = P.sbs(sc, "tmp8", [128, E])
        col1 = P.sbs(sc, "col1", [128, 1])
        grow = P.sbs(sc, "grow", [1, L], BF16)
        gb = P.sbs(sc, "gb", [128, 512])
        xbf = [P.sbs(sc, "moex", [128, 512], BF16) for _ in range(8)]
        ytmp = P.sbs(sc, "ytmp", [128, 512])
        gscr_d = nc.dram_tensor("moe_scratch", [16, 128], F32, kind="Internal").ap()
        cc_in = nc.dram_tensor("moe_cc_in", [1, E], F32, kind="Internal").ap()
        cc_out = nc.dram_tensor("moe_cc_out", [E, E], F32, kind="Internal").ap()

        cnt_mm = None
        pcnt, pcnt_d = self.pf.acquire()
        mask_h = [None] * 16
        gate_h = [None] * 16
        prev_sh = None
        for t in range(16):
            tc2 = slice(128 * t, 128 * (t + 1))
            pt, pd = self.pf.acquire()
            mm = None
            for k in range(8):
                mm = P.emit("tensor",
                            lambda e, p=pt, kk=k, c=tc2:
                            e.matmul(p[:, 0:E], lhsT=self.x[kk][:, c],
                                     rhs=wg[:, kk, :], start=(kk == 0), stop=(kk == 7)),
                            [wg_h, self.last_x[k]] + ([pd] if k == 0 else [mm]))
            nmx = P.emit("vector", lambda e, p=pt: e.reduce_max(
                col1[:, 0:1], p[:, 0:E], axis=AX.X, negate=True),
                [mm, gate_h[t - 1] if t > 0 else None])
            mh = P.emit("vector", lambda e, p=pt, o=mask_sb[t]:
                        e.tensor_scalar(out=o[:], in0=p[:, 0:E], scalar1=col1[:, 0:1],
                                        scalar2=None, op0=ALU.add), [nmx])
            mh2 = P.emit("vector", lambda e, o=mask_sb[t]:
                         e.tensor_scalar(out=o[:], in0=o[:], scalar1=-1e-30,
                                         scalar2=None, op0=ALU.is_ge), [mh])
            eh = P.emit("scalar", lambda e, p=pt, o=tmp8:
                        e.activation(out=o[:], in_=p[:, 0:E], func=AF.Exp,
                                     bias=col1[:, 0:1]),
                        [nmx, mh, prev_sh])
            sh = P.emit("vector", lambda e, o=tmp8: e.reduce_sum(
                col1[:, 0:1], o[:], axis=AX.X), [eh, mh2])
            gh = P.emit("vector", lambda e, t2=t: e.reciprocal(
                gate_sb[:, t2:t2 + 1], col1[:, 0:1]), [sh])
            mask_h[t] = mh2
            gate_h[t] = gh
            prev_sh = sh
            self.pf.release(pt, [mh2, eh])
            cnt_mm = P.emit("tensor",
                            lambda e, p=pcnt, m=mask_sb[t], t2=t:
                            e.matmul(p[0:E, 0:1], lhsT=m[:], rhs=self.ones_col[:, 0:1],
                                     start=(t2 == 0), stop=(t2 == 15)),
                            [mh2, self.cr[0], cnt_mm] + ([pcnt_d] if t == 0 else []))
        ev_c = P.emit("vector", lambda e, p=pcnt: e.tensor_copy(counts[:], p[0:E, 0:1]),
                      [cnt_mm])
        self.pf.release(pcnt, ev_c)
        ptc, ptc_d = self.pf.acquire()
        mmt = P.emit("tensor", lambda e, p=ptc: e.matmul(
            p[0:1, 0:E], lhsT=counts[:], rhs=idf[0:E, 0:E],
            start=True, stop=True), [ev_c, idf_h, ptc_d])
        ev_cr = P.emit("vector", lambda e, p=ptc: e.tensor_copy(crow[:], p[0:1, 0:E]),
                       [mmt])
        self.pf.release(ptc, ev_cr)
        d1 = P.dma("gpsimd", cc_in[:], crow[:], [ev_cr])
        ccs = P.emit("gpsimd", lambda e: e.collective_compute(
            "AllGather", ALU.bypass, replica_groups=[list(range(8))],
            ins=[cc_in.opt()], outs=[cc_out.opt()]), [d1])
        d2 = P.dma("gpsimd", gath[:], cc_out[:], [ccs])
        pto, pto_d = self.pf.acquire()
        mmo = P.emit("tensor", lambda e, p=pto: e.matmul(
            p[0:1, 0:E], lhsT=presel[:], rhs=gath[:],
            start=True, stop=True), [d2, psel_h, pto_d])
        ev_or = P.emit("vector", lambda e, p=pto: e.tensor_copy(offrow[:], p[0:1, 0:E]),
                       [mmo])
        self.pf.release(pto, ev_or)
        offb_h = self.bcast_row(offrow[0:1, :], E, offb[:], [ev_or], f32=True)

        gf_h = [None] * 16
        for t in range(16):
            pt, pd = self.pf.acquire()
            mm = None
            for j in range(t):
                mm = P.emit("tensor", lambda e, p=pt, m=mask_sb[j], jj=j:
                            e.matmul(p[:, 0:E], lhsT=self.ones_chunk[:, 0:128],
                                     rhs=m[:], start=(jj == 0), stop=False),
                            [mask_h[j], self.cr[3]] + ([pd] if j == 0 else [mm]))
            mm = P.emit("tensor", lambda e, p=pt, m=mask_sb[t], st=(t == 0):
                        e.matmul(p[:, 0:E], lhsT=tri[:], rhs=m[:],
                                 start=st, stop=False),
                        [mask_h[t], tri_h, mm] + ([pd] if t == 0 else []))
            mm = P.emit("tensor", lambda e, p=pt: e.matmul(
                p[:, 0:E], lhsT=idf[:], rhs=offb[:],
                start=False, stop=True), [offb_h, idf_h, mm])
            kh = P.emit("vector", lambda e, p=pt: e.tensor_scalar(
                out=tmp8[:], in0=p[:, 0:E], scalar1=float(CAP) + 0.5, scalar2=None,
                op0=ALU.is_lt), [mm, gf_h[t - 1] if t > 0 else prev_sh])
            kh2 = P.emit("vector", lambda e, m=mask_sb[t]:
                         e.tensor_mul(tmp8[:], tmp8[:], m[:]), [kh, mask_h[t]])
            kh3 = P.emit("vector", lambda e: e.reduce_sum(
                col1[:, 0:1], tmp8[:], axis=AX.X), [kh2])
            gf_h[t] = P.emit("vector", lambda e, t2=t: e.tensor_mul(
                gf_sb[:, t2:t2 + 1], gate_sb[:, t2:t2 + 1], col1[:, 0:1]),
                [kh3, gate_h[t]])
            self.pf.release(pt, [kh, kh2])
        ptg, ptg_d = self.pf.acquire()
        tg = P.emit("tensor", lambda e, p=ptg: e.transpose(
            p[0:16, 0:128], in_=gf_sb[:], identity=idf[:]), gf_h + [idf_h, ptg_d])
        ev_tg = P.emit("vector", lambda e, p=ptg: e.tensor_copy(
            gb[0:16, 0:128], p[0:16, 0:128]), [tg])
        self.pf.release(ptg, ev_tg)
        d3 = P.dma("gpsimd", gscr_d[:], gb[0:16, 0:128], [ev_tg])
        d4 = P.dma("gpsimd", grow[0:1, :], gscr_d.rearrange("a b -> (a b)").unsqueeze(0), [d3])

        for s2 in range(4):
            cs = slice(512 * s2, 512 * (s2 + 1))
            xbf_h = self.cast_bf(xbf, cs, self.last_x)
            gbh = self.bcast_row(grow[0:1, cs], 512, gb[:], [d4, ev_tg])
            prev_y = None
            for oc in range(8):
                pt, pd = self.pf.acquire()
                mm = None
                for k in range(8):
                    mm = P.emit("tensor",
                                lambda e, p=pt, kk=k, o=oc:
                                e.matmul(p[:, 0:512],
                                         lhsT=we[:, kk, 128 * o:128 * (o + 1)],
                                         rhs=xbf[kk][:], start=(kk == 0), stop=False),
                                [we_h, xbf_h[k]] + ([pd] if k == 0 else [mm]))
                mm = P.emit("tensor", lambda e, p=pt, o=oc: e.matmul(
                    p[:, 0:512], lhsT=be_bf[0:1, 128 * o:128 * (o + 1)],
                    rhs=ones512_bf[0:1, :], start=False, stop=True),
                    [be_h, o5_h, mm])
                h1 = P.emit("vector", lambda e, p=pt:
                            e.tensor_mul(ytmp[:], p[:, 0:512], gb[:]),
                            [mm, gbh, prev_y])
                h2 = P.emit("vector", lambda e, s3=x[oc], c=cs:
                            e.tensor_add(s3[:, c], s3[:, c], ytmp[:]),
                            [h1, self.last_x[oc]])
                self.last_x[oc] = h2
                prev_y = h2
                self.pf.release(pt, h1)
            bars = P.barrier()
            self.last_x = [bars["vector"]] * 8
        sc.close()

    def emit_l1_ffn(self):
        P = self.P
        x = self.x
        sc = P.scope()
        NSTATE = 4096
        w1_d = self.din("l1f_w1", [D, NSTATE])
        b1_d = self.din("l1f_b1", [1, NSTATE])
        w2_d = self.din("l1f_w2", [NSTATE, D])
        b2_d = self.din("l1f_b2", [1, D])

        w1 = P.sbs(sc, "w1", [128, 8, NSTATE], BF16)
        w1_hk = [P.dma("gpsimd", w1[:, k, :],
                       w1_d[128 * k:128 * (k + 1), :]) for k in range(8)]
        b1_bf = P.sbs(sc, "b1", [1, NSTATE], BF16)
        b1_h = P.dma("gpsimd", b1_bf[:], b1_d[:])
        b2_bf = P.sbs(sc, "b2", [1, D], BF16)
        b2_h = P.dma("gpsimd", b2_bf[:], b2_d[:])
        ones512_bf = P.sbs(sc, "ones512c", [1, 512], BF16)
        o5_h = P.emit("gpsimd", lambda e: e.memset(ones512_bf[:], 1.0))
        w2b = [P.sbs(sc, "w2b", [128, 32, 128], BF16) for _ in range(2)]
        xbf = [P.sbs(sc, "f1x", [128, 512], BF16) for _ in range(8)]
        h1t = [P.sbs(sc, "h1t", [128, 512], BF16) for _ in range(32)]
        w2_loads = {}
        w2_cons = {}

        for s2 in range(4):
            cs = slice(512 * s2, 512 * (s2 + 1))
            xbf_h = self.cast_bf(xbf, cs, self.last_x)
            h1_h = [None] * 32
            for hc in range(8):
                for pt2 in range(4):
                    ti = 4 * hc + pt2
                    ncol = slice(512 * hc + 128 * pt2, 512 * hc + 128 * (pt2 + 1))
                    pt, pd = self.pf.acquire()
                    mm = None
                    for k in range(8):
                        mm = P.emit("tensor",
                                    lambda e, p=pt, kk=k, nc2=ncol:
                                    e.matmul(p[:, 0:512], lhsT=w1[:, kk, nc2],
                                             rhs=xbf[kk][:],
                                             start=(kk == 0), stop=False),
                                    [w1_hk[k], xbf_h[k]]
                                    + ([pd] if k == 0 else [mm]))
                    mm = P.emit("tensor", lambda e, p=pt, nc2=ncol: e.matmul(
                        p[:, 0:512], lhsT=b1_bf[0:1, nc2], rhs=ones512_bf[0:1, :],
                        start=False, stop=True), [b1_h, o5_h, mm])
                    gh = P.emit("scalar", lambda e, p=pt, o=h1t[ti]:
                                e.activation(out=o[:], in_=p[:, 0:512], func=AF.Gelu),
                                [mm, h1_h[ti]])
                    h1_h[ti] = gh
                    self.pf.release(pt, gh)
            for oc in range(8):
                wb = w2b[oc % 2]
                ld = P.dma("gpsimd", wb.ap(),
                           w2_d[:, 128 * oc:128 * (oc + 1)]
                           .rearrange("(c p) n -> p c n", p=128),
                           [w2_cons.get(oc % 2)])
                w2_loads[(s2, oc)] = ld
                pt, pd = self.pf.acquire()
                mm = None
                for k in range(32):
                    mm = P.emit("tensor",
                                lambda e, p=pt, kk=k, w=wb:
                                e.matmul(p[:, 0:512], lhsT=w[:, kk, :], rhs=h1t[kk][:],
                                         start=(kk == 0), stop=False),
                                [ld, h1_h[k]] + ([pd] if k == 0 else [mm]))
                mm = P.emit("tensor", lambda e, p=pt, o=oc: e.matmul(
                    p[:, 0:512], lhsT=b2_bf[0:1, 128 * o:128 * (o + 1)],
                    rhs=ones512_bf[0:1, :], start=False, stop=True),
                    [b2_h, o5_h, mm])
                ev = P.emit("vector", lambda e, p=pt, s3=x[oc], c=cs:
                            e.tensor_add(s3[:, c], s3[:, c], p[:, 0:512]),
                            [mm, self.last_x[oc]])
                self.last_x[oc] = ev
                self.pf.release(pt, ev)
                w2_cons[oc % 2] = mm
            bars = P.barrier()
            self.last_x = [bars["vector"]] * 8
        sc.close()

    def emit_l1_attn(self):
        P = self.P
        x = self.x
        nc = P.nc
        sc = P.scope()
        H, DK, DV = 4, 256, 256
        SL = 512
        NSL = L // SL

        wq_d = self.din("l1_wq", [D, D])
        wk_d = self.din("l1_wk", [D, D])
        wv_d = self.din("l1_wv", [D, D])
        wo_d = self.din("l1_wo", [D, D])
        wb_d = self.din("l1_wb", [D, H])
        wg_d = self.din("l1_wg", [D, H])
        wl_d = self.din("l1_wl", [D, 12])
        bg_d = self.din("l1_bg", [1, H])
        bl_d = self.din("l1_bl", [1, 12])
        lt_d = self.din("l1_lt", [1, 12])
        rms_d = self.din("l1_rms", [128, 8])
        cq_d = self.din("l1_cq", [D, 4])
        ck_d = self.din("l1_ck", [D, 4])
        cv_d = self.din("l1_cv", [D, 4])
        fs_d = self.din("l1_fs", [D, 3])
        flw_d = self.din("l1_flw", [128, 8 * 31])
        dmask_d = self.din("c_dmask", [2 * 128, 128])

        wb_sb = P.sbs(sc, "l1_wbs", [128, 8, H], BF16)
        wg_sb = P.sbs(sc, "l1_wgs", [128, 8, H], BF16)
        wl_sb = P.sbs(sc, "l1_wls", [128, 8, 12], BF16)
        wb_h = P.dma("gpsimd", wb_sb.ap(), wb_d.rearrange("(c p) n -> p c n", p=128))
        wg_h = P.dma("gpsimd", wg_sb.ap(), wg_d.rearrange("(c p) n -> p c n", p=128))
        wl_h = P.dma("gpsimd", wl_sb.ap(), wl_d.rearrange("(c p) n -> p c n", p=128))
        cqc = P.sbs(sc, "l1_cqc", [128, 8, 4])
        ckc = P.sbs(sc, "l1_ckc", [128, 8, 4])
        cvc = P.sbs(sc, "l1_cvc", [128, 8, 4])
        fsc = P.sbs(sc, "l1_fsc", [128, 8, 3])
        cq_h = P.dma("sync", cqc.ap(), cq_d.rearrange("(c p) n -> p c n", p=128))
        ck_h = P.dma("sync", ckc.ap(), ck_d.rearrange("(c p) n -> p c n", p=128))
        cv_h = P.dma("sync", cvc.ap(), cv_d.rearrange("(c p) n -> p c n", p=128))
        fs_h = P.dma("sync", fsc.ap(), fs_d.rearrange("(c p) n -> p c n", p=128))
        bg_bf = P.sbs(sc, "l1_bg", [1, H], BF16)
        bl_bf = P.sbs(sc, "l1_bl", [1, 12], BF16)
        bg_h = P.dma("gpsimd", bg_bf[:], bg_d[:])
        bl_h = P.dma("gpsimd", bl_bf[:], bl_d[:])
        lt_sb = P.sbs(sc, "l1_lt", [1, 12])
        lt_h = P.dma("sync", lt_sb[:], lt_d[:])
        rms_sb = P.sbs(sc, "l1_rms", [128, 8])
        rms_h = P.dma("sync", rms_sb.ap(), rms_d[:])
        dmask = P.sbs(sc, "l1_dm", [128, 2, 128], BF16)
        dm_h = P.dma("gpsimd", dmask.ap(), dmask_d.rearrange("(m p) n -> p m n", p=128))
        mneg_low = dmask[:, 0, :]
        m_up_incl = dmask[:, 1, :]
        eps6 = P.sbs(sc, "eps6", [1, 1])
        e6_h = P.emit("gpsimd", lambda e: e.memset(eps6[:], 1e-6))
        ones1_bf = P.sbs(sc, "ones1b", [1, 128], BF16)
        o1b_h = P.emit("gpsimd", lambda e: e.memset(ones1_bf[:], 1.0))
        etrow = P.sbs(sc, "etrow", [1, 12], BF16)
        et1 = P.emit("scalar", lambda e: e.activation(
            out=etrow[:], in_=lt_sb[:], func=AF.Exp, scale=-1.0), [lt_h])
        etb = P.sbs(sc, "etb", [128, 12])
        etb_h = self.bcast_row(etrow[0:1, :], 12, etb[:], [et1])

        wbuf = [P.sbs(sc, "l1_w", [128, 8, 256], BF16) for _ in range(2)]
        flw = P.sbs(sc, "l1_flw", [128, 8, 31])
        flw_h = P.dma("sync", flw.ap(), flw_d.rearrange("p (o t) -> p o t", t=31))
        flh = [P.sbs(sc, "l1_flh", [128, n, 128], BF16) for n in (11, 10, 10)]
        pad = [P.sbs(sc, "l1_pad", [128, 3 + SL], BF16) for _ in range(8)]
        vsil = [P.sbs(sc, "l1_vsil", [128, 30 + SL], BF16) for _ in range(8)]
        tails = {t: [P.sbs(sc, f"l1_tl{t}", [128, 3], BF16) for _ in range(8)]
                 for t in ("q", "k", "v")}
        vtails = [P.sbs(sc, "l1_vtl", [128, 30], BF16) for _ in range(8)]
        qn = [P.sbs(sc, "l1_qn", [128, SL], BF16) for _ in range(8)]
        kn = [P.sbs(sc, "l1_kn", [128, SL], BF16) for _ in range(8)]
        dtile = [P.sbs(sc, "l1_dt", [128, 128], BF16) for _ in range(2)]
        vtm_sets = [P.sbs(sc, "l1_vtm", [128, DV], BF16) for _ in range(4)]
        ktm_sets = [P.sbs(sc, "l1_ktm", [128, DK], BF16) for _ in range(4)]
        kbtm_sets = [P.sbs(sc, "l1_kbtm", [128, DK], BF16) for _ in range(4)]
        beta_tm = [P.sbs(sc, "l1_btm", [128, H]) for _ in range(4)]
        gates_tm = [P.sbs(sc, "l1_gtm", [128, 16]) for _ in range(4)]
        dsc_sets = [{n: P.sbs(sc, f"l1_{n}", [128, 128], BF16)
                     for n in ("Nt", "Mt", "N2", "M2", "N4", "M4", "N8", "M8",
                               "N16", "M16", "N32", "M32", "N64", "Xc", "attn",
                               "u1", "u2", "w1", "w2", "ws1", "ws2")}
                    for _ in range(4)]
        Sf = [P.sbs(sc, "l1_S", [128, DV]) for _ in range(8)]
        Sbf = [P.sbs(sc, "l1_Sb", [128, DV], BF16) for _ in range(8)]
        macc = [P.sbs(sc, "l1_macc", [128, SL]) for _ in range(8)]
        mtmp = P.sbs(sc, "l1_mtmp", [128, SL], BF16)
        mtmp2 = P.sbs(sc, "l1_mtmp2", [128, SL], BF16)
        gbc = [P.sbs(sc, "l1_gbc", [128, SL], BF16) for _ in range(2)]
        gbp = [P.sbs(sc, "l1_gbp", [128, SL], BF16) for _ in range(3)]
        l2row = P.sbs(sc, "l1_l2r", [1, SL])
        l2rb = P.sbs(sc, "l1_l2rb", [1, SL], BF16)
        rbc = P.sbs(sc, "l1_rbc", [128, SL], BF16)
        sq_bf = P.sbs(sc, "l1_sqbf", [128, SL], BF16)
        osb = [P.sbs(sc, "l1_osb", [128, SL], BF16) for _ in range(8)]
        selg_d = self.din("c_selg", [16, 16 * 128])
        selg = P.sbs(sc, "l1_selg", [16, 16, 128], BF16)
        selg_h = P.dma("gpsimd", selg.ap(),
                       selg_d.rearrange("a (r n) -> a r n", n=128))
        gtr4 = P.sbs(sc, "l1_gtr4", [16, 4, 128], BF16)
        idf2 = P.sbs(sc, "l1_idf", [128, 128])
        idf2_h = P.dma("sync", idf2[:], self.idty_d[:])

        s_init = []
        for i in range(8):
            s_init.append(P.emit("gpsimd", lambda e, s2=Sf[i]: e.memset(s2[:], 0.0)))
            s_init.append(P.emit("gpsimd", lambda e, s2=Sbf[i]: e.memset(s2[:], 0.0)))
            for t in ("q", "k", "v"):
                s_init.append(P.emit("gpsimd", lambda e, s2=tails[t][i]:
                                     e.memset(s2[:], 0.0)))
            s_init.append(P.emit("gpsimd", lambda e, s2=vtails[i]: e.memset(s2[:], 0.0)))
        init_bar = P.barrier()
        self.last_x = [init_bar["vector"]] * 8
        sbf_h = {i: [init_bar["vector"]] for i in range(8)}
        sf_h = {i: [init_bar["vector"]] for i in range(8)}

        xbf = [P.sbs(sc, "l1_xbf", [128, SL], BF16) for _ in range(8)]
        wslot = {"i": 0, "cons": {0: None, 1: None}}

        def stream_w(dram, col0):
            bidx = wslot["i"] % 2
            buf = wbuf[bidx]
            h = P.dma("gpsimd", buf.ap(),
                      dram[:, col0:col0 + 256].rearrange("(c p) n -> p c n", p=128),
                      [wslot["cons"][bidx]])
            wslot["i"] += 1
            return buf, h, bidx

        PRB = os.environ.get("DEV_L1_PROBE")

        def probe(src_tiles, cs2, deps_fn):
            hs3 = []
            for i in range(8):
                hs3.append(P.emit("vector",
                                  lambda e, s3=x[i], o=src_tiles[i], c=cs2:
                                  e.tensor_copy(s3[:, c], o[:]),
                                  [deps_fn(i), self.last_x[i]]))
            return hs3

        for sl2 in range(NSL):
            cs = slice(SL * sl2, SL * (sl2 + 1))
            xbf_h = self.cast_bf(xbf, cs, self.last_x)
            tens_h = {}
            dt_use = {0: None, 1: None}

            if PRB == "xbf":
                for i in range(8):
                    P.emit("vector", lambda e, s3=x[i], o=xbf[i], c=cs:
                           e.tensor_copy(s3[:, c], o[:]),
                           [xbf_h[i], self.last_x[i]])
                bars = P.barrier()
                self.last_x = [bars["vector"]] * 8
                continue
            bg_done = []
            for ch in range(4 if not os.environ.get("DEV_L1_NOGATES") else 0):
                lcol = slice(128 * ch, 128 * (ch + 1))
                ptb, pdb = self.pf.acquire()
                mm = None
                for k in range(8):
                    mm = P.emit("tensor",
                                lambda e, p=ptb, kk=k, c2=lcol:
                                e.matmul(p[:, 0:H], lhsT=xbf[kk][:, c2],
                                         rhs=wb_sb[:, kk, :],
                                         start=(kk == 0), stop=(kk == 7)),
                                [wb_h, xbf_h[k]] + ([pdb] if k == 0 else [mm]))
                bh = P.emit("scalar", lambda e, p=ptb, o=beta_tm[ch]:
                            e.activation(out=o[:], in_=p[:, 0:H], func=AF.Sigmoid),
                            [mm])
                self.pf.release(ptb, bh)
                ptg, pdg = self.pf.acquire()
                mm = None
                for k in range(8):
                    mm = P.emit("tensor",
                                lambda e, p=ptg, kk=k, c2=lcol:
                                e.matmul(p[:, 0:12], lhsT=xbf[kk][:, c2],
                                         rhs=wl_sb[:, kk, :],
                                         start=(kk == 0), stop=False),
                                [wl_h, xbf_h[k]] + ([pdg] if k == 0 else [mm]))
                mm = P.emit("tensor", lambda e, p=ptg: e.matmul(
                    p[:, 0:12], lhsT=ones1_bf[0:1, :], rhs=bl_bf[0:1, :],
                    start=False, stop=True), [bl_h, o1b_h, mm])
                gt = gates_tm[ch]
                g1 = P.emit("vector", lambda e, p=ptg, o=gt:
                            e.tensor_mul(o[:, 4:16], p[:, 0:12], etb[:]),
                            [mm, etb_h])
                self.pf.release(ptg, g1)
                m1 = P.emit("vector", lambda e, o=gt: e.tensor_tensor(
                    out=o[:, 0:4], in0=o[:, 4:16:3], in1=o[:, 5:16:3], op=ALU.max),
                    [g1])
                m2 = P.emit("vector", lambda e, o=gt: e.tensor_tensor(
                    out=o[:, 0:4], in0=o[:, 0:4], in1=o[:, 6:16:3], op=ALU.max),
                    [m1])
                subs = []
                for j in range(3):
                    sj = P.emit("vector", lambda e, o=gt, jj=j: e.tensor_sub(
                        o[:, 4 + jj:16:3], o[:, 4 + jj:16:3], o[:, 0:4]),
                        [m2] + subs)
                    subs.append(sj)
                ex = P.emit("scalar", lambda e, o=gt: e.activation(
                    out=o[:, 4:16], in_=o[:, 4:16], func=AF.Exp), subs)
                sm = P.emit("vector", lambda e, o=gt: e.tensor_add(
                    o[:, 0:4], o[:, 4:16:3], o[:, 5:16:3]), [ex])
                sm2 = P.emit("vector", lambda e, o=gt: e.tensor_add(
                    o[:, 0:4], o[:, 0:4], o[:, 6:16:3]), [sm])
                rc = P.emit("vector", lambda e, o=gt: e.reciprocal(
                    o[:, 0:4], o[:, 0:4]), [sm2])
                mls = []
                for j in range(3):
                    mj = P.emit("vector", lambda e, o=gt, jj=j: e.tensor_mul(
                        o[:, 4 + jj:16:3], o[:, 4 + jj:16:3], o[:, 0:4]),
                        [rc] + mls)
                    mls.append(mj)
                ptw, pdw = self.pf.acquire()
                mm = None
                for k in range(8):
                    mm = P.emit("tensor",
                                lambda e, p=ptw, kk=k, c2=lcol:
                                e.matmul(p[:, 0:H], lhsT=xbf[kk][:, c2],
                                         rhs=wg_sb[:, kk, :],
                                         start=(kk == 0), stop=False),
                                [wg_h, xbf_h[k]] + ([pdw] if k == 0 else [mm]))
                mm = P.emit("tensor", lambda e, p=ptw: e.matmul(
                    p[:, 0:H], lhsT=ones1_bf[0:1, :], rhs=bg_bf[0:1, :],
                    start=False, stop=True), [bg_h, o1b_h, mm])
                wgh = P.emit("scalar", lambda e, p=ptw, o=gt: e.activation(
                    out=o[:, 0:4], in_=p[:, 0:H], func=AF.Sigmoid),
                    [mm, rc] + mls)
                self.pf.release(ptw, wgh)
                bg_done += [wgh, bh] + mls
            gds = []
            for ch2 in range(4 if not os.environ.get("DEV_L1_NOGATES") else 0):
                ptt, ptt_d = self.pf.acquire()
                tg2 = P.emit("tensor", lambda e, p=ptt, g2=gates_tm[ch2]:
                             e.transpose(p[0:16, 0:128], in_=g2[:], identity=idf2[:]),
                             bg_done + [idf2_h, ptt_d])
                evt = P.emit("vector", lambda e, p=ptt, c2=ch2: e.tensor_copy(
                    gtr4[:, c2, :], p[0:16, 0:128]), [tg2])
                self.pf.release(ptt, evt)
                gds.append(evt)

            for tname in ("q", "k", "v"):
                dram = {"q": wq_d, "k": wk_d, "v": wv_d}[tname]
                cw, cw_h = {"q": (cqc, cq_h), "k": (ckc, ck_h),
                            "v": (cvc, cv_h)}[tname]
                proj_h = []
                for half2 in range(4):
                    wbuf_, wh_, bidx = stream_w(dram, 256 * half2)
                    for oc in range(2 * half2, 2 * half2 + 2):
                        th = P.emit("vector",
                                    lambda e, d2=pad[oc], t2=tails[tname][oc]:
                                    e.tensor_copy(d2[:, 0:3], t2[:]),
                                    [s_init, tens_h.get(("padr", oc)),
                                     tens_h.get(("padc", oc))])
                        pt, pd = self.pf.acquire()
                        mm = None
                        for k in range(8):
                            mm = P.emit(
                                "tensor",
                                lambda e, p=pt, kk=k, o=oc - 2 * half2, w=wbuf_:
                                e.matmul(p[:, 0:SL],
                                         lhsT=w[:, kk, 128 * o:128 * (o + 1)],
                                         rhs=xbf[kk][:],
                                         start=(kk == 0), stop=(kk == 7)),
                                [wh_, xbf_h[k]] + ([pd] if k == 0 else [mm]))
                        ph = P.emit("scalar", lambda e, p=pt, d2=pad[oc]:
                                    e.copy(d2[:, 3:3 + SL], p[:, 0:SL]), [mm, th])
                        self.pf.release(pt, ph)
                        proj_h.append((th, ph))
                        wslot["cons"][bidx] = mm
                if PRB == "praw" and tname == "q":
                    for i in range(8):
                        P.emit("vector", lambda e, s3=x[i], o=pad[i], c=cs:
                               e.tensor_copy(s3[:, c], o[:, 3:3 + SL]),
                               [proj_h[i][1], self.last_x[i]])
                    break
                for oc in range(8):
                    th, ph = proj_h[oc]
                    tsv = P.emit("vector", lambda e, d2=pad[oc],
                                 t2=tails[tname][oc]:
                                 e.tensor_copy(t2[:], d2[:, SL:SL + 3]), [ph])
                    pt, pd = self.pf.acquire()
                    mm = None
                    for tap in range(4):
                        dt2 = dtile[tap % 2]
                        bh2 = P.emit("vector",
                                     lambda e, d2=dt2, w=cw, o=oc, t2=tap:
                                     e.tensor_scalar(out=d2[:], in0=self.idty_bf[:],
                                                     scalar1=w[:, o, t2:t2 + 1],
                                                     scalar2=None, op0=ALU.mult),
                                     [cw_h, self.cr[5], dt_use[tap % 2]])
                        mm = P.emit("tensor",
                                    lambda e, p=pt, d2=dt2, s2=pad[oc], t2=tap:
                                    e.matmul(p[:, 0:SL], lhsT=d2[:],
                                             rhs=s2[:, t2:t2 + SL],
                                             start=(t2 == 0), stop=(t2 == 3)),
                                    [bh2, ph, th] + ([pd] if tap == 0 else [mm]))
                        dt_use[tap % 2] = mm
                    if tname == "v":
                        vh = P.emit("vector", lambda e, d2=vsil[oc], t2=vtails[oc]:
                                    e.tensor_copy(d2[:, 0:30], t2[:]),
                                    [s_init, tens_h.get(("vsr", oc))])
                        sh2 = P.emit("scalar", lambda e, p=pt, d2=vsil[oc]:
                                     e.activation(out=d2[:, 30:30 + SL],
                                                  in_=p[:, 0:SL], func=AF.Silu),
                                     [mm, vh])
                        tens_h[("vsave", oc)] = P.emit(
                            "vector", lambda e, d2=vsil[oc], t2=vtails[oc]:
                            e.tensor_copy(t2[:], d2[:, SL:SL + 30]), [sh2])
                    else:
                        dst2 = qn[oc] if tname == "q" else kn[oc]
                        sh2 = P.emit("scalar", lambda e, p=pt, d2=dst2:
                                     e.activation(out=d2[:], in_=p[:, 0:SL],
                                                  func=AF.Silu),
                                     [mm, tens_h.get(("sil", tname, oc))])
                    tens_h[("sil", tname, oc)] = sh2
                    tens_h[("padr", oc)] = tsv
                    tens_h[("padc", oc)] = mm
                    self.pf.release(pt, sh2)

            if PRB == "praw":
                bars = P.barrier()
                self.last_x = [bars["vector"]] * 8
                continue
            if PRB in ("qsil", "ksil"):
                tiles2 = qn if PRB == "qsil" else kn
                probe(tiles2, cs, lambda i: tens_h[("sil", PRB[0], i)])
                bars = P.barrier()
                self.last_x = [bars["vector"]] * 8
                continue
            for tname, tiles in (("q", qn), ("k", kn)):
                for h in range(H):
                    t0, t1 = 2 * h, 2 * h + 1
                    pt, pd = self.pf.acquire()
                    mm = None
                    for i2, ti in enumerate((t0, t1)):
                        sq2 = P.emit("scalar", lambda e, s2=tiles[ti]:
                                     e.activation(out=sq_bf[:], in_=s2[:],
                                                  func=AF.Square),
                                     [tens_h[("sil", tname, ti)],
                                      tens_h.get("sqd"), mm])
                        mm = P.emit("tensor", lambda e, p=pt, i3=i2:
                                    e.matmul(p[0:1, 0:SL],
                                             lhsT=self.ones_col_bf[:, 0:1],
                                             rhs=sq_bf[:], start=(i3 == 0),
                                             stop=(i3 == 1)),
                                    [sq2, self.cr[1]] + ([pd] if i2 == 0 else []))
                        tens_h["sqd"] = mm
                    r1 = P.emit("scalar", lambda e, p=pt: e.activation(
                        out=l2row[:], in_=p[0:1, 0:SL], func=AF.Ln,
                        bias=eps6[0:1, 0:1]), [mm, e6_h, tens_h.get("l2u")])
                    r2 = P.emit("scalar", lambda e: e.activation(
                        out=l2rb[:], in_=l2row[:], func=AF.Exp, scale=-0.5), [r1])
                    self.pf.release(pt, r1)
                    rb = self.bcast_row(l2rb[0:1, :], SL, rbc[:],
                                        [r2, tens_h.get("rbcu")])
                    u1 = P.emit("vector", lambda e, s2=tiles[t0]:
                                e.tensor_mul(s2[:], s2[:], rbc[:]),
                                [rb, tens_h[("sil", tname, t0)]])
                    u2 = P.emit("vector", lambda e, s2=tiles[t1]:
                                e.tensor_mul(s2[:], s2[:], rbc[:]),
                                [rb, u1, tens_h[("sil", tname, t1)]])
                    tens_h[("sil", tname, t0)] = u1
                    tens_h[("sil", tname, t1)] = u2
                    tens_h["l2u"] = [u1, u2]
                    tens_h["rbcu"] = [u1, u2]

            if PRB in ("qn", "kn"):
                tiles2 = qn if PRB == "qn" else kn
                probe(tiles2, cs, lambda i: tens_h[("sil", PRB[0], i)])
                bars = P.barrier()
                self.last_x = [bars["vector"]] * 8
                continue
            for ch in range(int(os.environ.get("DEV_L1_NCH", 4))):
                lcol = slice(128 * ch, 128 * (ch + 1))

                def delta_head(h, si):
                    t0, t1 = 2 * h, 2 * h + 1
                    dsc2 = dsc_sets[si]
                    v_tm = vtm_sets[si]
                    k_tm = ktm_sets[si]
                    kb_tm = kbtm_sets[si]
                    tkey = lambda n2: (si, n2)
                    tms = []
                    for (srcg, sname, off, dst2) in ((vsil, "v", 30, v_tm),
                                                     (kn, "k", 0, k_tm)):
                        for i2, ti in enumerate((t0, t1)):
                            ptb2, pdb2 = self.pb.acquire()
                            if srcg is vsil:
                                src_ap = srcg[ti][:, off + 128 * ch:
                                                  off + 128 * (ch + 1)]
                                sdep = tens_h[("vsave", ti)]
                            else:
                                src_ap = srcg[ti][:, lcol]
                                sdep = tens_h[("sil", "k", ti)]
                            tr = P.emit("tensor", lambda e, p=ptb2, s2=src_ap:
                                        e.transpose(p[:, 0:128], in_=s2,
                                                    identity=self.idty_bf[:]),
                                        [sdep, pdb2, self.cr[5]])
                            ev = P.emit("scalar", lambda e, p=ptb2, d2=dst2, i3=i2:
                                        e.copy(d2[:, 128 * i3:128 * (i3 + 1)],
                                               p[:, 0:128]),
                                        [tr, tens_h.get(tkey(("tmu", sname, i2)))])
                            self.pb.release(ptb2, ev)
                            tms.append(ev)
                    yield
                    bcol = beta_tm[ch]
                    vb = P.emit("vector", lambda e, b2=bcol, hh=h, vt2=v_tm:
                                e.tensor_scalar(out=vt2[:], in0=vt2[:],
                                                scalar1=b2[:, hh:hh + 1],
                                                scalar2=None, op0=ALU.mult),
                                tms + bg_done + [tens_h.get(tkey("vtmu"))])
                    kb2 = P.emit("vector", lambda e, b2=bcol, hh=h, kt2=k_tm,
                                 kb3=kb_tm:
                                 e.tensor_scalar(out=kb3[:], in0=kt2[:],
                                                 scalar1=b2[:, hh:hh + 1],
                                                 scalar2=None, op0=ALU.mult),
                                 tms + bg_done + [tens_h.get(tkey("kbu"))])
                    pa, pa_d = self.pf.acquire()
                    mm = None
                    for i2, ti in enumerate((t0, t1)):
                        mm = P.emit("tensor",
                                    lambda e, p=pa, s2=kn[ti], i3=i2, c2=lcol:
                                    e.matmul(p[:, 0:128], lhsT=s2[:, c2],
                                             rhs=s2[:, c2], start=(i3 == 0),
                                             stop=(i3 == 1)),
                                    [tens_h[("sil", "k", ti)]]
                                    + ([pa_d] if i2 == 0 else []),
                                    sdeps=[mm] if mm else [])
                    nh = P.emit("vector", lambda e, p=pa, b2=bcol, hh=h, d2=dsc2:
                                e.scalar_tensor_tensor(
                                    out=d2["Nt"][:], in0=p[:, 0:128],
                                    scalar=b2[:, hh:hh + 1], in1=mneg_low,
                                    op0=ALU.mult, op1=ALU.mult),
                                [mm, dm_h, tens_h.get(tkey("Ntu"))] + bg_done)
                    self.pf.release(pa, nh)
                    yield
                    ptb2, pdb2 = self.pb.acquire()
                    trm = P.emit("tensor", lambda e, p=ptb2, d2=dsc2: e.transpose(
                        p[:, 0:128], in_=d2["Nt"][:], identity=self.idty_bf[:]),
                        [nh, pdb2, self.cr[5]])
                    mh = P.emit("scalar", lambda e, p=ptb2, d2=dsc2: e.copy(
                        d2["Mt"][:], p[:, 0:128]), [trm, tens_h.get(tkey("Mtu"))])
                    self.pb.release(ptb2, mh)
                    yield
                    pw_h = {"Nt": nh, "Mt": mh}
                    for (dstn, lt2, rt2) in (
                        ("N2", "Mt", "Nt"), ("M2", "Nt", "Mt"),
                        ("N4", "M2", "N2"), ("M4", "N2", "M2"),
                        ("N8", "M4", "N4"), ("M8", "N4", "M4"),
                        ("N16", "M8", "N8"), ("M16", "N8", "M8"),
                        ("N32", "M16", "N16"), ("M32", "N16", "M16"),
                        ("N64", "M32", "N32"),
                    ):
                        pp, pp_d = self.pf.acquire()
                        mm2 = P.emit("tensor",
                                     lambda e, p=pp, l2=dsc2[lt2], r2=dsc2[rt2]:
                                     e.matmul(p[:, 0:128], lhsT=l2[:], rhs=r2[:],
                                              start=True, stop=True),
                                     [pw_h[lt2], pw_h[rt2], pp_d])
                        ev2 = P.emit("scalar", lambda e, p=pp, d2=dsc2[dstn]:
                                     e.copy(d2[:], p[:, 0:128]),
                                     [mm2, tens_h.get(tkey((dstn, "u")))])
                        self.pf.release(pp, ev2)
                        pw_h[dstn] = ev2
                        yield
                    xh = P.emit("vector", lambda e, d2=dsc2: e.scalar_tensor_tensor(
                        out=d2["Xc"][:], in0=d2["Mt"][:], scalar=1.0,
                        in1=self.idty_bf[:], op0=ALU.mult, op1=ALU.add),
                        [mh, self.cr[5], tens_h.get(tkey("Xcu"))])
                    for j2 in ("N2", "N4", "N8", "N16", "N32", "N64"):
                        pp, pp_d = self.pf.acquire()
                        mm2 = P.emit("tensor", lambda e, p=pp, l2=dsc2[j2], d2=dsc2:
                                     e.matmul(p[:, 0:128], lhsT=l2[:],
                                              rhs=d2["Xc"][:],
                                              start=True, stop=False),
                                     [pw_h[j2], xh, pp_d])
                        mm2 = P.emit("tensor", lambda e, p=pp, d2=dsc2: e.matmul(
                            p[:, 0:128], lhsT=self.idty_bf[:], rhs=d2["Xc"][:],
                            start=False, stop=True), [xh, self.cr[5]],
                            sdeps=[mm2])
                        xh = P.emit("scalar", lambda e, p=pp, d2=dsc2: e.copy(
                            d2["Xc"][:], p[:, 0:128]), [mm2])
                        self.pf.release(pp, xh)
                        yield
                    sbf_readers = {t0: [], t1: []}
                    pw, pw_d = self.pf.acquire()
                    mmw = P.emit("tensor", lambda e, p=pw, d2=dsc2, kb3=kb_tm:
                                 e.matmul(p[:, 0:256], lhsT=d2["Xc"][:], rhs=kb3[:],
                                          start=True, stop=True), [kb2, xh, pw_d])
                    wn1 = P.emit("scalar", lambda e, p=pw, d2=dsc2: e.copy(
                        d2["w1"][:], p[:, 0:128]),
                        [mmw, tens_h.get(tkey(("w", 0)))])
                    wn2 = P.emit("scalar", lambda e, p=pw, d2=dsc2: e.copy(
                        d2["w2"][:], p[:, 128:256]),
                        [mmw, wn1, tens_h.get(tkey(("w", 1)))])
                    self.pf.release(pw, [wn1, wn2])
                    yield
                    wts = []
                    for i2 in range(2):
                        ptb3, pdb3 = self.pb.acquire()
                        trw = P.emit("tensor", lambda e, p=ptb3, i3=i2, d2=dsc2:
                                     e.transpose(p[:, 0:128],
                                                 in_=d2["w%d" % (i3 + 1)][:],
                                                 identity=self.idty_bf[:]),
                                     [wn1 if i2 == 0 else wn2, pdb3, self.cr[5]])
                        wtv = P.emit("scalar", lambda e, p=ptb3, i3=i2, d2=dsc2:
                                     e.copy(d2["w%d" % (i3 + 1)][:], p[:, 0:128]),
                                     [trw])
                        self.pb.release(ptb3, wtv)
                        wts.append(wtv)
                    yield
                    pws, pws_d = self.pf.acquire()
                    mws = None
                    for i2 in range(2):
                        mws = P.emit("tensor", lambda e, p=pws, i3=i2, d2=dsc2,
                                     sb3=Sbf[2 * h + i2]: e.matmul(
                            p[:, 0:256], lhsT=d2["w%d" % (i3 + 1)][:],
                            rhs=sb3[:],
                            start=(i3 == 0), stop=(i3 == 1)),
                            [wts[i2], sbf_h[2 * h + i2]] +
                            ([pws_d] if i2 == 0 else []),
                            sdeps=[mws] if mws else [])
                        sbf_readers[2 * h + i2].append(mws)
                    ws1 = P.emit("scalar", lambda e, p=pws, d2=dsc2: e.copy(
                        d2["ws1"][:], p[:, 0:128]),
                        [mws, tens_h.get(tkey(("ws", 0)))])
                    ws2 = P.emit("scalar", lambda e, p=pws, d2=dsc2: e.copy(
                        d2["ws2"][:], p[:, 128:256]),
                        [mws, ws1, tens_h.get(tkey(("ws", 1)))])
                    self.pf.release(pws, [ws1, ws2])
                    yield
                    pu, pu_d = self.pf.acquire()
                    mmu = P.emit("tensor", lambda e, p=pu, d2=dsc2, vt2=v_tm:
                                 e.matmul(p[:, 0:256], lhsT=d2["Xc"][:], rhs=vt2[:],
                                          start=True, stop=True), [xh, vb, pu_d])
                    uh1 = P.emit("vector", lambda e, p=pu, d2=dsc2: e.tensor_sub(
                        d2["u1"][:], p[:, 0:128], d2["ws1"][:]),
                        [mmu, ws1, tens_h.get(tkey("u1u"))])
                    uh2 = P.emit("vector", lambda e, p=pu, d2=dsc2: e.tensor_sub(
                        d2["u2"][:], p[:, 128:256], d2["ws2"][:]),
                        [mmu, uh1, ws2, tens_h.get(tkey("u2u"))])
                    self.pf.release(pu, [uh1, uh2])
                    yield
                    pat, pat_d = self.pf.acquire()
                    mm = None
                    for i2, ti in enumerate((t0, t1)):
                        mm = P.emit("tensor",
                                    lambda e, p=pat, kk=kn[ti], q2=qn[ti], c2=lcol,
                                    i3=i2:
                                    e.matmul(p[:, 0:128], lhsT=kk[:, c2],
                                             rhs=q2[:, c2], start=(i3 == 0),
                                             stop=(i3 == 1)),
                                    [tens_h[("sil", "k", ti)],
                                     tens_h[("sil", "q", ti)]]
                                    + ([pat_d] if i2 == 0 else []),
                                    sdeps=[mm] if mm else [])
                    ath = P.emit("vector", lambda e, p=pat, d2=dsc2: e.tensor_tensor(
                        out=d2["attn"][:], in0=p[:, 0:128], in1=m_up_incl,
                        op=ALU.mult), [mm, dm_h, tens_h.get(tkey("attnu"))])
                    self.pf.release(pat, ath)
                    yield
                    mo = None
                    for i2, ti in enumerate((t0, t1)):
                        po2, po2_d = self.pf.acquire()
                        mo = None
                        for j3, tj in enumerate((t0, t1)):
                            mo = P.emit("tensor",
                                        lambda e, p=po2, tj2=tj, i3=i2, c2=lcol:
                                        e.matmul(p[:, 0:128],
                                                 lhsT=Sbf[tj2][:, 128 * i3:
                                                               128 * (i3 + 1)],
                                                 rhs=qn[tj2][:, c2],
                                                 start=(j4 == 0), stop=False)
                                        if False else
                                        e.matmul(p[:, 0:128],
                                                 lhsT=Sbf[tj2][:, 128 * i3:
                                                               128 * (i3 + 1)],
                                                 rhs=qn[tj2][:, c2],
                                                 start=(tj2 == t0), stop=False),
                                        [sbf_h[tj], tens_h[("sil", "q", tj)]]
                                        + ([po2_d] if j3 == 0 else []),
                                        sdeps=[mo] if mo else [])
                        sbf_readers[t0].append(mo)
                        sbf_readers[t1].append(mo)
                        mo = P.emit("tensor",
                                    lambda e, p=po2, i3=i2, d2=dsc2:
                                    e.matmul(p[:, 0:128],
                                             lhsT=d2["u1" if i3 == 0 else "u2"][:],
                                             rhs=d2["attn"][:],
                                             start=False, stop=True),
                                    [uh1 if i2 == 0 else uh2, ath],
                                    sdeps=[mo])
                        me = P.emit("vector", lambda e, p=po2, d2=macc[ti], c2=lcol:
                                    e.tensor_copy(d2[:, c2], p[:, 0:128]),
                                    [mo, tens_h.get(("maccu", ti))])
                        tens_h[("maccu", ti)] = me
                        self.pf.release(po2, me)
                    yield
                    m2b = None
                    for i2, ti in enumerate((t0, t1)):
                        ps2, ps2_d = self.pf.acquire()
                        m1b = P.emit("tensor", lambda e, p=ps2, i3=i2, d2=dsc2,
                                     kt2=k_tm: e.matmul(
                            p[:, 0:128], lhsT=kt2[:, 128 * i3:128 * (i3 + 1)],
                            rhs=d2["u1"][:], start=True, stop=True),
                            [uh1, tms, ps2_d])
                        m2b = P.emit("tensor", lambda e, p=ps2, i3=i2, d2=dsc2,
                                     kt2=k_tm: e.matmul(
                            p[:, 128:256], lhsT=kt2[:, 128 * i3:128 * (i3 + 1)],
                            rhs=d2["u2"][:], start=True, stop=True),
                            [uh2, tms], sdeps=[m1b])
                        su = P.emit("vector", lambda e, p=ps2, s2=Sf[ti]:
                                    e.tensor_add(s2[:], s2[:], p[:, 0:256]),
                                    [m2b] + sf_h[ti])
                        sb2 = P.emit("vector", lambda e, s2=Sf[ti], d2=Sbf[ti]:
                                     e.tensor_copy(d2[:], s2[:]),
                                     [su] + sbf_h[ti] + sbf_readers[ti])
                        self.pf.release(ps2, su)
                        sf_h[ti] = [su]
                        sbf_h[ti] = [sb2]
                    late = [m2b, uh2, sbf_h[t1][0]]
                    for key2 in ("vtmu", "kbu", "Ntu", "Mtu", "attnu",
                                 "u1u", "u2u", "Xcu"):
                        tens_h[tkey(key2)] = late
                    for dstn in ("N2", "M2", "N4", "M4", "N8", "M8", "N16",
                                 "M16", "N32", "M32", "N64"):
                        tens_h[tkey((dstn, "u"))] = late
                    for i2 in range(2):
                        tens_h[tkey(("w", i2))] = late
                        tens_h[tkey(("ws", i2))] = late
                    for sname in ("v", "k"):
                        for i2 in range(2):
                            tens_h[tkey(("tmu", sname, i2))] = late

                gens = [delta_head(hh2, hh2) for hh2 in range(4)]
                alive = list(gens)
                while alive:
                    for g in list(alive):
                        try:
                            next(g)
                        except StopIteration:
                            alive.remove(g)

            if PRB == "delta":
                probe(macc, cs, lambda i: tens_h[("maccu", i)])
                bars = P.barrier()
                self.last_x = [bars["vector"]] * 8
                continue
            if PRB == "u2":
                bars = P.barrier()
                self.last_x = [bars["vector"]] * 8
                continue
            fl_use = {j: None for j in range(3)}
            for h in range(H):
                t0, t1 = 2 * h, 2 * h + 1
                pt_, pd_ = self.pf.acquire()
                mmb = None
                for ch in range(4):
                    mmb = P.emit("tensor", lambda e, p=pt_, c2=ch, hh=h:
                                 e.matmul(p[:, 128 * c2:128 * (c2 + 1)],
                                          lhsT=selg[:, hh, :], rhs=gtr4[:, c2, :],
                                          start=True, stop=True),
                                 [gds[ch], selg_h, pd_]
                                 + ([mmb] if mmb else []))
                bc0 = P.emit("vector", lambda e, p=pt_: e.tensor_copy(
                    gbc[0][:], p[:, 0:SL]), [mmb, tens_h.get("gbc0u")])
                bc1 = P.emit("vector", lambda e, p=pt_: e.tensor_scalar(
                    out=gbc[1][:], in0=p[:, 0:SL], scalar1=-1.0, scalar2=1.0,
                    op0=ALU.mult, op1=ALU.add), [mmb, tens_h.get("gbc1u")])
                self.pf.release(pt_, [bc0, bc1])
                pbs = []
                for j in range(3):
                    ptj, pdj = self.pf.acquire()
                    mmj = None
                    for ch in range(4):
                        mmj = P.emit("tensor", lambda e, p=ptj, c2=ch, jj=j, hh=h:
                                     e.matmul(p[:, 128 * c2:128 * (c2 + 1)],
                                              lhsT=selg[:, 4 + 3 * hh + jj, :],
                                              rhs=gtr4[:, c2, :],
                                              start=True, stop=True),
                                     [gds[ch], selg_h, pdj]
                                     + ([mmj] if mmj else []))
                    bj = P.emit("scalar", lambda e, p=ptj, jj=j: e.copy(
                        gbp[jj][:], p[:, 0:SL]), [mmj, tens_h.get(("gbpu", j))])
                    self.pf.release(ptj, bj)
                    pbs.append(bj)
                for ti in (t0, t1):
                    bld = {}
                    for hb, (tb, ntap, eng2) in enumerate(
                            ((0, 11, "vector"), (11, 10, "gpsimd"),
                             (21, 10, "vector"))):
                        bld[hb] = P.emit(
                            eng2,
                            lambda e, d2=flh[hb], o=ti, t3=tb, n3=ntap:
                            e.tensor_tensor(
                                out=d2[:, 0:n3, :],
                                in0=self.idty_bf.ap().unsqueeze(1)
                                .broadcast_to([128, n3, 128]),
                                in1=flw[:, o, t3:t3 + n3].unsqueeze(2)
                                .broadcast_to([128, n3, 128]),
                                op=ALU.mult),
                            [flw_h, self.cr[5], fl_use[hb]])
                    ptl, ptl_d = self.pf.acquire()
                    mm = None
                    for tap in range(31):
                        hb = 0 if tap < 11 else (1 if tap < 21 else 2)
                        off = (0, 11, 21)[hb]
                        mm = P.emit("tensor",
                                    lambda e, p=ptl, d2=flh[hb],
                                    t2=tap, o2=off, s2=vsil[ti]:
                                    e.matmul(p[:, 0:SL], lhsT=d2[:, t2 - o2, :],
                                             rhs=s2[:, t2:t2 + SL],
                                             start=(t2 == 0), stop=(t2 == 30)),
                                    [bld[hb], tens_h[("vsave", ti)]]
                                    + ([ptl_d] if tap == 0 else [mm]))
                        fl_use[hb] = mm
                    pts, pts_d = self.pf.acquire()
                    mm2 = None
                    for tap in range(3):
                        dt2 = dtile[tap % 2]
                        bh2 = P.emit("vector",
                                     lambda e, d2=dt2, t2=tap, o=ti:
                                     e.tensor_scalar(out=d2[:], in0=self.idty_bf[:],
                                                     scalar1=fsc[:, o, t2:t2 + 1],
                                                     scalar2=None, op0=ALU.mult),
                                     [fs_h, self.cr[5], dt_use[tap % 2]])
                        mm2 = P.emit("tensor",
                                     lambda e, p=pts, d2=dt2, t2=tap, s2=vsil[ti]:
                                     e.matmul(p[:, 0:SL], lhsT=d2[:],
                                              rhs=s2[:, 28 + t2:28 + t2 + SL],
                                              start=(t2 == 0), stop=(t2 == 2)),
                                     [bh2, tens_h[("vsave", ti)]]
                                     + ([pts_d] if tap == 0 else [mm2]))
                        dt_use[tap % 2] = mm2
                    e1 = P.emit("vector", lambda e, d2=macc[ti]: e.tensor_mul(
                        d2[:], d2[:], gbc[0][:]),
                        [bc0, tens_h.get(("maccu", ti))])
                    e2 = P.emit("vector", lambda e, p=ptl: e.tensor_mul(
                        mtmp[:], p[:, 0:SL], gbp[2][:]),
                        [mm, pbs[2], tens_h.get("mtmpu")])
                    self.pf.release(ptl, e2)
                    e3 = P.emit("vector", lambda e, p=pts: e.tensor_mul(
                        mtmp2[:], p[:, 0:SL], gbp[1][:]),
                        [mm2, pbs[1], tens_h.get("mtmp2u")])
                    self.pf.release(pts, e3)
                    e4 = P.emit("vector", lambda e: e.tensor_add(
                        mtmp[:], mtmp[:], mtmp2[:]), [e2, e3])
                    e5 = P.emit("vector", lambda e, d2=vsil[ti]: e.tensor_mul(
                        mtmp2[:], d2[:, 30:30 + SL], gbp[0][:]),
                        [pbs[0], tens_h[("vsave", ti)], e4])
                    e6 = P.emit("vector", lambda e: e.tensor_add(
                        mtmp[:], mtmp[:], mtmp2[:]), [e4, e5])
                    e7 = P.emit("vector", lambda e: e.tensor_mul(
                        mtmp[:], mtmp[:], gbc[1][:]), [e6, bc1])
                    e8 = P.emit("vector", lambda e, d2=macc[ti]: e.tensor_add(
                        d2[:], d2[:], mtmp[:]), [e7, e1])
                    tens_h[("maccu", ti)] = e8
                    tens_h["mtmpu"] = e7
                    tens_h["mtmp2u"] = e6
                    tens_h[("gbpu", 0)] = e5
                    tens_h[("gbpu", 1)] = e3
                    tens_h[("gbpu", 2)] = e2
                    tens_h["gbc0u"] = e1
                    tens_h["gbc1u"] = e7
                pt, pd = self.pf.acquire()
                mm = None
                for i2, ti in enumerate((t0, t1)):
                    sq2 = P.emit("scalar", lambda e, d2=macc[ti]:
                                 e.activation(out=sq_bf[:], in_=d2[:],
                                              func=AF.Square),
                                 [tens_h[("maccu", ti)], tens_h.get("sqd"), mm])
                    mm = P.emit("tensor", lambda e, p=pt, i3=i2:
                                e.matmul(p[0:1, 0:SL],
                                         lhsT=self.ones_col_bf[:, 0:1],
                                         rhs=sq_bf[:], start=(i3 == 0),
                                         stop=(i3 == 1)),
                                [sq2, self.cr[1]] + ([pd] if i2 == 0 else []))
                    tens_h["sqd"] = mm
                r1 = P.emit("scalar", lambda e, p=pt: e.activation(
                    out=l2row[:], in_=p[0:1, 0:SL], func=AF.Ln, scale=1.0 / DV,
                    bias=self.eps_col[0:1, 0:1]), [mm, self.cr[4],
                                                   tens_h.get("l2u")])
                r2 = P.emit("scalar", lambda e: e.activation(
                    out=l2rb[:], in_=l2row[:], func=AF.Exp, scale=-0.5), [r1])
                self.pf.release(pt, r1)
                rb = self.bcast_row(l2rb[0:1, :], SL, rbc[:],
                                    [r2, tens_h.get("rbcu")])
                us = []
                for ti in (t0, t1):
                    ua = P.emit("vector", lambda e, d2=macc[ti]: e.tensor_mul(
                        d2[:], d2[:], rbc[:]), [rb, tens_h[("maccu", ti)]] + us)
                    ub = P.emit("vector", lambda e, d2=macc[ti], o=osb[ti], t2=ti:
                                e.tensor_scalar(out=o[:], in0=d2[:],
                                                scalar1=rms_sb[:, t2:t2 + 1],
                                                scalar2=None, op0=ALU.mult),
                                [ua, rms_h, tens_h.get(("osbu", ti))])
                    tens_h[("maccu", ti)] = ub
                    tens_h[("osbu", ti)] = ub
                    us.append(ub)
                tens_h["l2u"] = us
                tens_h["rbcu"] = us

            if PRB == "mix":
                probe(macc, cs, lambda i: tens_h[("maccu", i)])
                bars = P.barrier()
                self.last_x = [bars["vector"]] * 8
                continue
            for half2 in range(4):
                wo_b, wo_hh, wo_bidx = stream_w(wo_d, 256 * half2)
                for oc in range(2 * half2, 2 * half2 + 2):
                    pt, pd = self.pf.acquire()
                    mm = None
                    for k in range(8):
                        mm = P.emit(
                            "tensor",
                            lambda e, p=pt, kk=k, o=oc - 2 * half2, w=wo_b:
                            e.matmul(p[:, 0:SL],
                                     lhsT=w[:, kk, 128 * o:128 * (o + 1)],
                                     rhs=osb[kk][:],
                                     start=(kk == 0), stop=(kk == 7)),
                            [wo_hh, tens_h[("osbu", k)]]
                            + ([pd] if k == 0 else [mm]))
                    ev = P.emit("vector", lambda e, p=pt, s3=x[oc], c=cs:
                                e.tensor_add(s3[:, c], s3[:, c], p[:, 0:SL]),
                                [mm, self.last_x[oc]])
                    self.last_x[oc] = ev
                    self.pf.release(pt, ev)
                    wslot["cons"][wo_bidx] = mm

            bars = P.barrier()
            self.last_x = [bars["vector"]] * 8
            for i in range(8):
                sf_h[i] = [bars["vector"]]
                sbf_h[i] = [bars["vector"]]
        sc.close()

def host_consts():
    idx = np.arange(128)
    mneg_low = -(idx[None, :] < idx[:, None]).astype(np.float32)
    m_up_incl = (idx[:, None] <= idx[None, :]).astype(np.float32)
    dmask = np.concatenate([mneg_low, m_up_incl], 0).astype(np.float32)
    idty = np.eye(128, dtype=np.float32)
    idx = np.arange(128)
    b8 = idx // 8
    m8d = ((b8[None, :] == b8[:, None]) & (idx[None, :] >= idx[:, None]))
    m8 = (b8[None, :] == b8[:, None] + 1) & (b8[None, :] % 2 == 1)
    b16 = idx // 16
    m16 = (b16[None, :] == b16[:, None] + 1) & (b16[None, :] % 2 == 1)
    gmask = np.concatenate([m8d, m8, m16], 0).astype(np.float32)
    selg = np.zeros((16, 16 * 128), np.float32)
    for r in range(16):
        selg[r, 128 * r:128 * (r + 1)] = 1.0
    return {"c_idty": idty, "c_gmask": gmask, "c_dmask": dmask, "c_selg": selg}


def build_program(blocks):
    b = Builder()
    if "l0a" in blocks:
        b.emit_l0_attn()
        if not os.environ.get("DEV_NOLN"):
            b.emit_postln("l0a")
    if "l0f" in blocks:
        b.emit_l0_ffn()
        b.emit_postln("l0f")
    if "l1a" in blocks:
        b.emit_l1_attn()
        if not os.environ.get("DEV_L1_PROBE"):
            b.emit_postln("l1a")
    if "l1f" in blocks:
        b.emit_l1_ffn()
        b.emit_postln("l1f")
    counts = b.finish()
    return b, counts


def _fl_diag(fl):
    flr = fl.reshape(D, 31)
    out = np.zeros((31, 8, 128, 128), np.float32)
    p = np.arange(128)
    for t in range(31):
        for o in range(8):
            out[t, o, p, p] = flr[o * 128 + p, t]
    return out.reshape(31 * 8 * 128, 128)


def host_inputs(inp, core):
    f32 = lambda k: np.ascontiguousarray(np.asarray(inp[k], np.float32))
    m = {
        "l0_wq": f32("L0_attn_Wq"), "l0_wf": f32("L0_attn_Wf"),
        "l0_wi": f32("L0_attn_Wi"), "l0_wo": f32("L0_attn_Wo"),
        "l0f_wg": f32("L0_ffn_Wg"), "l0f_we": f32("L0_ffn_We"),
        "l0f_be": f32("L0_ffn_be").reshape(1, D),
        "presel": (np.arange(8) < core).astype(np.float32).reshape(8, 1),
        "l1_wq": f32("L1_attn_Wq"), "l1_wk": f32("L1_attn_Wk"),
        "l1_wv": f32("L1_attn_Wv"), "l1_wo": f32("L1_attn_Wo"),
        "l1_wb": f32("L1_attn_Wb"), "l1_wg": f32("L1_attn_Wg"),
        "l1_wl": f32("L1_attn_Wl"),
        "l1_bg": f32("L1_attn_bg").reshape(1, 4),
        "l1_bl": f32("L1_attn_bl").reshape(1, 12),
        "l1_lt": np.repeat(np.asarray(inp["L1_attn_log_temp"], np.float32), 3)
        .reshape(1, 12).copy(),
        "l1_rms": np.ascontiguousarray(
            np.tile(np.asarray(inp["L1_attn_rms_w"], np.float32), 4)
            .reshape(8, 128).T),
        "l1_cq": f32("L1_attn_conv_q"), "l1_ck": f32("L1_attn_conv_k"),
        "l1_cv": f32("L1_attn_conv_v"),
        "l1_fs": np.asarray(inp["L1_attn_fir_short"], np.float32)
        .reshape(D, 3).copy(),
        "l1_flw": np.ascontiguousarray(
            np.asarray(inp["L1_attn_fir_long"], np.float32).reshape(8, 128, 31)
            .transpose(1, 0, 2).reshape(128, 8 * 31)),
        "l1f_w1": f32("L1_ffn_w1"),
        "l1f_b1": f32("L1_ffn_b1").reshape(1, 4096),
        "l1f_w2": f32("L1_ffn_w2"),
        "l1f_b2": f32("L1_ffn_b2").reshape(1, D),
    }
    return m


_BUILD = None


def kernel(**inputs):
    global _BUILD
    if _BUILD is None:
        _BUILD = build_program(("l0a", "l0f", "l1a", "l1f"))[0]
    b = _BUILD
    x = np.ascontiguousarray(np.asarray(inputs["L0_attn_hidden_states"], np.float32))
    consts = host_consts()
    in_maps = []
    for c in range(8):
        m = {"xT": np.ascontiguousarray(x[c].T)}
        m.update(consts)
        m.update(host_inputs(inputs, c))
        m = {k2: m[k2] for k2 in b.inputs}
        in_maps.append(m)
    res = run_bass_kernel_spmd(b.P.nc, in_maps, core_ids=list(range(8)))
    out = np.stack([np.asarray(res.results[c]["outT"]).T for c in range(8)])
    return np.ascontiguousarray(out.astype(np.float32))


# version 17
# speedup vs baseline: 1.0019x; 1.0019x over previous
import os
import numpy as np
import ml_dtypes
from contextlib import ExitStack
import concourse.bass as bass
import concourse.mybir as mybir
from concourse.bass_utils import run_bass_kernel_spmd

F32 = mybir.dt.float32
BF16 = mybir.dt.bfloat16
AF = mybir.ActivationFunctionType
ALU = mybir.AluOpType
AX = mybir.AxisListType

L = 2048
D = 1024
ENGINES = ("sync", "vector", "scalar", "tensor", "gpsimd")


class Planner:

    def __init__(self):
        self.nc = bass.Bass("TRN2", target_bir_lowering=False)
        self.ops = {e: [] for e in ENGINES}
        self.ctx = ExitStack()
        self._uid = 0
        self.barrier_tile = None
        self.dma_count = {e: 0 for e in ENGINES}

    def name(self, base):
        self._uid += 1
        return f"{base}_{self._uid}"

    def sb(self, base, shape, dtype=F32):
        return self.ctx.enter_context(self.nc.sbuf_tensor(self.name(base), shape, dtype))

    def scope(self):
        return ExitStack()

    def sbs(self, scope, base, shape, dtype=F32):
        return scope.enter_context(self.nc.sbuf_tensor(self.name(base), shape, dtype))

    def psum(self, base, shape, dtype=F32):
        return self.ctx.enter_context(self.nc.psum_tensor(self.name(base), shape, dtype))

    def emit(self, eng, fn, deps=(), dma=False, sdeps=()):
        ds = []
        stack = list(deps)
        while stack:
            d = stack.pop()
            if d is None:
                continue
            if isinstance(d, list):
                stack.extend(d)
            else:
                ds.append(d)
        sd = [d for d in sdeps if d is not None]
        h = (eng, len(self.ops[eng]))
        self.ops[eng].append({"fn": fn, "deps": ds, "dma": dma, "need": dma,
                              "sdeps": sd})
        return h

    def dma(self, eng, out, in_, deps=()):
        h = self.emit(eng, lambda e, o=out, i=in_: e.dma_start(out=o, in_=i),
                      deps, dma=True)
        self.dma_count[eng] += 16
        return h

    def barrier(self):
        tails = {e: (e, len(self.ops[e]) - 1) for e in ENGINES if self.ops[e]}
        snap = dict(self.dma_count)
        hs = {}
        for e in ENGINES:
            deps = [t for te, t in tails.items() if te != e]
            if e == "scalar" and e in tails:
                deps = deps + [tails[e]]
            if e in ("vector", "gpsimd"):
                hs[e] = self.emit(e, lambda en: en.engine_nop(), deps)
            elif e == "scalar" and self.barrier_tile is not None:
                hs[e] = self.emit(e, lambda en: en.copy(
                    self.barrier_tile[0:1, 0:1], self.barrier_tile[0:1, 0:1]), deps)
            else:
                hs[e] = self.emit(e, lambda en: en.nop(), deps)
            self.ops[e][hs[e][1]]["dsnap"] = snap
        return hs

    def finalize(self):
        nc = self.nc
        for e in ENGINES:
            for op in self.ops[e]:
                for (de, di) in op["deps"]:
                    self.ops[de][di]["need"] = True
                for (de, di) in op.get("sdeps", []):
                    self.ops[de][di]["need"] = True
        for e in ENGINES:
            cc = 0
            cd = 0
            for op in self.ops[e]:
                if op["need"]:
                    if op["dma"]:
                        op["inc"] = 16
                        cd += 16
                    else:
                        op["inc"] = 1
                        cc += 1
                else:
                    op["inc"] = 0
                op["cum"] = cd if op["dma"] else cc
        sems = {e: self.ctx.enter_context(self.nc.semaphore(f"sem_{e}"))
                for e in ENGINES}
        dsems = {e: self.ctx.enter_context(self.nc.semaphore(f"dsem_{e}"))
                 for e in ENGINES}
        block = self.ctx.enter_context(nc.Block())

        def runner(e):
            def run(eng):
                waited = {x: 0 for x in ENGINES}
                dwaited = {x: 0 for x in ENGINES}
                my_dma = 0
                for op in self.ops[e]:
                    if op["dma"] and my_dma > 0:
                        if my_dma > dwaited[e]:
                            eng.wait_ge(dsems[e], my_dma)
                            dwaited[e] = my_dma
                    for de2, v2 in op.get("dsnap", {}).items():
                        if v2 > dwaited[de2]:
                            eng.wait_ge(dsems[de2], v2)
                            dwaited[de2] = v2
                    for (de, di) in list(op["deps"]) + list(op.get("sdeps", [])):
                        p = self.ops[de][di]
                        if (de == e == "tensor" and not p["dma"]
                                and (de, di) not in op.get("sdeps", [])):
                            continue
                        v = p["cum"]
                        if p["dma"]:
                            if v > dwaited[de]:
                                eng.wait_ge(dsems[de], v)
                                dwaited[de] = v
                        else:
                            if v > waited[de]:
                                eng.wait_ge(sems[de], v)
                                waited[de] = v
                    inst = op["fn"](eng)
                    if op["inc"]:
                        inst.then_inc(dsems[e] if op["dma"] else sems[e], op["inc"])
                    if op["dma"]:
                        my_dma += 16
            return run

        block.sync(runner("sync"))
        block.vector(runner("vector"))
        block.scalar(runner("scalar"))
        block.tensor(runner("tensor"))
        block.gpsimd(runner("gpsimd"))
        return {e: len(self.ops[e]) for e in ENGINES}


class PsumPool:
    def __init__(self, tensors):
        self.free = [[t, []] for t in tensors]

    def acquire(self):
        t, dep = self.free.pop(0)
        return t, dep

    def release(self, t, deps):
        if not isinstance(deps, list):
            deps = [deps]
        self.free.append([t, deps])


class Builder:
    def __init__(self):
        self.P = Planner()
        nc = self.P.nc
        P = self.P
        self.inputs = {}

        self.xT_d = self.din("xT", [D, L])
        self.out_d = nc.dram_tensor("outT", [D, L], F32, kind="ExternalOutput").ap()

        self.pf = PsumPool([P.psum("pf", [128, 512]) for _ in range(5)])
        self.pb = PsumPool([P.psum("pb", [128, 1024], BF16) for _ in range(2)])
        self.pr = PsumPool([P.psum("pr", [1, 512])])

        self.ones_col = P.sb("ones_col", [128, 1])
        self.ones_col_bf = P.sb("ones_col_bf", [128, 1], BF16)
        self.ones_row = P.sb("ones_row", [1, 128])
        self.ones_row_bf = P.sb("ones_row_bf", [1, 128], BF16)
        self.ones_chunk = P.sb("ones_chunk", [128, 128])
        self.eps_col = P.sb("eps_col", [1, 1])
        P.barrier_tile = self.eps_col
        cr = []
        cr.append(P.emit("gpsimd", lambda e: e.memset(self.ones_col[:], 1.0)))
        cr.append(P.emit("gpsimd", lambda e: e.memset(self.ones_col_bf[:], 1.0)))
        cr.append(P.emit("gpsimd", lambda e: e.memset(self.ones_row[:], 1.0)))
        cr.append(P.emit("gpsimd",
                         lambda e: e.memset(self.ones_row_bf[:], 1.0)))
        cr.append(P.emit("gpsimd", lambda e: e.memset(self.ones_chunk[:], 1.0)))
        cr.append(P.emit("gpsimd", lambda e: e.memset(self.eps_col[:], 1e-5)))
        self.idty_d = self.din("c_idty", [128, 128])
        self.idty_bf = P.sb("idty_bf", [128, 128], BF16)
        cr.append(P.dma("gpsimd", self.idty_bf[:], self.idty_d[:]))
        self.cr = cr

        self.x = [P.sb("x", [128, L]) for _ in range(8)]
        self.last_x = [P.dma("sync", self.x[i][:], self.xT_d[128 * i:128 * (i + 1), :])
                       for i in range(8)]

    def din(self, name2, shape):
        ap = self.P.nc.dram_tensor(name2, shape, F32, kind="ExternalInput").ap()
        self.inputs[name2] = shape
        return ap

    def load_w_bf(self, dram_ap, rows, cols, deps=(), scope=None):
        if scope is None:
            t = self.P.sb("w", [128, rows // 128, cols], BF16)
        else:
            t = self.P.sbs(scope, "w", [128, rows // 128, cols], BF16)
        h = self.P.dma("gpsimd", t.ap(),
                       dram_ap.rearrange("(c p) n -> p c n", p=128), deps)
        return t, h

    def finish(self):
        self.P.barrier()
        sts = [self.P.dma("sync", self.out_d[128 * i:128 * (i + 1), :], self.x[i][:])
               for i in range(8)]
        self.P.emit("sync", lambda e: e.nop(), sts)
        return self.P.finalize()

    def cast_bf(self, dst_tiles, cs, deps_per_tile, tag=""):
        hs = []
        for i in range(8):
            eng = "vector" if i % 2 else "scalar"
            if eng == "vector":
                fn = (lambda e, o=dst_tiles[i], s=self.x[i], c=cs:
                      e.tensor_copy(o[:], s[:, c] if c is not None else s[:]))
            else:
                fn = (lambda e, o=dst_tiles[i], s=self.x[i], c=cs:
                      e.copy(o[:], s[:, c] if c is not None else s[:]))
            hs.append(self.P.emit(eng, fn, [deps_per_tile[i]]))
        return hs

    def proj(self, weight, w_dep, rhs_tiles, rhs_dep, cs, ncols, out_cb):
        hs = []
        for oc in range(8):
            pt, pd = self.pf.acquire()
            mm = None
            for k in range(8):
                mm = self.P.emit(
                    "tensor",
                    lambda e, p=pt, w=weight, kk=k, o=oc, r=rhs_tiles[k], c=cs, n=ncols:
                    e.matmul(p[:, 0:n], lhsT=w[:, kk, 128 * o:128 * (o + 1)],
                             rhs=(r[:, c] if c is not None else r[:]),
                             start=(kk == 0), stop=(kk == 7)),
                    [w_dep, rhs_dep[k]] + ([pd] if k == 0 else [mm]))
            ev = out_cb(oc, pt, mm)
            self.pf.release(pt, ev)
            hs.append(ev if not isinstance(ev, list) else ev[-1])
        return hs

    def bcast_row(self, row_ap, n, dst_ap, deps, f32=False):
        pt, pd = self.pf.acquire()
        ones = self.ones_row if f32 else self.ones_row_bf
        mm = self.P.emit("tensor", lambda e, p=pt, o1=ones: e.matmul(
            p[:, 0:n], lhsT=o1[0:1, :], rhs=row_ap,
            start=True, stop=True), deps + [pd, self.cr[2], self.cr[-1]])
        ev = self.P.emit("vector", lambda e, p=pt: e.tensor_copy(dst_ap, p[:, 0:n]),
                         [mm])
        self.pf.release(pt, ev)
        return ev

    def emit_l0_attn(self):
        P = self.P
        x = self.x
        C = 128
        NS = int(os.environ.get("DEV_NS", L // C))
        STAGE = int(os.environ.get("DEV_STAGE", 99))
        H, DK, DV = 16, 64, 64
        SCALE = DK ** -0.5
        LV = ("qs8s", "qs8e", "qs16", "qs32", "qs64", "ks8e", "ks16", "ks32", "ks64")
        sc = P.scope()

        wq, wq_h = self.load_w_bf(self.din("l0_wq", [D, D]), D, D, scope=sc)
        wf, wf_h = self.load_w_bf(self.din("l0_wf", [D, D]), D, D, scope=sc)
        wi, wi_h = self.load_w_bf(self.din("l0_wi", [D, D]), D, D, scope=sc)
        wo, wo_h = self.load_w_bf(self.din("l0_wo", [D, D]), D, D, scope=sc)
        gm_d = self.din("c_gmask", [3 * 128, 128])
        masks = P.sbs(sc, "gmask", [128, 3, 128])
        mask_h = P.dma("sync", masks.ap(), gm_d.rearrange("(m p) n -> p m n", p=128))
        m8d, m8o, m16 = masks[:, 0, :], masks[:, 1, :], masks[:, 2, :]

        Ft = [P.sbs(sc, "l0_F", [128, C]) for _ in range(8)]
        Bt = [P.sbs(sc, "l0_B", [128, 132]) for _ in range(8)]
        Gt = [P.sbs(sc, "l0_G", [128, C]) for _ in range(8)]
        Kt = [P.sbs(sc, "l0_LNK", [128, C]) for _ in range(8)]
        eb = [P.sbs(sc, "l0_eb", [128, C], BF16) for _ in range(8)]
        xbf = [P.sbs(sc, "l0_xbf", [128, C], BF16) for _ in range(8)]
        vtm = P.sbs(sc, "l0_vtm", [128, D], BF16)
        ksw = [P.sbs(sc, "l0_ksw", [128, 5, 128], BF16) for _ in range(8)]
        qsw = [P.sbs(sc, "l0_qsw", [128, 6, 128], BF16) for _ in range(8)]
        wtmp = [P.sbs(sc, "l0_wtmp", [128, 6, 128]) for _ in range(3)]
        wtmp_h = [None] * 3

        class _Lane:
            def __init__(self, t, lane):
                self.t, self.lane = t, lane

            def __getitem__(self, idx):
                if isinstance(idx, tuple):
                    rw, cc = idx
                else:
                    rw, cc = slice(None), slice(None)
                return self.t[rw, self.lane, cc]
        KLANE = {"ks8e": 0, "ks16": 1, "ks32": 2, "ks64": 3}
        QLANE = {"qs8s": 0, "qs8e": 1, "qs16": 2, "qs32": 3, "qs64": 4}
        lv = {n: [_Lane(ksw[i], KLANE[n]) for i in range(8)] for n in KLANE}
        lv.update({n: [_Lane(qsw[i], QLANE[n]) for i in range(8)] for n in QLANE})
        Zq = [_Lane(qsw[i], 5) for i in range(8)]
        KH = [_Lane(ksw[i], 4) for i in range(8)]
        KHtm = P.sbs(sc, "l0_khtm", [128, D], BF16)
        Sf = [P.sbs(sc, "l0_S", [128, DV]) for _ in range(8)]
        Sbf = [P.sbs(sc, "l0_Sbf", [128, DV], BF16) for _ in range(8)]
        wcol = [P.sbs(sc, "l0_wcol", [128, 1]) for _ in range(8)]
        osb = [P.sbs(sc, "l0_o", [128, C], BF16) for _ in range(8)]
        a_sb = {n: P.sbs(sc, f"l0_{n}", [128, 128], BF16)
                for n in ("a8d", "a8o", "a16", "a64", "a32")}
        rqb = P.sbs(sc, "l0_rqb", [128, C])
        rowf = P.sbs(sc, "l0_rowf", [1, C])
        rowq = P.sbs(sc, "l0_rowq", [1, C])
        rowqb = P.sbs(sc, "l0_rowqb", [1, C], BF16)

        s_init = []
        b_zero = []
        for i in range(8):
            s_init.append(P.emit("gpsimd", lambda e, s=Sf[i]: e.memset(s[:], 0.0)))
            s_init.append(P.emit("gpsimd", lambda e, s=Sbf[i]: e.memset(s[:], 0.0)))
            b_zero.append(P.emit("gpsimd", lambda e, b=Bt[i]:
                                 e.memset(b[:, 0:1], 0.0)))
        s_init = s_init + b_zero
        sbf_h = {i: s_init for i in range(8)}
        sf_h = {i: s_init for i in range(8)}
        entry_x = list(self.last_x)
        war = {}
        ar_last = {n: [mask_h] for n in a_sb}

        xbf_pending = None
        for sl in range(NS):
            cs = slice(C * sl, C * (sl + 1))
            if xbf_pending is not None:
                xbf_h = xbf_pending
            else:
                xbf_h = self.cast_bf(
                    xbf, cs,
                    [[entry_x[i]] + war.get("xbf", []) for i in range(8)])
            if STAGE < 1:
                bars = self.P.barrier()
                self.last_x = [bars["vector"]] * 8
                continue

            F_h = [None] * 8
            ef_h = [None] * 8

            def f_evac(oc, pt, mm):
                h1 = P.emit("vector", lambda e, p=pt, o=Ft[oc]:
                            e.tensor_copy(o[:], p[:, 0:C]),
                            [mm, war.get(("ft", oc))])
                h2 = P.emit("scalar", lambda e, p=pt, o=eb[oc]:
                            e.activation(out=o[:], in_=p[:, 0:C], func=AF.Exp),
                            [mm, h1, war.get("eb")])
                F_h[oc] = h1
                ef_h[oc] = h2
                return [h1, h2]
            self.proj(wf, wf_h, xbf, xbf_h, None, C, f_evac)

            if STAGE < 12:
                bars = self.P.barrier()
                self.last_x = [bars["vector"]] * 8
                continue
            pt, pd = self.pr.acquire()
            mm = None
            for k in range(8):
                mm = P.emit("tensor", lambda e, p=pt, r=eb[k], kk=k:
                            e.matmul(p[0:1, 0:C], lhsT=self.ones_col_bf[:, 0:1],
                                     rhs=r[:], start=(kk == 0), stop=(kk == 7)),
                            [ef_h[k], self.cr[1]] + ([pd] if k == 0 else [mm]))
            sumf_mm = mm
            lsf_h = P.emit("scalar", lambda e, p=pt: e.activation(
                out=rowf[:], in_=p[0:1, 0:C], func=AF.Ln),
                [mm, war.get("rowf")])
            self.pr.release(pt, lsf_h)
            pt2, pd2 = self.pf.acquire()
            mmb = P.emit("tensor", lambda e, p=pt2: e.matmul(
                p[:, 0:C], lhsT=self.ones_row[0:1, :], rhs=rowf[0:1, :],
                start=True, stop=True), [lsf_h, pd2, self.cr[2]])
            g_h = [P.emit("vector", lambda e, p=pt2, f=Ft[i]:
                          e.tensor_sub(f[:], f[:], p[:, 0:C]), [mmb, F_h[i]])
                   for i in range(8)]
            self.pf.release(pt2, g_h)

            if STAGE < 13:
                bars = self.P.barrier()
                self.last_x = [bars["vector"]] * 8
                continue
            B_h = {}
            for i in range(8):
                B_h[i] = P.emit(
                    "vector", lambda e, f=Ft[i], b=Bt[i]:
                    e.tensor_tensor_scan(
                        out=b[:, 1:129], data0=self.ones_chunk[:, 0:128],
                        data1=f[:], initial=0.0, op0=ALU.mult, op1=ALU.add),
                    [g_h[i], self.cr[3], war.get(("bt", i))])
            H_h = [None] * 8
            ft_read = [None] * 8
            for i in range(8):
                h1 = P.emit("scalar", lambda e, f=Ft[i], o=eb[i]:
                            e.activation(out=o[:], in_=f[:], func=AF.Exp),
                            [g_h[i], sumf_mm])
                ft_read[i] = h1
                h2 = P.emit("scalar", lambda e, s=eb[i], o=Kt[i]:
                            e.activation(out=o[:], in_=s[:], func=AF.Ln,
                                         bias=self.ones_col[:, 0:1], scale=-1.0),
                            [h1, self.cr[0], war.get(("kt", i))])
                H_h[i] = P.emit("vector", lambda e, kk=Kt[i], b=Bt[i]:
                                e.tensor_sub(kk[:], kk[:], b[:, 1:129]),
                                [h2, B_h[i]])

            ksw_h = [None] * 8
            kh1s = [None] * 8
            for i in range(8):
                slot = i % 3
                t = wtmp[slot]
                h1s = []
                for lane, ssz in enumerate((8, 16, 32, 64)):
                    nblk = 128 // ssz
                    bb = Bt[i][:, ssz::ssz][:, 0:nblk].unsqueeze(2) \
                        .broadcast_to([128, nblk, ssz])
                    eng2 = "vector" if (i + lane) % 2 == 0 else "gpsimd"
                    h1s.append(P.emit(
                        eng2,
                        lambda e, t2=t, a=Kt[i], b=bb, ln2=lane, ss2=ssz:
                        e.tensor_tensor(
                            out=t2[:, ln2, :].rearrange("p (b s) -> p b s", s=ss2),
                            in0=a.ap().rearrange("p (b s) -> p b s", s=ss2),
                            in1=b, op=ALU.add),
                        [H_h[i], B_h[i], b_zero[i], wtmp_h[slot]]))
                h1s.append(P.emit(
                    "vector" if i % 2 else "gpsimd",
                    lambda e, t2=t, k2=Kt[i], b=Bt[i]:
                    e.tensor_scalar(out=t2[:, 4, :], in0=k2[:],
                                    scalar1=b[:, 128:129], scalar2=None,
                                    op0=ALU.add),
                    [H_h[i], B_h[i], wtmp_h[slot]]))
                ksw_h[i] = P.emit("scalar", lambda e, t2=t, o=ksw[i]:
                                  e.activation(out=o[:, 0:5, :], in_=t2[:, 0:5, :],
                                               func=AF.Exp),
                                  h1s + [war.get(("lvk", i))])
                wtmp_h[slot] = ksw_h[i]
                kh1s[i] = list(h1s)
            kh_h = ksw_h

            if STAGE < 14:
                bars = self.P.barrier()
                self.last_x = [bars["vector"]] * 8
                continue
            G_h = [None] * 8
            eq_h = [None] * 8

            def q_evac(oc, pt_, mmh):
                h1 = P.emit("vector", lambda e, p=pt_, o=Gt[oc], b=Bt[oc]:
                            e.tensor_add(o[:], p[:, 0:C], b[:, 1:129]),
                            [mmh, B_h[oc], war.get(("gt", oc))])
                h2 = P.emit("scalar", lambda e, p=pt_, o=eb[oc]:
                            e.activation(out=o[:], in_=p[:, 0:C], func=AF.Exp),
                            [mmh, H_h[oc], h1])
                G_h[oc] = h1
                eq_h[oc] = h2
                return [h1, h2]
            self.proj(wq, wq_h, xbf, xbf_h, None, C, q_evac)

            qsw_h = [None] * 8
            qh1s = [None] * 8
            for i in range(8):
                slot = i % 3
                t = wtmp[slot]
                h1s = []
                for lane, (ssz, off) in enumerate(
                        ((8, 0), (8, 8), (16, 0), (32, 0), (64, 0))):
                    nblk = 128 // ssz
                    bb = Bt[i][:, off::ssz][:, 0:nblk].unsqueeze(2) \
                        .broadcast_to([128, nblk, ssz])
                    eng2 = "vector" if (i + lane) % 2 == 0 else "gpsimd"
                    h1s.append(P.emit(
                        eng2,
                        lambda e, t2=t, a=Gt[i], b=bb, ln2=lane, ss2=ssz:
                        e.tensor_tensor(
                            out=t2[:, ln2, :].rearrange("p (b s) -> p b s", s=ss2),
                            in0=a.ap().rearrange("p (b s) -> p b s", s=ss2),
                            in1=b, op=ALU.subtract),
                        [G_h[i], B_h[i], b_zero[i], wtmp_h[slot]]))
                h1s.append(P.emit("vector" if i % 2 else "gpsimd",
                                  lambda e, t2=t, g2=Gt[i]:
                                  e.tensor_copy(t2[:, 5, :], g2[:]),
                                  [G_h[i], wtmp_h[slot]]))
                qsw_h[i] = P.emit("scalar", lambda e, t2=t, o=qsw[i]:
                                  e.activation(out=o[:], in_=t2[:], func=AF.Exp),
                                  h1s + [war.get(("lvq", i))])
                wtmp_h[slot] = qsw_h[i]
                qh1s[i] = list(h1s)
            lv_h = {}
            for i in range(8):
                for n in ("ks8e", "ks16", "ks32", "ks64"):
                    lv_h[(i, n)] = ksw_h[i]
                for n in ("qs8s", "qs8e", "qs16", "qs32", "qs64"):
                    lv_h[(i, n)] = qsw_h[i]

            pt, pd = self.pr.acquire()
            mmq = None
            for k in range(8):
                mmq = P.emit("tensor", lambda e, p=pt, r=eb[k], kk=k:
                             e.matmul(p[0:1, 0:C], lhsT=self.ones_col_bf[:, 0:1],
                                      rhs=r[:], start=(kk == 0), stop=(kk == 7)),
                             [eq_h[k], self.cr[1]] + ([pd] if k == 0 else [mmq]))
            rq1 = P.emit("vector", lambda e, p=pt:
                         e.reciprocal(rowq[0:1, :], p[0:1, 0:C]),
                         [mmq, war.get("rowq")])
            self.pr.release(pt, rq1)
            rq2 = P.emit("scalar", lambda e: e.mul(rowqb[:], rowq[:], SCALE),
                         [rq1, war.get("rowqb")])

            if STAGE < 15:
                bars = self.P.barrier()
                self.last_x = [bars["vector"]] * 8
                continue
            vtm_h = []
            for hf in range(2):
                pt, pd = self.pf.acquire()
                mm = None
                for k in range(8):
                    mm = P.emit(
                        "tensor",
                        lambda e, p=pt, kk=k, h=hf, c=cs:
                        e.matmul(p[:, 0:512], lhsT=xbf[kk][:],
                                 rhs=wi[:, kk, 512 * h:512 * (h + 1)],
                                 start=(kk == 0), stop=(kk == 7)),
                        [wi_h, xbf_h[k]] + ([pd] if k == 0 else [mm]))
                ev = P.emit("vector", lambda e, p=pt, h=hf:
                            e.tensor_copy(vtm[:, 512 * h:512 * (h + 1)],
                                          p[:, 0:512]), [mm, war.get("vtm")])
                self.pf.release(pt, ev)
                vtm_h.append(ev)

            if STAGE < 20:
                bars = self.P.barrier()
                self.last_x = [bars["vector"]] * 8
                continue
            ktm_h = []
            for i in range(8):
                ptb, pdb = self.pb.acquire()
                t1 = P.emit("tensor", lambda e, p=ptb, s=KH[i]:
                            e.transpose(p[:, 0:128], in_=s[:],
                                        identity=self.idty_bf[:]),
                            [kh_h[i], pdb, self.cr[5]])
                t2 = P.emit("vector", lambda e, p=ptb, i2=i:
                            e.tensor_copy(KHtm[:, 128 * i2:128 * (i2 + 1)],
                                          p[:, 0:128]),
                            [t1, war.get(("khtm", i))])
                self.pb.release(ptb, t2)
                ktm_h.append(t2)
            rqb_h = self.bcast_row(rowqb[0:1, :], C, rqb[:],
                                   [rq2, war.get("rqb")])

            if STAGE < 30:
                bars = self.P.barrier()
                self.last_x = [bars["vector"]] * 8
                continue
            o_h = {i: [] for i in range(8)}
            o_by_tile = {}
            for h in range(H):
                ti, half = h // 2, h % 2
                rows = slice(64 * half, 64 * (half + 1))
                tp = (64 * half, 0)
                hcols = slice(64 * h, 64 * (h + 1))
                pa = {}
                for nm2, lt, rt in (("a8d", "ks8e", "qs8e"),
                                    ("a8o", "ks8e", "qs8s"),
                                    ("a16", "ks16", "qs16")):
                    pt, pd = self.pf.acquire()
                    mm = P.emit(
                        "tensor",
                        lambda e, p=pt, l=lv[lt][ti], r=lv[rt][ti], rw=rows, tpp=tp:
                        e.matmul(p[:, 0:128], lhsT=l[rw, :], rhs=r[rw, :],
                                 start=True, stop=True, tile_position=tpp),
                        [lv_h[(ti, lt)], lv_h[(ti, rt)], pd])
                    pa[nm2] = (pt, mm)
                evs = {}
                for nm2, mk in (("a8d", m8d), ("a8o", m8o), ("a16", m16)):
                    evs[nm2] = P.emit(
                        "vector", lambda e, p=pa[nm2][0], m=mk, o=a_sb[nm2]:
                        e.tensor_tensor(out=o[:], in0=p[:, 0:128], in1=m,
                                        op=ALU.mult),
                        [pa[nm2][1]] + ar_last[nm2])
                    self.pf.release(pa[nm2][0], evs[nm2])
                pt64, pd64 = self.pf.acquire()
                mm64 = P.emit("tensor",
                              lambda e, p=pt64, l=lv["ks64"][ti], r=lv["qs64"][ti],
                              rw=rows, tpp=tp:
                              e.matmul(p[0:64, 64:128], lhsT=l[rw, 0:64],
                                       rhs=r[rw, 64:128],
                                       start=True, stop=True, tile_position=tpp),
                              [lv_h[(ti, "ks64")], lv_h[(ti, "qs64")], pd64])
                evs["a64"] = P.emit("scalar", lambda e, p=pt64:
                                    e.copy(a_sb["a64"][0:64, 64:128], p[0:64, 64:128]),
                                    [mm64] + ar_last["a64"])
                self.pf.release(pt64, evs["a64"])
                pt32, pd32 = self.pf.acquire()
                mm32a = P.emit("tensor",
                               lambda e, p=pt32, l=lv["ks32"][ti], r=lv["qs32"][ti],
                               rw=rows, tpp=tp:
                               e.matmul(p[0:32, 32:64], lhsT=l[rw, 0:32],
                                        rhs=r[rw, 32:64],
                                        start=True, stop=True, tile_position=tpp),
                               [lv_h[(ti, "ks32")], lv_h[(ti, "qs32")], pd32])
                mm32b = P.emit("tensor",
                               lambda e, p=pt32, l=lv["ks32"][ti], r=lv["qs32"][ti],
                               rw=rows, tpp=(64 * half, 64):
                               e.matmul(p[64:96, 96:128], lhsT=l[rw, 64:96],
                                        rhs=r[rw, 96:128],
                                        start=True, stop=True, tile_position=tpp),
                               [lv_h[(ti, "ks32")], lv_h[(ti, "qs32")], mm32a])
                ev32a = P.emit("scalar", lambda e, p=pt32:
                               e.copy(a_sb["a32"][0:32, 32:64], p[0:32, 32:64]),
                               [mm32a] + ar_last["a32"])
                ev32b = P.emit("scalar", lambda e, p=pt32:
                               e.copy(a_sb["a32"][64:96, 96:128], p[64:96, 96:128]),
                               [mm32b, ev32a])
                evs["a32"] = ev32b
                self.pf.release(pt32, [ev32a, ev32b])
                po, po_d = self.pf.acquire()
                mo = P.emit("tensor",
                            lambda e, p=po, s=Sbf[ti], z=Zq[ti], rw=rows,
                            tpp=(64 * half, 64 * half):
                            e.matmul(p[rw, 0:128], lhsT=s[rw, :], rhs=z[rw, :],
                                     start=True, stop=False, tile_position=tpp),
                            [sbf_h[ti], qsw_h[ti], po_d])
                intra = [("a8d", None, None), ("a8o", None, None),
                         ("a16", None, None),
                         ("a64", (0, 64), (64, 128)),
                         ("a32", (0, 32), (32, 64)),
                         ("a32b", (64, 96), (96, 128))]
                for idx2, (nm2, jr, ir) in enumerate(intra):
                    key = "a32" if nm2 == "a32b" else nm2
                    last = (idx2 == len(intra) - 1)
                    if jr is None:
                        fn = (lambda e, p=po, v=vtm, hc=hcols, rw=rows, o=a_sb[key],
                              lst=last, tpp=(0, 64 * half):
                              e.matmul(p[rw, 0:128], lhsT=v[:, hc], rhs=o[:],
                                       start=False, stop=lst, tile_position=tpp))
                    else:
                        j0, j1 = jr
                        i0, i1 = ir
                        fn = (lambda e, p=po, v=vtm, hc=hcols, rw=rows, o=a_sb[key],
                              j0=j0, j1=j1, i0=i0, i1=i1, lst=last,
                              tpp=(j0, 64 * half):
                              e.matmul(p[rw, i0:i1],
                                       lhsT=v[j0:j1, hc], rhs=o[j0:j1, i0:i1],
                                       start=False, stop=lst, tile_position=tpp))
                    mo_new = P.emit("tensor", fn, [vtm_h, evs[key]],
                                    sdeps=[mo] if mo else [])
                    mo = mo_new
                oev = P.emit("vector",
                             lambda e, p=po, o=osb[ti], rw=rows:
                             e.tensor_tensor(out=o[rw, :], in0=p[rw, 0:128],
                                             in1=rqb[rw, :], op=ALU.mult),
                             [mo, rqb_h, war.get(("osb", ti))] + o_h[ti])
                self.pf.release(po, oev)
                o_h[ti].append(oev)
                for nm2 in ("a8d", "a8o", "a16", "a64", "a32"):
                    ar_last[nm2] = [mo]
                o_by_tile.setdefault(ti, []).append(oev)

            wc_h = [None] * 8
            dec_h = [None] * 8
            for ti in range(8 if STAGE >= 40 else 0):
                wc = P.emit("scalar", lambda e, b=Bt[ti], w=wcol[ti]:
                            e.activation(out=w[:], in_=b[:, 128:129], func=AF.Exp),
                            [B_h[ti], war.get(("wcol", ti))])
                wc_h[ti] = wc
                dec = P.emit("scalar", lambda e, s=Sf[ti], w=wcol[ti]:
                             e.mul(s[:], s[:], w[:, 0:1]),
                             [wc] + sf_h[ti] + o_by_tile[ti])
                dec_h[ti] = dec
                sus = []
                for half in range(2):
                    h = 2 * ti + half
                    rows = slice(64 * half, 64 * (half + 1))
                    ps2, ps2d = self.pf.acquire()
                    msu = P.emit("tensor",
                                 lambda e, p=ps2, kt=KHtm, v=vtm, rw=rows,
                                 hc=slice(64 * h, 64 * (h + 1)),
                                 tpp=(0, 64 * half):
                                 e.matmul(p[rw, 0:64], lhsT=kt[:, hc],
                                          rhs=v[:, hc],
                                          start=True, stop=True, tile_position=tpp),
                                 [ktm_h[ti], vtm_h, ps2d])
                    su = P.emit("vector", lambda e, p=ps2, s=Sf[ti], rw=rows:
                                e.tensor_add(s[rw, :], s[rw, :], p[rw, 0:64]),
                                [msu, dec])
                    self.pf.release(ps2, su)
                    sus.append(su)
                sf_h[ti] = sus
                sbf_h[ti] = [P.emit("scalar", lambda e, s=Sf[ti], d=Sbf[ti]:
                                    e.copy(d[:], s[:]), sus)]

            if STAGE < 50:
                bars = self.P.barrier()
                self.last_x = [bars["vector"]] * 8
                for i in range(8):
                    sf_h[i] = [bars["vector"]]
                    sbf_h[i] = [bars["vector"]]
                continue
            def wo_evac(oc, pt_, mmh):
                ev = P.emit("vector", lambda e, p=pt_, s=x[oc], c=cs:
                            e.tensor_add(s[:, c], s[:, c], p[:, 0:C]),
                            [mmh, self.last_x[oc]])
                self.last_x[oc] = ev
                return ev
            wo_hs = self.proj(wo, wo_h, osb, [o_h[i] for i in range(8)],
                              None, C, wo_evac)

            war = {"xbf": list(vtm_h), "eb": [mmq], "rowf": [g_h[0]],
                   "rowq": [rq2], "rowqb": [rqb_h],
                   "rqb": [o_by_tile[ti][-1] for ti in range(8)],
                   "vtm": [o_by_tile[ti][-1] for ti in range(8)]
                   + [h for ti in range(8) for h in sf_h[ti]]}
            for i in range(8):
                war[("ft", i)] = [B_h[i], ft_read[i]]
                war[("bt", i)] = kh1s[i] + qh1s[i] + [wc_h[i]] + [G_h[i]]
                war[("kt", i)] = kh1s[i]
                war[("gt", i)] = qh1s[i]
                war[("lvk", i)] = o_by_tile[i] + [ktm_h[i]]
                war[("lvq", i)] = list(o_by_tile[i])
                war[("khtm", i)] = list(sf_h[i])
                war[("osb", i)] = [wo_hs[-1]]
                war[("wcol", i)] = [dec_h[i]]
        bars = self.P.barrier()
        self.last_x = [bars["vector"]] * 8
        sc.close()

    def emit_postln(self, tag):
        P = self.P
        x = self.x
        sc = P.scope()
        xbf = [P.sbs(sc, f"ln_xbf_{tag}", [128, 512], BF16) for _ in range(8)]
        sq = [P.sbs(sc, f"ln_sq_{tag}", [128, 512], BF16) for _ in range(2)]
        srow = P.sbs(sc, f"ln_srow_{tag}", [1, L])
        ssrow = P.sbs(sc, f"ln_ssrow_{tag}", [1, L])
        trow = P.sbs(sc, f"ln_t_{tag}", [1, L])
        srow_b = P.sbs(sc, f"ln_srow_b_{tag}", [1, L], BF16)
        ssrow_b = P.sbs(sc, f"ln_ssrow_b_{tag}", [1, L], BF16)
        rbs = [P.sbs(sc, f"ln_rb_{tag}", [128, 512]) for _ in range(2)]
        mbs = [P.sbs(sc, f"ln_mb_{tag}", [128, 512]) for _ in range(2)]
        seg_h = []
        prev_cast = {i: None for i in range(8)}
        prev_sq = {}
        for ct in range(4):
            cseg = slice(512 * ct, 512 * (ct + 1))
            cast_h = []
            for i in range(8):
                eng = "vector" if i % 2 else "scalar"
                if eng == "vector":
                    fn = (lambda e, o=xbf[i], s2=x[i], c=cseg:
                          e.tensor_copy(o[:], s2[:, c]))
                else:
                    fn = (lambda e, o=xbf[i], s2=x[i], c=cseg:
                          e.copy(o[:], s2[:, c]))
                cast_h.append(P.emit(eng, fn, [self.last_x[i], prev_cast[i]]))
            pt, pd = self.pr.acquire()
            pt2, pd2 = self.pf.acquire()
            mm1 = mm2 = None
            for i in range(8):
                mm1 = P.emit("tensor",
                             lambda e, p=pt, r=xbf[i], kk=i:
                             e.matmul(p[0:1, 0:512], lhsT=self.ones_col_bf[:, 0:1],
                                      rhs=r[:], start=(kk == 0), stop=(kk == 7)),
                             [cast_h[i], self.cr[1]] + ([pd] if i == 0 else [mm1]))
                sqt = sq[i % 2]
                sh = P.emit("scalar",
                            lambda e, o=sqt, s2=xbf[i]:
                            e.activation(out=o[:], in_=s2[:], func=AF.Square),
                            [cast_h[i], prev_sq.get(i % 2)])
                mm2 = P.emit("tensor",
                             lambda e, p=pt2, r=sqt, kk=i:
                             e.matmul(p[0:1, 0:512], lhsT=self.ones_col_bf[:, 0:1],
                                      rhs=r[:], start=(kk == 0), stop=(kk == 7)),
                             [sh, self.cr[1]] + ([pd2] if i == 0 else [mm2]))
                prev_sq[i % 2] = mm2
                prev_cast[i] = [mm1, sh]
            e1 = P.emit("vector", lambda e, p=pt, c=cseg:
                        e.tensor_copy(srow[0:1, c], p[0:1, 0:512]), [mm1])
            e2 = P.emit("vector", lambda e, p=pt2, c=cseg:
                        e.tensor_copy(ssrow[0:1, c], p[0:1, 0:512]), [mm2])
            self.pr.release(pt, e1)
            self.pf.release(pt2, e2)
            seg_h.append((e1, e2))
        newx = list(self.last_x)
        war = {0: [], 1: []}
        seg_hh = {}
        for ct in range(4):
            cseg = slice(512 * ct, 512 * (ct + 1))
            e1, e2 = seg_h[ct]
            h1 = P.emit("scalar", lambda e, c=cseg:
                        e.mul(srow[0:1, c], srow[0:1, c], 1.0 / D), [e1])
            h2 = P.emit("scalar", lambda e, c=cseg:
                        e.activation(out=trow[0:1, c], in_=srow[0:1, c],
                                     func=AF.Square), [h1])
            h3 = P.emit("vector", lambda e, c=cseg: e.scalar_tensor_tensor(
                out=ssrow[0:1, c], in0=ssrow[0:1, c], scalar=1.0 / D,
                in1=trow[0:1, c], op0=ALU.mult, op1=ALU.subtract), [h2, e2])
            h5 = P.emit("scalar", lambda e, c=cseg:
                        e.activation(out=ssrow[0:1, c], in_=ssrow[0:1, c],
                                     func=AF.Ln, bias=self.eps_col[0:1, 0:1]),
                        [h3, self.cr[4]])
            h6 = P.emit("scalar", lambda e, c=cseg:
                        e.activation(out=ssrow[0:1, c], in_=ssrow[0:1, c],
                                     func=AF.Exp, scale=-0.5), [h5])
            h78 = P.emit("vector", lambda e, c=cseg: e.scalar_tensor_tensor(
                out=srow[0:1, c], in0=srow[0:1, c], scalar=-1.0,
                in1=ssrow[0:1, c], op0=ALU.mult, op1=ALU.mult), [h6, h1])
            h9 = P.emit("scalar", lambda e, c=cseg:
                        e.copy(ssrow_b[0:1, c], ssrow[0:1, c]), [h6])
            h10 = P.emit("scalar", lambda e, c=cseg:
                         e.copy(srow_b[0:1, c], srow[0:1, c]), [h78])
            seg_hh[ct] = (h9, h10)
        for ct in range(4):
            cseg = slice(512 * ct, 512 * (ct + 1))
            h9, h10 = seg_hh[ct]
            b1 = self.bcast_row(ssrow_b[0:1, cseg], 512, rbs[ct % 2][:],
                                [h9] + war[ct % 2])
            b2 = self.bcast_row(srow_b[0:1, cseg], 512, mbs[ct % 2][:],
                                [h10] + war[ct % 2])
            applies = []
            for i in range(8):
                eng = "gpsimd" if i % 4 == 3 else "vector"
                a = P.emit(eng, lambda e, s2=x[i], c=cseg, r=rbs[ct % 2]:
                           e.tensor_mul(s2[:, c], s2[:, c], r[:]),
                           [b1, newx[i]])
                n2 = P.emit(eng, lambda e, s2=x[i], c=cseg, m=mbs[ct % 2]:
                            e.tensor_add(s2[:, c], s2[:, c], m[:]), [a, b2])
                applies.append(n2)
                newx[i] = n2
            war[ct % 2] = applies
        self.last_x = newx
        bars = self.P.barrier()
        self.last_x = [bars["vector"]] * 8
        sc.close()

    def emit_l0_ffn(self):
        P = self.P
        x = self.x
        nc = P.nc
        sc = P.scope()
        E = 8
        CAP = 2048 * 8 // 8

        wg_d = self.din("l0f_wg", [D, E])
        we_d = self.din("l0f_we", [D, D])
        be_d = self.din("l0f_be", [1, D])
        ps_d = self.din("presel", [E, 1])

        wg = P.sbs(sc, "wg", [128, 8, E])
        wg_h = P.dma("sync", wg.ap(), wg_d.rearrange("(c p) n -> p c n", p=128))
        we, we_h = self.load_w_bf(we_d, D, D, scope=sc)
        be_bf = P.sbs(sc, "be", [1, D], BF16)
        be_h = P.dma("gpsimd", be_bf[:], be_d[:])
        presel = P.sbs(sc, "presel", [E, 1])
        psel_h = P.dma("sync", presel[:], ps_d[:])
        idf = P.sbs(sc, "idf", [128, 128])
        idf_h = P.dma("sync", idf[:], self.idty_d[:])
        ones512_bf = P.sbs(sc, "ones512b", [1, 512], BF16)
        o5_h = P.emit("gpsimd", lambda e: e.memset(ones512_bf[:], 1.0))
        tri = P.sbs(sc, "tri", [128, 128])
        t1 = P.emit("gpsimd", lambda e: e.memset(tri[:], 1.0))
        tri_h = P.emit("gpsimd", lambda e: e.affine_select(
            out=tri[:], in_=tri[:], compare_op=ALU.is_ge, fill=0.0,
            base=0, pattern=[[1, 128]], channel_multiplier=-1), [t1])

        mask_sb = [P.sbs(sc, "moemask", [128, E]) for _ in range(16)]
        gate_sb = P.sbs(sc, "moegate", [128, 16])
        gf_sb = P.sbs(sc, "moegf", [128, 16])
        counts = P.sbs(sc, "cnt", [E, 1])
        crow = P.sbs(sc, "crow", [1, E])
        gath = P.sbs(sc, "gath", [E, E])
        offrow = P.sbs(sc, "offrow", [1, E])
        offb = P.sbs(sc, "offb", [128, E])
        tmp8 = P.sbs(sc, "tmp8", [128, E])
        col1 = P.sbs(sc, "col1", [128, 1])
        grow = P.sbs(sc, "grow", [1, L], BF16)
        gb = P.sbs(sc, "gb", [128, 512])
        xbf = [P.sbs(sc, "moex", [128, 512], BF16) for _ in range(8)]
        ytmp = P.sbs(sc, "ytmp", [128, 512])
        gscr_d = nc.dram_tensor("moe_scratch", [16, 128], F32, kind="Internal").ap()
        cc_in = nc.dram_tensor("moe_cc_in", [1, E], F32, kind="Internal").ap()
        cc_out = nc.dram_tensor("moe_cc_out", [E, E], F32, kind="Internal").ap()

        cnt_mm = None
        pcnt, pcnt_d = self.pf.acquire()
        mask_h = [None] * 16
        gate_h = [None] * 16
        prev_sh = None
        for t in range(16):
            tc2 = slice(128 * t, 128 * (t + 1))
            pt, pd = self.pf.acquire()
            mm = None
            for k in range(8):
                mm = P.emit("tensor",
                            lambda e, p=pt, kk=k, c=tc2:
                            e.matmul(p[:, 0:E], lhsT=self.x[kk][:, c],
                                     rhs=wg[:, kk, :], start=(kk == 0), stop=(kk == 7)),
                            [wg_h, self.last_x[k]] + ([pd] if k == 0 else [mm]))
            nmx = P.emit("vector", lambda e, p=pt: e.reduce_max(
                col1[:, 0:1], p[:, 0:E], axis=AX.X, negate=True),
                [mm, gate_h[t - 1] if t > 0 else None])
            mh = P.emit("vector", lambda e, p=pt, o=mask_sb[t]:
                        e.tensor_scalar(out=o[:], in0=p[:, 0:E], scalar1=col1[:, 0:1],
                                        scalar2=None, op0=ALU.add), [nmx])
            mh2 = P.emit("vector", lambda e, o=mask_sb[t]:
                         e.tensor_scalar(out=o[:], in0=o[:], scalar1=-1e-30,
                                         scalar2=None, op0=ALU.is_ge), [mh])
            eh = P.emit("scalar", lambda e, p=pt, o=tmp8:
                        e.activation(out=o[:], in_=p[:, 0:E], func=AF.Exp,
                                     bias=col1[:, 0:1]),
                        [nmx, mh, prev_sh])
            sh = P.emit("vector", lambda e, o=tmp8: e.reduce_sum(
                col1[:, 0:1], o[:], axis=AX.X), [eh, mh2])
            gh = P.emit("vector", lambda e, t2=t: e.reciprocal(
                gate_sb[:, t2:t2 + 1], col1[:, 0:1]), [sh])
            mask_h[t] = mh2
            gate_h[t] = gh
            prev_sh = sh
            self.pf.release(pt, [mh2, eh])
            cnt_mm = P.emit("tensor",
                            lambda e, p=pcnt, m=mask_sb[t], t2=t:
                            e.matmul(p[0:E, 0:1], lhsT=m[:], rhs=self.ones_col[:, 0:1],
                                     start=(t2 == 0), stop=(t2 == 15)),
                            [mh2, self.cr[0], cnt_mm] + ([pcnt_d] if t == 0 else []))
        ev_c = P.emit("vector", lambda e, p=pcnt: e.tensor_copy(counts[:], p[0:E, 0:1]),
                      [cnt_mm])
        self.pf.release(pcnt, ev_c)
        ptc, ptc_d = self.pf.acquire()
        mmt = P.emit("tensor", lambda e, p=ptc: e.matmul(
            p[0:1, 0:E], lhsT=counts[:], rhs=idf[0:E, 0:E],
            start=True, stop=True), [ev_c, idf_h, ptc_d])
        ev_cr = P.emit("vector", lambda e, p=ptc: e.tensor_copy(crow[:], p[0:1, 0:E]),
                       [mmt])
        self.pf.release(ptc, ev_cr)
        d1 = P.dma("gpsimd", cc_in[:], crow[:], [ev_cr])
        ccs = P.emit("gpsimd", lambda e: e.collective_compute(
            "AllGather", ALU.bypass, replica_groups=[list(range(8))],
            ins=[cc_in.opt()], outs=[cc_out.opt()]), [d1])
        d2 = P.dma("gpsimd", gath[:], cc_out[:], [ccs])
        pto, pto_d = self.pf.acquire()
        mmo = P.emit("tensor", lambda e, p=pto: e.matmul(
            p[0:1, 0:E], lhsT=presel[:], rhs=gath[:],
            start=True, stop=True), [d2, psel_h, pto_d])
        ev_or = P.emit("vector", lambda e, p=pto: e.tensor_copy(offrow[:], p[0:1, 0:E]),
                       [mmo])
        self.pf.release(pto, ev_or)
        offb_h = self.bcast_row(offrow[0:1, :], E, offb[:], [ev_or], f32=True)

        gf_h = [None] * 16
        for t in range(16):
            pt, pd = self.pf.acquire()
            mm = None
            for j in range(t):
                mm = P.emit("tensor", lambda e, p=pt, m=mask_sb[j], jj=j:
                            e.matmul(p[:, 0:E], lhsT=self.ones_chunk[:, 0:128],
                                     rhs=m[:], start=(jj == 0), stop=False),
                            [mask_h[j], self.cr[3]] + ([pd] if j == 0 else [mm]))
            mm = P.emit("tensor", lambda e, p=pt, m=mask_sb[t], st=(t == 0):
                        e.matmul(p[:, 0:E], lhsT=tri[:], rhs=m[:],
                                 start=st, stop=False),
                        [mask_h[t], tri_h, mm] + ([pd] if t == 0 else []))
            mm = P.emit("tensor", lambda e, p=pt: e.matmul(
                p[:, 0:E], lhsT=idf[:], rhs=offb[:],
                start=False, stop=True), [offb_h, idf_h, mm])
            kh = P.emit("vector", lambda e, p=pt: e.tensor_scalar(
                out=tmp8[:], in0=p[:, 0:E], scalar1=float(CAP) + 0.5, scalar2=None,
                op0=ALU.is_lt), [mm, gf_h[t - 1] if t > 0 else prev_sh])
            kh2 = P.emit("vector", lambda e, m=mask_sb[t]:
                         e.tensor_mul(tmp8[:], tmp8[:], m[:]), [kh, mask_h[t]])
            kh3 = P.emit("vector", lambda e: e.reduce_sum(
                col1[:, 0:1], tmp8[:], axis=AX.X), [kh2])
            gf_h[t] = P.emit("vector", lambda e, t2=t: e.tensor_mul(
                gf_sb[:, t2:t2 + 1], gate_sb[:, t2:t2 + 1], col1[:, 0:1]),
                [kh3, gate_h[t]])
            self.pf.release(pt, [kh, kh2])
        ptg, ptg_d = self.pf.acquire()
        tg = P.emit("tensor", lambda e, p=ptg: e.transpose(
            p[0:16, 0:128], in_=gf_sb[:], identity=idf[:]), gf_h + [idf_h, ptg_d])
        ev_tg = P.emit("vector", lambda e, p=ptg: e.tensor_copy(
            gb[0:16, 0:128], p[0:16, 0:128]), [tg])
        self.pf.release(ptg, ev_tg)
        d3 = P.dma("gpsimd", gscr_d[:], gb[0:16, 0:128], [ev_tg])
        d4 = P.dma("gpsimd", grow[0:1, :], gscr_d.rearrange("a b -> (a b)").unsqueeze(0), [d3])

        for s2 in range(4):
            cs = slice(512 * s2, 512 * (s2 + 1))
            xbf_h = self.cast_bf(xbf, cs, self.last_x)
            gbh = self.bcast_row(grow[0:1, cs], 512, gb[:], [d4, ev_tg])
            prev_y = None
            for oc in range(8):
                pt, pd = self.pf.acquire()
                mm = None
                for k in range(8):
                    mm = P.emit("tensor",
                                lambda e, p=pt, kk=k, o=oc:
                                e.matmul(p[:, 0:512],
                                         lhsT=we[:, kk, 128 * o:128 * (o + 1)],
                                         rhs=xbf[kk][:], start=(kk == 0), stop=False),
                                [we_h, xbf_h[k]] + ([pd] if k == 0 else [mm]))
                mm = P.emit("tensor", lambda e, p=pt, o=oc: e.matmul(
                    p[:, 0:512], lhsT=be_bf[0:1, 128 * o:128 * (o + 1)],
                    rhs=ones512_bf[0:1, :], start=False, stop=True),
                    [be_h, o5_h, mm])
                h1 = P.emit("vector", lambda e, p=pt:
                            e.tensor_mul(ytmp[:], p[:, 0:512], gb[:]),
                            [mm, gbh, prev_y])
                h2 = P.emit("vector", lambda e, s3=x[oc], c=cs:
                            e.tensor_add(s3[:, c], s3[:, c], ytmp[:]),
                            [h1, self.last_x[oc]])
                self.last_x[oc] = h2
                prev_y = h2
                self.pf.release(pt, h1)
            bars = P.barrier()
            self.last_x = [bars["vector"]] * 8
        sc.close()

    def emit_l1_ffn(self):
        P = self.P
        x = self.x
        sc = P.scope()
        NSTATE = 4096
        w1_d = self.din("l1f_w1", [D, NSTATE])
        b1_d = self.din("l1f_b1", [1, NSTATE])
        w2_d = self.din("l1f_w2", [NSTATE, D])
        b2_d = self.din("l1f_b2", [1, D])

        w1 = P.sbs(sc, "w1", [128, 8, NSTATE], BF16)
        w1_hk = [P.dma("gpsimd", w1[:, k, :],
                       w1_d[128 * k:128 * (k + 1), :]) for k in range(8)]
        b1_bf = P.sbs(sc, "b1", [1, NSTATE], BF16)
        b1_h = P.dma("gpsimd", b1_bf[:], b1_d[:])
        b2_bf = P.sbs(sc, "b2", [1, D], BF16)
        b2_h = P.dma("gpsimd", b2_bf[:], b2_d[:])
        ones512_bf = P.sbs(sc, "ones512c", [1, 512], BF16)
        o5_h = P.emit("gpsimd", lambda e: e.memset(ones512_bf[:], 1.0))
        w2b = [P.sbs(sc, "w2b", [128, 32, 128], BF16) for _ in range(2)]
        xbf = [P.sbs(sc, "f1x", [128, 512], BF16) for _ in range(8)]
        h1t = [P.sbs(sc, "h1t", [128, 512], BF16) for _ in range(32)]
        w2_loads = {}
        w2_cons = {}

        for s2 in range(4):
            cs = slice(512 * s2, 512 * (s2 + 1))
            xbf_h = self.cast_bf(xbf, cs, self.last_x)
            h1_h = [None] * 32
            for hc in range(8):
                for pt2 in range(4):
                    ti = 4 * hc + pt2
                    ncol = slice(512 * hc + 128 * pt2, 512 * hc + 128 * (pt2 + 1))
                    pt, pd = self.pf.acquire()
                    mm = None
                    for k in range(8):
                        mm = P.emit("tensor",
                                    lambda e, p=pt, kk=k, nc2=ncol:
                                    e.matmul(p[:, 0:512], lhsT=w1[:, kk, nc2],
                                             rhs=xbf[kk][:],
                                             start=(kk == 0), stop=False),
                                    [w1_hk[k], xbf_h[k]]
                                    + ([pd] if k == 0 else [mm]))
                    mm = P.emit("tensor", lambda e, p=pt, nc2=ncol: e.matmul(
                        p[:, 0:512], lhsT=b1_bf[0:1, nc2], rhs=ones512_bf[0:1, :],
                        start=False, stop=True), [b1_h, o5_h, mm])
                    gh = P.emit("scalar", lambda e, p=pt, o=h1t[ti]:
                                e.activation(out=o[:], in_=p[:, 0:512], func=AF.Gelu),
                                [mm, h1_h[ti]])
                    h1_h[ti] = gh
                    self.pf.release(pt, gh)
            for oc in range(8):
                wb = w2b[oc % 2]
                ld = P.dma("gpsimd", wb.ap(),
                           w2_d[:, 128 * oc:128 * (oc + 1)]
                           .rearrange("(c p) n -> p c n", p=128),
                           [w2_cons.get(oc % 2)])
                w2_loads[(s2, oc)] = ld
                pt, pd = self.pf.acquire()
                mm = None
                for k in range(32):
                    mm = P.emit("tensor",
                                lambda e, p=pt, kk=k, w=wb:
                                e.matmul(p[:, 0:512], lhsT=w[:, kk, :], rhs=h1t[kk][:],
                                         start=(kk == 0), stop=False),
                                [ld, h1_h[k]] + ([pd] if k == 0 else [mm]))
                mm = P.emit("tensor", lambda e, p=pt, o=oc: e.matmul(
                    p[:, 0:512], lhsT=b2_bf[0:1, 128 * o:128 * (o + 1)],
                    rhs=ones512_bf[0:1, :], start=False, stop=True),
                    [b2_h, o5_h, mm])
                ev = P.emit("vector", lambda e, p=pt, s3=x[oc], c=cs:
                            e.tensor_add(s3[:, c], s3[:, c], p[:, 0:512]),
                            [mm, self.last_x[oc]])
                self.last_x[oc] = ev
                self.pf.release(pt, ev)
                w2_cons[oc % 2] = mm
            bars = P.barrier()
            self.last_x = [bars["vector"]] * 8
        sc.close()

    def emit_l1_attn(self):
        P = self.P
        x = self.x
        nc = P.nc
        sc = P.scope()
        H, DK, DV = 4, 256, 256
        SL = 512
        NSL = L // SL

        wq_d = self.din("l1_wq", [D, D])
        wk_d = self.din("l1_wk", [D, D])
        wv_d = self.din("l1_wv", [D, D])
        wo_d = self.din("l1_wo", [D, D])
        wb_d = self.din("l1_wb", [D, H])
        wg_d = self.din("l1_wg", [D, H])
        wl_d = self.din("l1_wl", [D, 12])
        bg_d = self.din("l1_bg", [1, H])
        bl_d = self.din("l1_bl", [1, 12])
        lt_d = self.din("l1_lt", [1, 12])
        rms_d = self.din("l1_rms", [128, 8])
        cq_d = self.din("l1_cq", [D, 4])
        ck_d = self.din("l1_ck", [D, 4])
        cv_d = self.din("l1_cv", [D, 4])
        fs_d = self.din("l1_fs", [D, 3])
        flw_d = self.din("l1_flw", [128, 8 * 31])
        dmask_d = self.din("c_dmask", [2 * 128, 128])

        wb_sb = P.sbs(sc, "l1_wbs", [128, 8, H], BF16)
        wg_sb = P.sbs(sc, "l1_wgs", [128, 8, H], BF16)
        wl_sb = P.sbs(sc, "l1_wls", [128, 8, 12], BF16)
        wb_h = P.dma("gpsimd", wb_sb.ap(), wb_d.rearrange("(c p) n -> p c n", p=128))
        wg_h = P.dma("gpsimd", wg_sb.ap(), wg_d.rearrange("(c p) n -> p c n", p=128))
        wl_h = P.dma("gpsimd", wl_sb.ap(), wl_d.rearrange("(c p) n -> p c n", p=128))
        cqc = P.sbs(sc, "l1_cqc", [128, 8, 4])
        ckc = P.sbs(sc, "l1_ckc", [128, 8, 4])
        cvc = P.sbs(sc, "l1_cvc", [128, 8, 4])
        fsc = P.sbs(sc, "l1_fsc", [128, 8, 3])
        cq_h = P.dma("sync", cqc.ap(), cq_d.rearrange("(c p) n -> p c n", p=128))
        ck_h = P.dma("sync", ckc.ap(), ck_d.rearrange("(c p) n -> p c n", p=128))
        cv_h = P.dma("sync", cvc.ap(), cv_d.rearrange("(c p) n -> p c n", p=128))
        fs_h = P.dma("sync", fsc.ap(), fs_d.rearrange("(c p) n -> p c n", p=128))
        bg_bf = P.sbs(sc, "l1_bg", [1, H], BF16)
        bl_bf = P.sbs(sc, "l1_bl", [1, 12], BF16)
        bg_h = P.dma("gpsimd", bg_bf[:], bg_d[:])
        bl_h = P.dma("gpsimd", bl_bf[:], bl_d[:])
        lt_sb = P.sbs(sc, "l1_lt", [1, 12])
        lt_h = P.dma("sync", lt_sb[:], lt_d[:])
        rms_sb = P.sbs(sc, "l1_rms", [128, 8])
        rms_h = P.dma("sync", rms_sb.ap(), rms_d[:])
        dmask = P.sbs(sc, "l1_dm", [128, 2, 128], BF16)
        dm_h = P.dma("gpsimd", dmask.ap(), dmask_d.rearrange("(m p) n -> p m n", p=128))
        mneg_low = dmask[:, 0, :]
        m_up_incl = dmask[:, 1, :]
        eps6 = P.sbs(sc, "eps6", [1, 1])
        e6_h = P.emit("gpsimd", lambda e: e.memset(eps6[:], 1e-6))
        ones1_bf = P.sbs(sc, "ones1b", [1, 128], BF16)
        o1b_h = P.emit("gpsimd", lambda e: e.memset(ones1_bf[:], 1.0))
        etrow = P.sbs(sc, "etrow", [1, 12], BF16)
        et1 = P.emit("scalar", lambda e: e.activation(
            out=etrow[:], in_=lt_sb[:], func=AF.Exp, scale=-1.0), [lt_h])
        etb = P.sbs(sc, "etb", [128, 12])
        etb_h = self.bcast_row(etrow[0:1, :], 12, etb[:], [et1])

        wbuf = [P.sbs(sc, "l1_w", [128, 8, 256], BF16) for _ in range(2)]
        flw = P.sbs(sc, "l1_flw", [128, 8, 31])
        flw_h = P.dma("sync", flw.ap(), flw_d.rearrange("p (o t) -> p o t", t=31))
        flh = [P.sbs(sc, "l1_flh", [128, n, 128], BF16) for n in (11, 10, 10)]
        pad = [P.sbs(sc, "l1_pad", [128, 3 + SL], BF16) for _ in range(8)]
        vsil = [P.sbs(sc, "l1_vsil", [128, 30 + SL], BF16) for _ in range(8)]
        tails = {t: [P.sbs(sc, f"l1_tl{t}", [128, 3], BF16) for _ in range(8)]
                 for t in ("q", "k", "v")}
        vtails = [P.sbs(sc, "l1_vtl", [128, 30], BF16) for _ in range(8)]
        qn = [P.sbs(sc, "l1_qn", [128, SL], BF16) for _ in range(8)]
        kn = [P.sbs(sc, "l1_kn", [128, SL], BF16) for _ in range(8)]
        dtile = [P.sbs(sc, "l1_dt", [128, 128], BF16) for _ in range(2)]
        vtm_sets = [P.sbs(sc, "l1_vtm", [128, DV], BF16) for _ in range(4)]
        ktm_sets = [P.sbs(sc, "l1_ktm", [128, DK], BF16) for _ in range(4)]
        kbtm_sets = [P.sbs(sc, "l1_kbtm", [128, DK], BF16) for _ in range(4)]
        beta_tm = [P.sbs(sc, "l1_btm", [128, H]) for _ in range(4)]
        gates_tm = [P.sbs(sc, "l1_gtm", [128, 16]) for _ in range(4)]
        dsc_sets = [{n: P.sbs(sc, f"l1_{n}", [128, 128], BF16)
                     for n in ("Nt", "Mt", "N2", "M2", "N4", "M4", "N8", "M8",
                               "N16", "M16", "N32", "M32", "N64", "Xc", "attn",
                               "u1", "u2", "w1", "w2", "ws1", "ws2")}
                    for _ in range(4)]
        Sf = [P.sbs(sc, "l1_S", [128, DV]) for _ in range(8)]
        Sbf = [P.sbs(sc, "l1_Sb", [128, DV], BF16) for _ in range(8)]
        macc = [P.sbs(sc, "l1_macc", [128, SL]) for _ in range(8)]
        mtmp = P.sbs(sc, "l1_mtmp", [128, SL], BF16)
        mtmp2 = P.sbs(sc, "l1_mtmp2", [128, SL], BF16)
        gbc = [P.sbs(sc, "l1_gbc", [128, SL], BF16) for _ in range(2)]
        gbp = [P.sbs(sc, "l1_gbp", [128, SL], BF16) for _ in range(3)]
        l2row = P.sbs(sc, "l1_l2r", [1, SL])
        l2rb = P.sbs(sc, "l1_l2rb", [1, SL], BF16)
        rbc = P.sbs(sc, "l1_rbc", [128, SL], BF16)
        sq_bf = P.sbs(sc, "l1_sqbf", [128, SL], BF16)
        osb = [P.sbs(sc, "l1_osb", [128, SL], BF16) for _ in range(8)]
        selg_d = self.din("c_selg", [16, 16 * 128])
        selg = P.sbs(sc, "l1_selg", [16, 16, 128], BF16)
        selg_h = P.dma("gpsimd", selg.ap(),
                       selg_d.rearrange("a (r n) -> a r n", n=128))
        gtr4 = P.sbs(sc, "l1_gtr4", [16, 4, 128], BF16)
        idf2 = P.sbs(sc, "l1_idf", [128, 128])
        idf2_h = P.dma("sync", idf2[:], self.idty_d[:])

        s_init = []
        for i in range(8):
            s_init.append(P.emit("gpsimd", lambda e, s2=Sf[i]: e.memset(s2[:], 0.0)))
            s_init.append(P.emit("gpsimd", lambda e, s2=Sbf[i]: e.memset(s2[:], 0.0)))
            for t in ("q", "k", "v"):
                s_init.append(P.emit("gpsimd", lambda e, s2=tails[t][i]:
                                     e.memset(s2[:], 0.0)))
            s_init.append(P.emit("gpsimd", lambda e, s2=vtails[i]: e.memset(s2[:], 0.0)))
        init_bar = P.barrier()
        self.last_x = [init_bar["vector"]] * 8
        sbf_h = {i: [init_bar["vector"]] for i in range(8)}
        sf_h = {i: [init_bar["vector"]] for i in range(8)}

        xbf = [P.sbs(sc, "l1_xbf", [128, SL], BF16) for _ in range(8)]
        wslot = {"i": 0, "cons": {0: None, 1: None}}

        def stream_w(dram, col0):
            bidx = wslot["i"] % 2
            buf = wbuf[bidx]
            h = P.dma("gpsimd", buf.ap(),
                      dram[:, col0:col0 + 256].rearrange("(c p) n -> p c n", p=128),
                      [wslot["cons"][bidx]])
            wslot["i"] += 1
            return buf, h, bidx

        PRB = os.environ.get("DEV_L1_PROBE")

        def probe(src_tiles, cs2, deps_fn):
            hs3 = []
            for i in range(8):
                hs3.append(P.emit("vector",
                                  lambda e, s3=x[i], o=src_tiles[i], c=cs2:
                                  e.tensor_copy(s3[:, c], o[:]),
                                  [deps_fn(i), self.last_x[i]]))
            return hs3

        for sl2 in range(NSL):
            cs = slice(SL * sl2, SL * (sl2 + 1))
            xbf_h = self.cast_bf(xbf, cs, self.last_x)
            tens_h = {}
            dt_use = {0: None, 1: None}

            if PRB == "xbf":
                for i in range(8):
                    P.emit("vector", lambda e, s3=x[i], o=xbf[i], c=cs:
                           e.tensor_copy(s3[:, c], o[:]),
                           [xbf_h[i], self.last_x[i]])
                bars = P.barrier()
                self.last_x = [bars["vector"]] * 8
                continue
            bg_done = []
            for ch in range(4 if not os.environ.get("DEV_L1_NOGATES") else 0):
                lcol = slice(128 * ch, 128 * (ch + 1))
                ptb, pdb = self.pf.acquire()
                mm = None
                for k in range(8):
                    mm = P.emit("tensor",
                                lambda e, p=ptb, kk=k, c2=lcol:
                                e.matmul(p[:, 0:H], lhsT=xbf[kk][:, c2],
                                         rhs=wb_sb[:, kk, :],
                                         start=(kk == 0), stop=(kk == 7)),
                                [wb_h, xbf_h[k]] + ([pdb] if k == 0 else [mm]))
                bh = P.emit("scalar", lambda e, p=ptb, o=beta_tm[ch]:
                            e.activation(out=o[:], in_=p[:, 0:H], func=AF.Sigmoid),
                            [mm])
                self.pf.release(ptb, bh)
                ptg, pdg = self.pf.acquire()
                mm = None
                for k in range(8):
                    mm = P.emit("tensor",
                                lambda e, p=ptg, kk=k, c2=lcol:
                                e.matmul(p[:, 0:12], lhsT=xbf[kk][:, c2],
                                         rhs=wl_sb[:, kk, :],
                                         start=(kk == 0), stop=False),
                                [wl_h, xbf_h[k]] + ([pdg] if k == 0 else [mm]))
                mm = P.emit("tensor", lambda e, p=ptg: e.matmul(
                    p[:, 0:12], lhsT=ones1_bf[0:1, :], rhs=bl_bf[0:1, :],
                    start=False, stop=True), [bl_h, o1b_h, mm])
                gt = gates_tm[ch]
                g1 = P.emit("vector", lambda e, p=ptg, o=gt:
                            e.tensor_mul(o[:, 4:16], p[:, 0:12], etb[:]),
                            [mm, etb_h])
                self.pf.release(ptg, g1)
                m1 = P.emit("vector", lambda e, o=gt: e.tensor_tensor(
                    out=o[:, 0:4], in0=o[:, 4:16:3], in1=o[:, 5:16:3], op=ALU.max),
                    [g1])
                m2 = P.emit("vector", lambda e, o=gt: e.tensor_tensor(
                    out=o[:, 0:4], in0=o[:, 0:4], in1=o[:, 6:16:3], op=ALU.max),
                    [m1])
                subs = []
                for j in range(3):
                    sj = P.emit("vector", lambda e, o=gt, jj=j: e.tensor_sub(
                        o[:, 4 + jj:16:3], o[:, 4 + jj:16:3], o[:, 0:4]),
                        [m2] + subs)
                    subs.append(sj)
                ex = P.emit("scalar", lambda e, o=gt: e.activation(
                    out=o[:, 4:16], in_=o[:, 4:16], func=AF.Exp), subs)
                sm = P.emit("vector", lambda e, o=gt: e.tensor_add(
                    o[:, 0:4], o[:, 4:16:3], o[:, 5:16:3]), [ex])
                sm2 = P.emit("vector", lambda e, o=gt: e.tensor_add(
                    o[:, 0:4], o[:, 0:4], o[:, 6:16:3]), [sm])
                rc = P.emit("vector", lambda e, o=gt: e.reciprocal(
                    o[:, 0:4], o[:, 0:4]), [sm2])
                mls = []
                for j in range(3):
                    mj = P.emit("vector", lambda e, o=gt, jj=j: e.tensor_mul(
                        o[:, 4 + jj:16:3], o[:, 4 + jj:16:3], o[:, 0:4]),
                        [rc] + mls)
                    mls.append(mj)
                ptw, pdw = self.pf.acquire()
                mm = None
                for k in range(8):
                    mm = P.emit("tensor",
                                lambda e, p=ptw, kk=k, c2=lcol:
                                e.matmul(p[:, 0:H], lhsT=xbf[kk][:, c2],
                                         rhs=wg_sb[:, kk, :],
                                         start=(kk == 0), stop=False),
                                [wg_h, xbf_h[k]] + ([pdw] if k == 0 else [mm]))
                mm = P.emit("tensor", lambda e, p=ptw: e.matmul(
                    p[:, 0:H], lhsT=ones1_bf[0:1, :], rhs=bg_bf[0:1, :],
                    start=False, stop=True), [bg_h, o1b_h, mm])
                wgh = P.emit("scalar", lambda e, p=ptw, o=gt: e.activation(
                    out=o[:, 0:4], in_=p[:, 0:H], func=AF.Sigmoid),
                    [mm, rc] + mls)
                self.pf.release(ptw, wgh)
                bg_done += [wgh, bh] + mls
            gds = []
            for ch2 in range(4 if not os.environ.get("DEV_L1_NOGATES") else 0):
                ptt, ptt_d = self.pf.acquire()
                tg2 = P.emit("tensor", lambda e, p=ptt, g2=gates_tm[ch2]:
                             e.transpose(p[0:16, 0:128], in_=g2[:], identity=idf2[:]),
                             bg_done + [idf2_h, ptt_d])
                evt = P.emit("vector", lambda e, p=ptt, c2=ch2: e.tensor_copy(
                    gtr4[:, c2, :], p[0:16, 0:128]), [tg2])
                self.pf.release(ptt, evt)
                gds.append(evt)

            for tname in ("q", "k", "v"):
                dram = {"q": wq_d, "k": wk_d, "v": wv_d}[tname]
                cw, cw_h = {"q": (cqc, cq_h), "k": (ckc, ck_h),
                            "v": (cvc, cv_h)}[tname]
                proj_h = []
                for half2 in range(4):
                    wbuf_, wh_, bidx = stream_w(dram, 256 * half2)
                    for oc in range(2 * half2, 2 * half2 + 2):
                        th = P.emit("vector",
                                    lambda e, d2=pad[oc], t2=tails[tname][oc]:
                                    e.tensor_copy(d2[:, 0:3], t2[:]),
                                    [s_init, tens_h.get(("padr", oc)),
                                     tens_h.get(("padc", oc))])
                        pt, pd = self.pf.acquire()
                        mm = None
                        for k in range(8):
                            mm = P.emit(
                                "tensor",
                                lambda e, p=pt, kk=k, o=oc - 2 * half2, w=wbuf_:
                                e.matmul(p[:, 0:SL],
                                         lhsT=w[:, kk, 128 * o:128 * (o + 1)],
                                         rhs=xbf[kk][:],
                                         start=(kk == 0), stop=(kk == 7)),
                                [wh_, xbf_h[k]] + ([pd] if k == 0 else [mm]))
                        ph = P.emit("scalar", lambda e, p=pt, d2=pad[oc]:
                                    e.copy(d2[:, 3:3 + SL], p[:, 0:SL]), [mm, th])
                        self.pf.release(pt, ph)
                        proj_h.append((th, ph))
                        wslot["cons"][bidx] = mm
                if PRB == "praw" and tname == "q":
                    for i in range(8):
                        P.emit("vector", lambda e, s3=x[i], o=pad[i], c=cs:
                               e.tensor_copy(s3[:, c], o[:, 3:3 + SL]),
                               [proj_h[i][1], self.last_x[i]])
                    break
                for oc in range(8):
                    th, ph = proj_h[oc]
                    tsv = P.emit("vector", lambda e, d2=pad[oc],
                                 t2=tails[tname][oc]:
                                 e.tensor_copy(t2[:], d2[:, SL:SL + 3]), [ph])
                    pt, pd = self.pf.acquire()
                    mm = None
                    for tap in range(4):
                        dt2 = dtile[tap % 2]
                        bh2 = P.emit("vector",
                                     lambda e, d2=dt2, w=cw, o=oc, t2=tap:
                                     e.tensor_scalar(out=d2[:], in0=self.idty_bf[:],
                                                     scalar1=w[:, o, t2:t2 + 1],
                                                     scalar2=None, op0=ALU.mult),
                                     [cw_h, self.cr[5], dt_use[tap % 2]])
                        mm = P.emit("tensor",
                                    lambda e, p=pt, d2=dt2, s2=pad[oc], t2=tap:
                                    e.matmul(p[:, 0:SL], lhsT=d2[:],
                                             rhs=s2[:, t2:t2 + SL],
                                             start=(t2 == 0), stop=(t2 == 3)),
                                    [bh2, ph, th] + ([pd] if tap == 0 else [mm]))
                        dt_use[tap % 2] = mm
                    if tname == "v":
                        vh = P.emit("vector", lambda e, d2=vsil[oc], t2=vtails[oc]:
                                    e.tensor_copy(d2[:, 0:30], t2[:]),
                                    [s_init, tens_h.get(("vsr", oc))])
                        sh2 = P.emit("scalar", lambda e, p=pt, d2=vsil[oc]:
                                     e.activation(out=d2[:, 30:30 + SL],
                                                  in_=p[:, 0:SL], func=AF.Silu),
                                     [mm, vh])
                        tens_h[("vsave", oc)] = P.emit(
                            "vector", lambda e, d2=vsil[oc], t2=vtails[oc]:
                            e.tensor_copy(t2[:], d2[:, SL:SL + 30]), [sh2])
                    else:
                        dst2 = qn[oc] if tname == "q" else kn[oc]
                        sh2 = P.emit("scalar", lambda e, p=pt, d2=dst2:
                                     e.activation(out=d2[:], in_=p[:, 0:SL],
                                                  func=AF.Silu),
                                     [mm, tens_h.get(("sil", tname, oc))])
                    tens_h[("sil", tname, oc)] = sh2
                    tens_h[("padr", oc)] = tsv
                    tens_h[("padc", oc)] = mm
                    self.pf.release(pt, sh2)

            if PRB == "praw":
                bars = P.barrier()
                self.last_x = [bars["vector"]] * 8
                continue
            if PRB in ("qsil", "ksil"):
                tiles2 = qn if PRB == "qsil" else kn
                probe(tiles2, cs, lambda i: tens_h[("sil", PRB[0], i)])
                bars = P.barrier()
                self.last_x = [bars["vector"]] * 8
                continue
            for tname, tiles in (("q", qn), ("k", kn)):
                for h in range(H):
                    t0, t1 = 2 * h, 2 * h + 1
                    pt, pd = self.pf.acquire()
                    mm = None
                    for i2, ti in enumerate((t0, t1)):
                        sq2 = P.emit("scalar", lambda e, s2=tiles[ti]:
                                     e.activation(out=sq_bf[:], in_=s2[:],
                                                  func=AF.Square),
                                     [tens_h[("sil", tname, ti)],
                                      tens_h.get("sqd"), mm])
                        mm = P.emit("tensor", lambda e, p=pt, i3=i2:
                                    e.matmul(p[0:1, 0:SL],
                                             lhsT=self.ones_col_bf[:, 0:1],
                                             rhs=sq_bf[:], start=(i3 == 0),
                                             stop=(i3 == 1)),
                                    [sq2, self.cr[1]] + ([pd] if i2 == 0 else []))
                        tens_h["sqd"] = mm
                    r1 = P.emit("scalar", lambda e, p=pt: e.activation(
                        out=l2row[:], in_=p[0:1, 0:SL], func=AF.Ln,
                        bias=eps6[0:1, 0:1]), [mm, e6_h, tens_h.get("l2u")])
                    r2 = P.emit("scalar", lambda e: e.activation(
                        out=l2rb[:], in_=l2row[:], func=AF.Exp, scale=-0.5), [r1])
                    self.pf.release(pt, r1)
                    rb = self.bcast_row(l2rb[0:1, :], SL, rbc[:],
                                        [r2, tens_h.get("rbcu")])
                    u1 = P.emit("vector", lambda e, s2=tiles[t0]:
                                e.tensor_mul(s2[:], s2[:], rbc[:]),
                                [rb, tens_h[("sil", tname, t0)]])
                    u2 = P.emit("vector", lambda e, s2=tiles[t1]:
                                e.tensor_mul(s2[:], s2[:], rbc[:]),
                                [rb, u1, tens_h[("sil", tname, t1)]])
                    tens_h[("sil", tname, t0)] = u1
                    tens_h[("sil", tname, t1)] = u2
                    tens_h["l2u"] = [u1, u2]
                    tens_h["rbcu"] = [u1, u2]

            if PRB in ("qn", "kn"):
                tiles2 = qn if PRB == "qn" else kn
                probe(tiles2, cs, lambda i: tens_h[("sil", PRB[0], i)])
                bars = P.barrier()
                self.last_x = [bars["vector"]] * 8
                continue
            for ch in range(int(os.environ.get("DEV_L1_NCH", 4))):
                lcol = slice(128 * ch, 128 * (ch + 1))

                def delta_head(h, si):
                    t0, t1 = 2 * h, 2 * h + 1
                    dsc2 = dsc_sets[si]
                    v_tm = vtm_sets[si]
                    k_tm = ktm_sets[si]
                    kb_tm = kbtm_sets[si]
                    tkey = lambda n2: (si, n2)
                    tms = []
                    for (srcg, sname, off, dst2) in ((vsil, "v", 30, v_tm),
                                                     (kn, "k", 0, k_tm)):
                        for i2, ti in enumerate((t0, t1)):
                            ptb2, pdb2 = self.pb.acquire()
                            if srcg is vsil:
                                src_ap = srcg[ti][:, off + 128 * ch:
                                                  off + 128 * (ch + 1)]
                                sdep = tens_h[("vsave", ti)]
                            else:
                                src_ap = srcg[ti][:, lcol]
                                sdep = tens_h[("sil", "k", ti)]
                            tr = P.emit("tensor", lambda e, p=ptb2, s2=src_ap:
                                        e.transpose(p[:, 0:128], in_=s2,
                                                    identity=self.idty_bf[:]),
                                        [sdep, pdb2, self.cr[5]])
                            ev = P.emit("scalar", lambda e, p=ptb2, d2=dst2, i3=i2:
                                        e.copy(d2[:, 128 * i3:128 * (i3 + 1)],
                                               p[:, 0:128]),
                                        [tr, tens_h.get(tkey(("tmu", sname, i2)))])
                            self.pb.release(ptb2, ev)
                            tms.append(ev)
                    yield
                    bcol = beta_tm[ch]
                    vb = P.emit("vector", lambda e, b2=bcol, hh=h, vt2=v_tm:
                                e.tensor_scalar(out=vt2[:], in0=vt2[:],
                                                scalar1=b2[:, hh:hh + 1],
                                                scalar2=None, op0=ALU.mult),
                                tms + bg_done + [tens_h.get(tkey("vtmu"))])
                    kb2 = P.emit("vector", lambda e, b2=bcol, hh=h, kt2=k_tm,
                                 kb3=kb_tm:
                                 e.tensor_scalar(out=kb3[:], in0=kt2[:],
                                                 scalar1=b2[:, hh:hh + 1],
                                                 scalar2=None, op0=ALU.mult),
                                 tms + bg_done + [tens_h.get(tkey("kbu"))])
                    pa, pa_d = self.pf.acquire()
                    mm = None
                    for i2, ti in enumerate((t0, t1)):
                        mm = P.emit("tensor",
                                    lambda e, p=pa, s2=kn[ti], i3=i2, c2=lcol:
                                    e.matmul(p[:, 0:128], lhsT=s2[:, c2],
                                             rhs=s2[:, c2], start=(i3 == 0),
                                             stop=(i3 == 1)),
                                    [tens_h[("sil", "k", ti)]]
                                    + ([pa_d] if i2 == 0 else []),
                                    sdeps=[mm] if mm else [])
                    nh = P.emit("vector", lambda e, p=pa, b2=bcol, hh=h, d2=dsc2:
                                e.scalar_tensor_tensor(
                                    out=d2["Nt"][:], in0=p[:, 0:128],
                                    scalar=b2[:, hh:hh + 1], in1=mneg_low,
                                    op0=ALU.mult, op1=ALU.mult),
                                [mm, dm_h, tens_h.get(tkey("Ntu"))] + bg_done)
                    self.pf.release(pa, nh)
                    yield
                    ptb2, pdb2 = self.pb.acquire()
                    trm = P.emit("tensor", lambda e, p=ptb2, d2=dsc2: e.transpose(
                        p[:, 0:128], in_=d2["Nt"][:], identity=self.idty_bf[:]),
                        [nh, pdb2, self.cr[5]])
                    mh = P.emit("scalar", lambda e, p=ptb2, d2=dsc2: e.copy(
                        d2["Mt"][:], p[:, 0:128]), [trm, tens_h.get(tkey("Mtu"))])
                    self.pb.release(ptb2, mh)
                    yield
                    pw_h = {"Nt": nh, "Mt": mh}
                    for (dstn, lt2, rt2) in (
                        ("N2", "Mt", "Nt"), ("M2", "Nt", "Mt"),
                        ("N4", "M2", "N2"), ("M4", "N2", "M2"),
                        ("N8", "M4", "N4"), ("M8", "N4", "M4"),
                        ("N16", "M8", "N8"), ("M16", "N8", "M8"),
                        ("N32", "M16", "N16"), ("M32", "N16", "M16"),
                        ("N64", "M32", "N32"),
                    ):
                        pp, pp_d = self.pf.acquire()
                        mm2 = P.emit("tensor",
                                     lambda e, p=pp, l2=dsc2[lt2], r2=dsc2[rt2]:
                                     e.matmul(p[:, 0:128], lhsT=l2[:], rhs=r2[:],
                                              start=True, stop=True),
                                     [pw_h[lt2], pw_h[rt2], pp_d])
                        ev2 = P.emit("scalar", lambda e, p=pp, d2=dsc2[dstn]:
                                     e.copy(d2[:], p[:, 0:128]),
                                     [mm2, tens_h.get(tkey((dstn, "u")))])
                        self.pf.release(pp, ev2)
                        pw_h[dstn] = ev2
                        yield
                    xh = P.emit("vector", lambda e, d2=dsc2: e.scalar_tensor_tensor(
                        out=d2["Xc"][:], in0=d2["Mt"][:], scalar=1.0,
                        in1=self.idty_bf[:], op0=ALU.mult, op1=ALU.add),
                        [mh, self.cr[5], tens_h.get(tkey("Xcu"))])
                    for j2 in ("N2", "N4", "N8", "N16", "N32", "N64"):
                        pp, pp_d = self.pf.acquire()
                        mm2 = P.emit("tensor", lambda e, p=pp, l2=dsc2[j2], d2=dsc2:
                                     e.matmul(p[:, 0:128], lhsT=l2[:],
                                              rhs=d2["Xc"][:],
                                              start=True, stop=False),
                                     [pw_h[j2], xh, pp_d])
                        mm2 = P.emit("tensor", lambda e, p=pp, d2=dsc2: e.matmul(
                            p[:, 0:128], lhsT=self.idty_bf[:], rhs=d2["Xc"][:],
                            start=False, stop=True), [xh, self.cr[5]],
                            sdeps=[mm2])
                        xh = P.emit("scalar", lambda e, p=pp, d2=dsc2: e.copy(
                            d2["Xc"][:], p[:, 0:128]), [mm2])
                        self.pf.release(pp, xh)
                        yield
                    sbf_readers = {t0: [], t1: []}
                    pw, pw_d = self.pf.acquire()
                    mmw = P.emit("tensor", lambda e, p=pw, d2=dsc2, kb3=kb_tm:
                                 e.matmul(p[:, 0:256], lhsT=d2["Xc"][:], rhs=kb3[:],
                                          start=True, stop=True), [kb2, xh, pw_d])
                    wn1 = P.emit("scalar", lambda e, p=pw, d2=dsc2: e.copy(
                        d2["w1"][:], p[:, 0:128]),
                        [mmw, tens_h.get(tkey(("w", 0)))])
                    wn2 = P.emit("scalar", lambda e, p=pw, d2=dsc2: e.copy(
                        d2["w2"][:], p[:, 128:256]),
                        [mmw, wn1, tens_h.get(tkey(("w", 1)))])
                    self.pf.release(pw, [wn1, wn2])
                    yield
                    wts = []
                    for i2 in range(2):
                        ptb3, pdb3 = self.pb.acquire()
                        trw = P.emit("tensor", lambda e, p=ptb3, i3=i2, d2=dsc2:
                                     e.transpose(p[:, 0:128],
                                                 in_=d2["w%d" % (i3 + 1)][:],
                                                 identity=self.idty_bf[:]),
                                     [wn1 if i2 == 0 else wn2, pdb3, self.cr[5]])
                        wtv = P.emit("scalar", lambda e, p=ptb3, i3=i2, d2=dsc2:
                                     e.copy(d2["w%d" % (i3 + 1)][:], p[:, 0:128]),
                                     [trw])
                        self.pb.release(ptb3, wtv)
                        wts.append(wtv)
                    yield
                    pws, pws_d = self.pf.acquire()
                    mws = None
                    for i2 in range(2):
                        mws = P.emit("tensor", lambda e, p=pws, i3=i2, d2=dsc2,
                                     sb3=Sbf[2 * h + i2]: e.matmul(
                            p[:, 0:256], lhsT=d2["w%d" % (i3 + 1)][:],
                            rhs=sb3[:],
                            start=(i3 == 0), stop=(i3 == 1)),
                            [wts[i2], sbf_h[2 * h + i2]] +
                            ([pws_d] if i2 == 0 else []),
                            sdeps=[mws] if mws else [])
                        sbf_readers[2 * h + i2].append(mws)
                    ws1 = P.emit("scalar", lambda e, p=pws, d2=dsc2: e.copy(
                        d2["ws1"][:], p[:, 0:128]),
                        [mws, tens_h.get(tkey(("ws", 0)))])
                    ws2 = P.emit("scalar", lambda e, p=pws, d2=dsc2: e.copy(
                        d2["ws2"][:], p[:, 128:256]),
                        [mws, ws1, tens_h.get(tkey(("ws", 1)))])
                    self.pf.release(pws, [ws1, ws2])
                    yield
                    pu, pu_d = self.pf.acquire()
                    mmu = P.emit("tensor", lambda e, p=pu, d2=dsc2, vt2=v_tm:
                                 e.matmul(p[:, 0:256], lhsT=d2["Xc"][:], rhs=vt2[:],
                                          start=True, stop=True), [xh, vb, pu_d])
                    uh1 = P.emit("vector", lambda e, p=pu, d2=dsc2: e.tensor_sub(
                        d2["u1"][:], p[:, 0:128], d2["ws1"][:]),
                        [mmu, ws1, tens_h.get(tkey("u1u"))])
                    uh2 = P.emit("vector", lambda e, p=pu, d2=dsc2: e.tensor_sub(
                        d2["u2"][:], p[:, 128:256], d2["ws2"][:]),
                        [mmu, uh1, ws2, tens_h.get(tkey("u2u"))])
                    self.pf.release(pu, [uh1, uh2])
                    yield
                    pat, pat_d = self.pf.acquire()
                    mm = None
                    for i2, ti in enumerate((t0, t1)):
                        mm = P.emit("tensor",
                                    lambda e, p=pat, kk=kn[ti], q2=qn[ti], c2=lcol,
                                    i3=i2:
                                    e.matmul(p[:, 0:128], lhsT=kk[:, c2],
                                             rhs=q2[:, c2], start=(i3 == 0),
                                             stop=(i3 == 1)),
                                    [tens_h[("sil", "k", ti)],
                                     tens_h[("sil", "q", ti)]]
                                    + ([pat_d] if i2 == 0 else []),
                                    sdeps=[mm] if mm else [])
                    ath = P.emit("vector", lambda e, p=pat, d2=dsc2: e.tensor_tensor(
                        out=d2["attn"][:], in0=p[:, 0:128], in1=m_up_incl,
                        op=ALU.mult), [mm, dm_h, tens_h.get(tkey("attnu"))])
                    self.pf.release(pat, ath)
                    yield
                    mo = None
                    for i2, ti in enumerate((t0, t1)):
                        po2, po2_d = self.pf.acquire()
                        mo = None
                        for j3, tj in enumerate((t0, t1)):
                            mo = P.emit("tensor",
                                        lambda e, p=po2, tj2=tj, i3=i2, c2=lcol:
                                        e.matmul(p[:, 0:128],
                                                 lhsT=Sbf[tj2][:, 128 * i3:
                                                               128 * (i3 + 1)],
                                                 rhs=qn[tj2][:, c2],
                                                 start=(j4 == 0), stop=False)
                                        if False else
                                        e.matmul(p[:, 0:128],
                                                 lhsT=Sbf[tj2][:, 128 * i3:
                                                               128 * (i3 + 1)],
                                                 rhs=qn[tj2][:, c2],
                                                 start=(tj2 == t0), stop=False),
                                        [sbf_h[tj], tens_h[("sil", "q", tj)]]
                                        + ([po2_d] if j3 == 0 else []),
                                        sdeps=[mo] if mo else [])
                        sbf_readers[t0].append(mo)
                        sbf_readers[t1].append(mo)
                        mo = P.emit("tensor",
                                    lambda e, p=po2, i3=i2, d2=dsc2:
                                    e.matmul(p[:, 0:128],
                                             lhsT=d2["u1" if i3 == 0 else "u2"][:],
                                             rhs=d2["attn"][:],
                                             start=False, stop=True),
                                    [uh1 if i2 == 0 else uh2, ath],
                                    sdeps=[mo])
                        me = P.emit("vector", lambda e, p=po2, d2=macc[ti], c2=lcol:
                                    e.tensor_copy(d2[:, c2], p[:, 0:128]),
                                    [mo, tens_h.get(("maccu", ti))])
                        tens_h[("maccu", ti)] = me
                        self.pf.release(po2, me)
                    yield
                    m2b = None
                    for i2, ti in enumerate((t0, t1)):
                        ps2, ps2_d = self.pf.acquire()
                        m1b = P.emit("tensor", lambda e, p=ps2, i3=i2, d2=dsc2,
                                     kt2=k_tm: e.matmul(
                            p[:, 0:128], lhsT=kt2[:, 128 * i3:128 * (i3 + 1)],
                            rhs=d2["u1"][:], start=True, stop=True),
                            [uh1, tms, ps2_d])
                        m2b = P.emit("tensor", lambda e, p=ps2, i3=i2, d2=dsc2,
                                     kt2=k_tm: e.matmul(
                            p[:, 128:256], lhsT=kt2[:, 128 * i3:128 * (i3 + 1)],
                            rhs=d2["u2"][:], start=True, stop=True),
                            [uh2, tms], sdeps=[m1b])
                        su = P.emit("vector", lambda e, p=ps2, s2=Sf[ti]:
                                    e.tensor_add(s2[:], s2[:], p[:, 0:256]),
                                    [m2b] + sf_h[ti])
                        sb2 = P.emit("vector", lambda e, s2=Sf[ti], d2=Sbf[ti]:
                                     e.tensor_copy(d2[:], s2[:]),
                                     [su] + sbf_h[ti] + sbf_readers[ti])
                        self.pf.release(ps2, su)
                        sf_h[ti] = [su]
                        sbf_h[ti] = [sb2]
                    late = [m2b, uh2, sbf_h[t1][0]]
                    for key2 in ("vtmu", "kbu", "Ntu", "Mtu", "attnu",
                                 "u1u", "u2u", "Xcu"):
                        tens_h[tkey(key2)] = late
                    for dstn in ("N2", "M2", "N4", "M4", "N8", "M8", "N16",
                                 "M16", "N32", "M32", "N64"):
                        tens_h[tkey((dstn, "u"))] = late
                    for i2 in range(2):
                        tens_h[tkey(("w", i2))] = late
                        tens_h[tkey(("ws", i2))] = late
                    for sname in ("v", "k"):
                        for i2 in range(2):
                            tens_h[tkey(("tmu", sname, i2))] = late

                gens = [delta_head(hh2, hh2) for hh2 in range(4)]
                alive = list(gens)
                while alive:
                    for g in list(alive):
                        try:
                            next(g)
                        except StopIteration:
                            alive.remove(g)

            if PRB == "delta":
                probe(macc, cs, lambda i: tens_h[("maccu", i)])
                bars = P.barrier()
                self.last_x = [bars["vector"]] * 8
                continue
            if PRB == "u2":
                bars = P.barrier()
                self.last_x = [bars["vector"]] * 8
                continue
            fl_use = {j: None for j in range(3)}
            for h in range(H):
                t0, t1 = 2 * h, 2 * h + 1
                pt_, pd_ = self.pf.acquire()
                mmb = None
                for ch in range(4):
                    mmb = P.emit("tensor", lambda e, p=pt_, c2=ch, hh=h:
                                 e.matmul(p[:, 128 * c2:128 * (c2 + 1)],
                                          lhsT=selg[:, hh, :], rhs=gtr4[:, c2, :],
                                          start=True, stop=True),
                                 [gds[ch], selg_h, pd_]
                                 + ([mmb] if mmb else []))
                bc0 = P.emit("vector", lambda e, p=pt_: e.tensor_copy(
                    gbc[0][:], p[:, 0:SL]), [mmb, tens_h.get("gbc0u")])
                bc1 = P.emit("vector", lambda e, p=pt_: e.tensor_scalar(
                    out=gbc[1][:], in0=p[:, 0:SL], scalar1=-1.0, scalar2=1.0,
                    op0=ALU.mult, op1=ALU.add), [mmb, tens_h.get("gbc1u")])
                self.pf.release(pt_, [bc0, bc1])
                pbs = []
                for j in range(3):
                    ptj, pdj = self.pf.acquire()
                    mmj = None
                    for ch in range(4):
                        mmj = P.emit("tensor", lambda e, p=ptj, c2=ch, jj=j, hh=h:
                                     e.matmul(p[:, 128 * c2:128 * (c2 + 1)],
                                              lhsT=selg[:, 4 + 3 * hh + jj, :],
                                              rhs=gtr4[:, c2, :],
                                              start=True, stop=True),
                                     [gds[ch], selg_h, pdj]
                                     + ([mmj] if mmj else []))
                    bj = P.emit("scalar", lambda e, p=ptj, jj=j: e.copy(
                        gbp[jj][:], p[:, 0:SL]), [mmj, tens_h.get(("gbpu", j))])
                    self.pf.release(ptj, bj)
                    pbs.append(bj)
                for ti in (t0, t1):
                    bld = {}
                    for hb, (tb, ntap, eng2) in enumerate(
                            ((0, 11, "vector"), (11, 10, "gpsimd"),
                             (21, 10, "vector"))):
                        bld[hb] = P.emit(
                            eng2,
                            lambda e, d2=flh[hb], o=ti, t3=tb, n3=ntap:
                            e.tensor_tensor(
                                out=d2[:, 0:n3, :],
                                in0=self.idty_bf.ap().unsqueeze(1)
                                .broadcast_to([128, n3, 128]),
                                in1=flw[:, o, t3:t3 + n3].unsqueeze(2)
                                .broadcast_to([128, n3, 128]),
                                op=ALU.mult),
                            [flw_h, self.cr[5], fl_use[hb]])
                    ptl, ptl_d = self.pf.acquire()
                    mm = None
                    for tap in range(31):
                        hb = 0 if tap < 11 else (1 if tap < 21 else 2)
                        off = (0, 11, 21)[hb]
                        mm = P.emit("tensor",
                                    lambda e, p=ptl, d2=flh[hb],
                                    t2=tap, o2=off, s2=vsil[ti]:
                                    e.matmul(p[:, 0:SL], lhsT=d2[:, t2 - o2, :],
                                             rhs=s2[:, t2:t2 + SL],
                                             start=(t2 == 0), stop=(t2 == 30)),
                                    [bld[hb], tens_h[("vsave", ti)]]
                                    + ([ptl_d] if tap == 0 else [mm]))
                        fl_use[hb] = mm
                    pts, pts_d = self.pf.acquire()
                    mm2 = None
                    for tap in range(3):
                        dt2 = dtile[tap % 2]
                        bh2 = P.emit("vector",
                                     lambda e, d2=dt2, t2=tap, o=ti:
                                     e.tensor_scalar(out=d2[:], in0=self.idty_bf[:],
                                                     scalar1=fsc[:, o, t2:t2 + 1],
                                                     scalar2=None, op0=ALU.mult),
                                     [fs_h, self.cr[5], dt_use[tap % 2]])
                        mm2 = P.emit("tensor",
                                     lambda e, p=pts, d2=dt2, t2=tap, s2=vsil[ti]:
                                     e.matmul(p[:, 0:SL], lhsT=d2[:],
                                              rhs=s2[:, 28 + t2:28 + t2 + SL],
                                              start=(t2 == 0), stop=(t2 == 2)),
                                     [bh2, tens_h[("vsave", ti)]]
                                     + ([pts_d] if tap == 0 else [mm2]))
                        dt_use[tap % 2] = mm2
                    e1 = P.emit("vector", lambda e, d2=macc[ti]: e.tensor_mul(
                        d2[:], d2[:], gbc[0][:]),
                        [bc0, tens_h.get(("maccu", ti))])
                    e2 = P.emit("vector", lambda e, p=ptl: e.tensor_mul(
                        mtmp[:], p[:, 0:SL], gbp[2][:]),
                        [mm, pbs[2], tens_h.get("mtmpu")])
                    self.pf.release(ptl, e2)
                    e3 = P.emit("vector", lambda e, p=pts: e.tensor_mul(
                        mtmp2[:], p[:, 0:SL], gbp[1][:]),
                        [mm2, pbs[1], tens_h.get("mtmp2u")])
                    self.pf.release(pts, e3)
                    e4 = P.emit("vector", lambda e: e.tensor_add(
                        mtmp[:], mtmp[:], mtmp2[:]), [e2, e3])
                    e5 = P.emit("vector", lambda e, d2=vsil[ti]: e.tensor_mul(
                        mtmp2[:], d2[:, 30:30 + SL], gbp[0][:]),
                        [pbs[0], tens_h[("vsave", ti)], e4])
                    e6 = P.emit("vector", lambda e: e.tensor_add(
                        mtmp[:], mtmp[:], mtmp2[:]), [e4, e5])
                    e7 = P.emit("vector", lambda e: e.tensor_mul(
                        mtmp[:], mtmp[:], gbc[1][:]), [e6, bc1])
                    e8 = P.emit("vector", lambda e, d2=macc[ti]: e.tensor_add(
                        d2[:], d2[:], mtmp[:]), [e7, e1])
                    tens_h[("maccu", ti)] = e8
                    tens_h["mtmpu"] = e7
                    tens_h["mtmp2u"] = e6
                    tens_h[("gbpu", 0)] = e5
                    tens_h[("gbpu", 1)] = e3
                    tens_h[("gbpu", 2)] = e2
                    tens_h["gbc0u"] = e1
                    tens_h["gbc1u"] = e7
                pt, pd = self.pf.acquire()
                mm = None
                for i2, ti in enumerate((t0, t1)):
                    sq2 = P.emit("scalar", lambda e, d2=macc[ti]:
                                 e.activation(out=sq_bf[:], in_=d2[:],
                                              func=AF.Square),
                                 [tens_h[("maccu", ti)], tens_h.get("sqd"), mm])
                    mm = P.emit("tensor", lambda e, p=pt, i3=i2:
                                e.matmul(p[0:1, 0:SL],
                                         lhsT=self.ones_col_bf[:, 0:1],
                                         rhs=sq_bf[:], start=(i3 == 0),
                                         stop=(i3 == 1)),
                                [sq2, self.cr[1]] + ([pd] if i2 == 0 else []))
                    tens_h["sqd"] = mm
                r1 = P.emit("scalar", lambda e, p=pt: e.activation(
                    out=l2row[:], in_=p[0:1, 0:SL], func=AF.Ln, scale=1.0 / DV,
                    bias=self.eps_col[0:1, 0:1]), [mm, self.cr[4],
                                                   tens_h.get("l2u")])
                r2 = P.emit("scalar", lambda e: e.activation(
                    out=l2rb[:], in_=l2row[:], func=AF.Exp, scale=-0.5), [r1])
                self.pf.release(pt, r1)
                rb = self.bcast_row(l2rb[0:1, :], SL, rbc[:],
                                    [r2, tens_h.get("rbcu")])
                us = []
                for ti in (t0, t1):
                    ua = P.emit("vector", lambda e, d2=macc[ti]: e.tensor_mul(
                        d2[:], d2[:], rbc[:]), [rb, tens_h[("maccu", ti)]] + us)
                    ub = P.emit("vector", lambda e, d2=macc[ti], o=osb[ti], t2=ti:
                                e.tensor_scalar(out=o[:], in0=d2[:],
                                                scalar1=rms_sb[:, t2:t2 + 1],
                                                scalar2=None, op0=ALU.mult),
                                [ua, rms_h, tens_h.get(("osbu", ti))])
                    tens_h[("maccu", ti)] = ub
                    tens_h[("osbu", ti)] = ub
                    us.append(ub)
                tens_h["l2u"] = us
                tens_h["rbcu"] = us

            if PRB == "mix":
                probe(macc, cs, lambda i: tens_h[("maccu", i)])
                bars = P.barrier()
                self.last_x = [bars["vector"]] * 8
                continue
            for half2 in range(4):
                wo_b, wo_hh, wo_bidx = stream_w(wo_d, 256 * half2)
                for oc in range(2 * half2, 2 * half2 + 2):
                    pt, pd = self.pf.acquire()
                    mm = None
                    for k in range(8):
                        mm = P.emit(
                            "tensor",
                            lambda e, p=pt, kk=k, o=oc - 2 * half2, w=wo_b:
                            e.matmul(p[:, 0:SL],
                                     lhsT=w[:, kk, 128 * o:128 * (o + 1)],
                                     rhs=osb[kk][:],
                                     start=(kk == 0), stop=(kk == 7)),
                            [wo_hh, tens_h[("osbu", k)]]
                            + ([pd] if k == 0 else [mm]))
                    ev = P.emit("vector", lambda e, p=pt, s3=x[oc], c=cs:
                                e.tensor_add(s3[:, c], s3[:, c], p[:, 0:SL]),
                                [mm, self.last_x[oc]])
                    self.last_x[oc] = ev
                    self.pf.release(pt, ev)
                    wslot["cons"][wo_bidx] = mm

            bars = P.barrier()
            self.last_x = [bars["vector"]] * 8
            for i in range(8):
                sf_h[i] = [bars["vector"]]
                sbf_h[i] = [bars["vector"]]
        sc.close()

def host_consts():
    idx = np.arange(128)
    mneg_low = -(idx[None, :] < idx[:, None]).astype(np.float32)
    m_up_incl = (idx[:, None] <= idx[None, :]).astype(np.float32)
    dmask = np.concatenate([mneg_low, m_up_incl], 0).astype(np.float32)
    idty = np.eye(128, dtype=np.float32)
    idx = np.arange(128)
    b8 = idx // 8
    m8d = ((b8[None, :] == b8[:, None]) & (idx[None, :] >= idx[:, None]))
    m8 = (b8[None, :] == b8[:, None] + 1) & (b8[None, :] % 2 == 1)
    b16 = idx // 16
    m16 = (b16[None, :] == b16[:, None] + 1) & (b16[None, :] % 2 == 1)
    gmask = np.concatenate([m8d, m8, m16], 0).astype(np.float32)
    selg = np.zeros((16, 16 * 128), np.float32)
    for r in range(16):
        selg[r, 128 * r:128 * (r + 1)] = 1.0
    return {"c_idty": idty, "c_gmask": gmask, "c_dmask": dmask, "c_selg": selg}


def build_program(blocks):
    b = Builder()
    if "l0a" in blocks:
        b.emit_l0_attn()
        if not os.environ.get("DEV_NOLN"):
            b.emit_postln("l0a")
    if "l0f" in blocks:
        b.emit_l0_ffn()
        b.emit_postln("l0f")
    if "l1a" in blocks:
        b.emit_l1_attn()
        if not os.environ.get("DEV_L1_PROBE"):
            b.emit_postln("l1a")
    if "l1f" in blocks:
        b.emit_l1_ffn()
        b.emit_postln("l1f")
    counts = b.finish()
    return b, counts


def _fl_diag(fl):
    flr = fl.reshape(D, 31)
    out = np.zeros((31, 8, 128, 128), np.float32)
    p = np.arange(128)
    for t in range(31):
        for o in range(8):
            out[t, o, p, p] = flr[o * 128 + p, t]
    return out.reshape(31 * 8 * 128, 128)


def host_inputs(inp, core):
    f32 = lambda k: np.ascontiguousarray(np.asarray(inp[k], np.float32))
    m = {
        "l0_wq": f32("L0_attn_Wq"), "l0_wf": f32("L0_attn_Wf"),
        "l0_wi": f32("L0_attn_Wi"), "l0_wo": f32("L0_attn_Wo"),
        "l0f_wg": f32("L0_ffn_Wg"), "l0f_we": f32("L0_ffn_We"),
        "l0f_be": f32("L0_ffn_be").reshape(1, D),
        "presel": (np.arange(8) < core).astype(np.float32).reshape(8, 1),
        "l1_wq": f32("L1_attn_Wq"), "l1_wk": f32("L1_attn_Wk"),
        "l1_wv": f32("L1_attn_Wv"), "l1_wo": f32("L1_attn_Wo"),
        "l1_wb": f32("L1_attn_Wb"), "l1_wg": f32("L1_attn_Wg"),
        "l1_wl": f32("L1_attn_Wl"),
        "l1_bg": f32("L1_attn_bg").reshape(1, 4),
        "l1_bl": f32("L1_attn_bl").reshape(1, 12),
        "l1_lt": np.repeat(np.asarray(inp["L1_attn_log_temp"], np.float32), 3)
        .reshape(1, 12).copy(),
        "l1_rms": np.ascontiguousarray(
            np.tile(np.asarray(inp["L1_attn_rms_w"], np.float32), 4)
            .reshape(8, 128).T),
        "l1_cq": f32("L1_attn_conv_q"), "l1_ck": f32("L1_attn_conv_k"),
        "l1_cv": f32("L1_attn_conv_v"),
        "l1_fs": np.asarray(inp["L1_attn_fir_short"], np.float32)
        .reshape(D, 3).copy(),
        "l1_flw": np.ascontiguousarray(
            np.asarray(inp["L1_attn_fir_long"], np.float32).reshape(8, 128, 31)
            .transpose(1, 0, 2).reshape(128, 8 * 31)),
        "l1f_w1": f32("L1_ffn_w1"),
        "l1f_b1": f32("L1_ffn_b1").reshape(1, 4096),
        "l1f_w2": f32("L1_ffn_w2"),
        "l1f_b2": f32("L1_ffn_b2").reshape(1, D),
    }
    return m


_BUILD = None


def kernel(**inputs):
    global _BUILD
    if _BUILD is None:
        _BUILD = build_program(("l0a", "l0f", "l1a", "l1f"))[0]
    b = _BUILD
    x = np.ascontiguousarray(np.asarray(inputs["L0_attn_hidden_states"], np.float32))
    consts = host_consts()
    in_maps = []
    for c in range(8):
        m = {"xT": np.ascontiguousarray(x[c].T)}
        m.update(consts)
        m.update(host_inputs(inputs, c))
        m = {k2: m[k2] for k2 in b.inputs}
        in_maps.append(m)
    res = run_bass_kernel_spmd(b.P.nc, in_maps, core_ids=list(range(8)))
    out = np.stack([np.asarray(res.results[c]["outT"]).T for c in range(8)])
    return np.ascontiguousarray(out.astype(np.float32))


# version 18
# speedup vs baseline: 1.0221x; 1.0202x over previous
import os
import numpy as np
import ml_dtypes
from contextlib import ExitStack
import concourse.bass as bass
import concourse.mybir as mybir
from concourse.bass_utils import run_bass_kernel_spmd

F32 = mybir.dt.float32
BF16 = mybir.dt.bfloat16
AF = mybir.ActivationFunctionType
ALU = mybir.AluOpType
AX = mybir.AxisListType

L = 2048
D = 1024
ENGINES = ("sync", "vector", "scalar", "tensor", "gpsimd")


class Planner:

    def __init__(self):
        self.nc = bass.Bass("TRN2", target_bir_lowering=False)
        self.ops = {e: [] for e in ENGINES}
        self.ctx = ExitStack()
        self._uid = 0
        self.barrier_tile = None
        self.dma_count = {e: 0 for e in ENGINES}

    def name(self, base):
        self._uid += 1
        return f"{base}_{self._uid}"

    def sb(self, base, shape, dtype=F32):
        return self.ctx.enter_context(self.nc.sbuf_tensor(self.name(base), shape, dtype))

    def scope(self):
        return ExitStack()

    def sbs(self, scope, base, shape, dtype=F32):
        return scope.enter_context(self.nc.sbuf_tensor(self.name(base), shape, dtype))

    def psum(self, base, shape, dtype=F32):
        return self.ctx.enter_context(self.nc.psum_tensor(self.name(base), shape, dtype))

    def emit(self, eng, fn, deps=(), dma=False, sdeps=()):
        ds = []
        stack = list(deps)
        while stack:
            d = stack.pop()
            if d is None:
                continue
            if isinstance(d, list):
                stack.extend(d)
            else:
                ds.append(d)
        sd = [d for d in sdeps if d is not None]
        h = (eng, len(self.ops[eng]))
        self.ops[eng].append({"fn": fn, "deps": ds, "dma": dma, "need": dma,
                              "sdeps": sd})
        return h

    def dma(self, eng, out, in_, deps=()):
        h = self.emit(eng, lambda e, o=out, i=in_: e.dma_start(out=o, in_=i),
                      deps, dma=True)
        self.dma_count[eng] += 16
        return h

    def barrier(self):
        tails = {e: (e, len(self.ops[e]) - 1) for e in ENGINES if self.ops[e]}
        snap = dict(self.dma_count)
        hs = {}
        for e in ENGINES:
            deps = [t for te, t in tails.items() if te != e]
            if e == "scalar" and e in tails:
                deps = deps + [tails[e]]
            if e in ("vector", "gpsimd"):
                hs[e] = self.emit(e, lambda en: en.engine_nop(), deps)
            elif e == "scalar" and self.barrier_tile is not None:
                hs[e] = self.emit(e, lambda en: en.copy(
                    self.barrier_tile[0:1, 0:1], self.barrier_tile[0:1, 0:1]), deps)
            else:
                hs[e] = self.emit(e, lambda en: en.nop(), deps)
            self.ops[e][hs[e][1]]["dsnap"] = snap
        return hs

    def finalize(self):
        nc = self.nc
        for e in ENGINES:
            for op in self.ops[e]:
                for (de, di) in op["deps"]:
                    self.ops[de][di]["need"] = True
                for (de, di) in op.get("sdeps", []):
                    self.ops[de][di]["need"] = True
        for e in ENGINES:
            cc = 0
            cd = 0
            for op in self.ops[e]:
                if op["need"]:
                    if op["dma"]:
                        op["inc"] = 16
                        cd += 16
                    else:
                        op["inc"] = 1
                        cc += 1
                else:
                    op["inc"] = 0
                op["cum"] = cd if op["dma"] else cc
        sems = {e: self.ctx.enter_context(self.nc.semaphore(f"sem_{e}"))
                for e in ENGINES}
        dsems = {e: self.ctx.enter_context(self.nc.semaphore(f"dsem_{e}"))
                 for e in ENGINES}
        block = self.ctx.enter_context(nc.Block())

        def runner(e):
            def run(eng):
                waited = {x: 0 for x in ENGINES}
                dwaited = {x: 0 for x in ENGINES}
                my_dma = 0
                for op in self.ops[e]:
                    if op["dma"] and my_dma > 0:
                        if my_dma > dwaited[e]:
                            eng.wait_ge(dsems[e], my_dma)
                            dwaited[e] = my_dma
                    for de2, v2 in op.get("dsnap", {}).items():
                        if v2 > dwaited[de2]:
                            eng.wait_ge(dsems[de2], v2)
                            dwaited[de2] = v2
                    for (de, di) in list(op["deps"]) + list(op.get("sdeps", [])):
                        p = self.ops[de][di]
                        if (de == e == "tensor" and not p["dma"]
                                and (de, di) not in op.get("sdeps", [])):
                            continue
                        v = p["cum"]
                        if p["dma"]:
                            if v > dwaited[de]:
                                eng.wait_ge(dsems[de], v)
                                dwaited[de] = v
                        else:
                            if v > waited[de]:
                                eng.wait_ge(sems[de], v)
                                waited[de] = v
                    inst = op["fn"](eng)
                    if op["inc"]:
                        inst.then_inc(dsems[e] if op["dma"] else sems[e], op["inc"])
                    if op["dma"]:
                        my_dma += 16
            return run

        block.sync(runner("sync"))
        block.vector(runner("vector"))
        block.scalar(runner("scalar"))
        block.tensor(runner("tensor"))
        block.gpsimd(runner("gpsimd"))
        return {e: len(self.ops[e]) for e in ENGINES}


class PsumPool:
    def __init__(self, tensors):
        self.free = [[t, []] for t in tensors]

    def acquire(self):
        t, dep = self.free.pop(0)
        return t, dep

    def release(self, t, deps):
        if not isinstance(deps, list):
            deps = [deps]
        self.free.append([t, deps])


class Builder:
    def __init__(self):
        self.P = Planner()
        nc = self.P.nc
        P = self.P
        self.inputs = {}

        self.xT_d = self.din("xT", [D, L])
        self.out_d = nc.dram_tensor("outT", [D, L], F32, kind="ExternalOutput").ap()

        self.pf = PsumPool([P.psum("pf", [128, 512]) for _ in range(5)])
        self.pb = PsumPool([P.psum("pb", [128, 1024], BF16) for _ in range(2)])
        self.pr = PsumPool([P.psum("pr", [1, 512])])

        self.ones_col = P.sb("ones_col", [128, 1])
        self.ones_col_bf = P.sb("ones_col_bf", [128, 1], BF16)
        self.ones_row = P.sb("ones_row", [1, 128])
        self.ones_row_bf = P.sb("ones_row_bf", [1, 128], BF16)
        self.ones_chunk = P.sb("ones_chunk", [128, 128])
        self.eps_col = P.sb("eps_col", [1, 1])
        P.barrier_tile = self.eps_col
        cr = []
        cr.append(P.emit("gpsimd", lambda e: e.memset(self.ones_col[:], 1.0)))
        cr.append(P.emit("gpsimd", lambda e: e.memset(self.ones_col_bf[:], 1.0)))
        cr.append(P.emit("gpsimd", lambda e: e.memset(self.ones_row[:], 1.0)))
        cr.append(P.emit("gpsimd",
                         lambda e: e.memset(self.ones_row_bf[:], 1.0)))
        cr.append(P.emit("gpsimd", lambda e: e.memset(self.ones_chunk[:], 1.0)))
        cr.append(P.emit("gpsimd", lambda e: e.memset(self.eps_col[:], 1e-5)))
        self.idty_d = self.din("c_idty", [128, 128])
        self.idty_bf = P.sb("idty_bf", [128, 128], BF16)
        cr.append(P.dma("gpsimd", self.idty_bf[:], self.idty_d[:]))
        self.cr = cr

        self.x = [P.sb("x", [128, L]) for _ in range(8)]
        self.last_x = [P.dma("sync", self.x[i][:], self.xT_d[128 * i:128 * (i + 1), :])
                       for i in range(8)]

    def din(self, name2, shape):
        ap = self.P.nc.dram_tensor(name2, shape, F32, kind="ExternalInput").ap()
        self.inputs[name2] = shape
        return ap

    def load_w_bf(self, dram_ap, rows, cols, deps=(), scope=None):
        if scope is None:
            t = self.P.sb("w", [128, rows // 128, cols], BF16)
        else:
            t = self.P.sbs(scope, "w", [128, rows // 128, cols], BF16)
        h = self.P.dma("gpsimd", t.ap(),
                       dram_ap.rearrange("(c p) n -> p c n", p=128), deps)
        return t, h

    def finish(self):
        self.P.barrier()
        sts = [self.P.dma("sync", self.out_d[128 * i:128 * (i + 1), :], self.x[i][:])
               for i in range(8)]
        self.P.emit("sync", lambda e: e.nop(), sts)
        return self.P.finalize()

    def cast_bf(self, dst_tiles, cs, deps_per_tile, tag=""):
        hs = []
        for i in range(8):
            eng = "vector" if i % 2 else "scalar"
            if eng == "vector":
                fn = (lambda e, o=dst_tiles[i], s=self.x[i], c=cs:
                      e.tensor_copy(o[:], s[:, c] if c is not None else s[:]))
            else:
                fn = (lambda e, o=dst_tiles[i], s=self.x[i], c=cs:
                      e.copy(o[:], s[:, c] if c is not None else s[:]))
            hs.append(self.P.emit(eng, fn, [deps_per_tile[i]]))
        return hs

    def proj(self, weight, w_dep, rhs_tiles, rhs_dep, cs, ncols, out_cb):
        hs = []
        for oc in range(8):
            pt, pd = self.pf.acquire()
            mm = None
            for k in range(8):
                mm = self.P.emit(
                    "tensor",
                    lambda e, p=pt, w=weight, kk=k, o=oc, r=rhs_tiles[k], c=cs, n=ncols:
                    e.matmul(p[:, 0:n], lhsT=w[:, kk, 128 * o:128 * (o + 1)],
                             rhs=(r[:, c] if c is not None else r[:]),
                             start=(kk == 0), stop=(kk == 7)),
                    [w_dep, rhs_dep[k]] + ([pd] if k == 0 else [mm]))
            ev = out_cb(oc, pt, mm)
            self.pf.release(pt, ev)
            hs.append(ev if not isinstance(ev, list) else ev[-1])
        return hs

    def bcast_row(self, row_ap, n, dst_ap, deps, f32=False):
        pt, pd = self.pf.acquire()
        ones = self.ones_row if f32 else self.ones_row_bf
        mm = self.P.emit("tensor", lambda e, p=pt, o1=ones: e.matmul(
            p[:, 0:n], lhsT=o1[0:1, :], rhs=row_ap,
            start=True, stop=True), deps + [pd, self.cr[2], self.cr[-1]])
        ev = self.P.emit("vector", lambda e, p=pt: e.tensor_copy(dst_ap, p[:, 0:n]),
                         [mm])
        self.pf.release(pt, ev)
        return ev

    def emit_l0_attn(self):
        P = self.P
        x = self.x
        C = 128
        NS = int(os.environ.get("DEV_NS", L // C))
        STAGE = int(os.environ.get("DEV_STAGE", 99))
        H, DK, DV = 16, 64, 64
        SCALE = DK ** -0.5
        LV = ("qs8s", "qs8e", "qs16", "qs32", "qs64", "ks8e", "ks16", "ks32", "ks64")
        sc = P.scope()

        wq, wq_h = self.load_w_bf(self.din("l0_wq", [D, D]), D, D, scope=sc)
        wf, wf_h = self.load_w_bf(self.din("l0_wf", [D, D]), D, D, scope=sc)
        wi, wi_h = self.load_w_bf(self.din("l0_wi", [D, D]), D, D, scope=sc)
        wo, wo_h = self.load_w_bf(self.din("l0_wo", [D, D]), D, D, scope=sc)
        gm_d = self.din("c_gmask", [3 * 128, 128])
        masks = P.sbs(sc, "gmask", [128, 3, 128])
        mask_h = P.dma("sync", masks.ap(), gm_d.rearrange("(m p) n -> p m n", p=128))
        m8d, m8o, m16 = masks[:, 0, :], masks[:, 1, :], masks[:, 2, :]

        Ft = [P.sbs(sc, "l0_F", [128, C]) for _ in range(8)]
        Bt = [P.sbs(sc, "l0_B", [128, 132]) for _ in range(8)]
        Gt = [P.sbs(sc, "l0_G", [128, C]) for _ in range(8)]
        Kt = [P.sbs(sc, "l0_LNK", [128, C]) for _ in range(8)]
        eb = [P.sbs(sc, "l0_eb", [128, C], BF16) for _ in range(8)]
        xbf = [P.sbs(sc, "l0_xbf", [128, C], BF16) for _ in range(8)]
        vtm = P.sbs(sc, "l0_vtm", [128, D], BF16)
        ksw = [P.sbs(sc, "l0_ksw", [128, 5, 128], BF16) for _ in range(8)]
        qsw = [P.sbs(sc, "l0_qsw", [128, 6, 128], BF16) for _ in range(8)]
        wtmp = [P.sbs(sc, "l0_wtmp", [128, 6, 128]) for _ in range(3)]
        wtmp_h = [None] * 3

        class _Lane:
            def __init__(self, t, lane):
                self.t, self.lane = t, lane

            def __getitem__(self, idx):
                if isinstance(idx, tuple):
                    rw, cc = idx
                else:
                    rw, cc = slice(None), slice(None)
                return self.t[rw, self.lane, cc]
        KLANE = {"ks8e": 0, "ks16": 1, "ks32": 2, "ks64": 3}
        QLANE = {"qs8s": 0, "qs8e": 1, "qs16": 2, "qs32": 3, "qs64": 4}
        lv = {n: [_Lane(ksw[i], KLANE[n]) for i in range(8)] for n in KLANE}
        lv.update({n: [_Lane(qsw[i], QLANE[n]) for i in range(8)] for n in QLANE})
        Zq = [_Lane(qsw[i], 5) for i in range(8)]
        KH = [_Lane(ksw[i], 4) for i in range(8)]
        KHtm = P.sbs(sc, "l0_khtm", [128, D], BF16)
        Sf = [P.sbs(sc, "l0_S", [128, DV]) for _ in range(8)]
        Sbf = [P.sbs(sc, "l0_Sbf", [128, DV], BF16) for _ in range(8)]
        wcol = [P.sbs(sc, "l0_wcol", [128, 1]) for _ in range(8)]
        osb = [P.sbs(sc, "l0_o", [128, C], BF16) for _ in range(8)]
        a_sb = {n: P.sbs(sc, f"l0_{n}", [128, 128], BF16)
                for n in ("a8d", "a8o", "a16", "a64", "a32")}
        rqb = P.sbs(sc, "l0_rqb", [128, C])
        rowf = P.sbs(sc, "l0_rowf", [1, C])
        rowq = P.sbs(sc, "l0_rowq", [1, C])
        rowqb = P.sbs(sc, "l0_rowqb", [1, C], BF16)

        s_init = []
        b_zero = []
        for i in range(8):
            s_init.append(P.emit("gpsimd", lambda e, s=Sf[i]: e.memset(s[:], 0.0)))
            s_init.append(P.emit("gpsimd", lambda e, s=Sbf[i]: e.memset(s[:], 0.0)))
            b_zero.append(P.emit("gpsimd", lambda e, b=Bt[i]:
                                 e.memset(b[:, 0:1], 0.0)))
        s_init = s_init + b_zero
        sbf_h = {i: s_init for i in range(8)}
        sf_h = {i: s_init for i in range(8)}
        entry_x = list(self.last_x)
        war = {}
        ar_last = {n: [mask_h] for n in a_sb}

        for sl in range(NS):
            cs = slice(C * sl, C * (sl + 1))
            xbf_h = self.cast_bf(
                xbf, cs,
                [[entry_x[i]] + war.get("xbf", []) for i in range(8)])
            if STAGE < 1:
                bars = self.P.barrier()
                self.last_x = [bars["vector"]] * 8
                continue

            F_h = [None] * 8
            ef_h = [None] * 8

            def f_evac(oc, pt, mm):
                h1 = P.emit("vector", lambda e, p=pt, o=Ft[oc]:
                            e.tensor_copy(o[:], p[:, 0:C]),
                            [mm, war.get(("ft", oc))])
                h2 = P.emit("scalar", lambda e, p=pt, o=eb[oc]:
                            e.activation(out=o[:], in_=p[:, 0:C], func=AF.Exp),
                            [mm, h1, war.get("eb")])
                F_h[oc] = h1
                ef_h[oc] = h2
                return [h1, h2]
            self.proj(wf, wf_h, xbf, xbf_h, None, C, f_evac)

            if STAGE < 12:
                bars = self.P.barrier()
                self.last_x = [bars["vector"]] * 8
                continue
            pt, pd = self.pr.acquire()
            mm = None
            for k in range(8):
                mm = P.emit("tensor", lambda e, p=pt, r=eb[k], kk=k:
                            e.matmul(p[0:1, 0:C], lhsT=self.ones_col_bf[:, 0:1],
                                     rhs=r[:], start=(kk == 0), stop=(kk == 7)),
                            [ef_h[k], self.cr[1]] + ([pd] if k == 0 else [mm]))
            sumf_mm = mm
            lsf_h = P.emit("scalar", lambda e, p=pt: e.activation(
                out=rowf[:], in_=p[0:1, 0:C], func=AF.Ln),
                [mm, war.get("rowf")])
            self.pr.release(pt, lsf_h)
            pt2, pd2 = self.pf.acquire()
            mmb = P.emit("tensor", lambda e, p=pt2: e.matmul(
                p[:, 0:C], lhsT=self.ones_row[0:1, :], rhs=rowf[0:1, :],
                start=True, stop=True), [lsf_h, pd2, self.cr[2]])
            g_h = [P.emit("vector", lambda e, p=pt2, f=Ft[i]:
                          e.tensor_sub(f[:], f[:], p[:, 0:C]), [mmb, F_h[i]])
                   for i in range(8)]
            self.pf.release(pt2, g_h)

            if STAGE < 13:
                bars = self.P.barrier()
                self.last_x = [bars["vector"]] * 8
                continue
            B_h = {}
            for i in range(8):
                B_h[i] = P.emit(
                    "vector", lambda e, f=Ft[i], b=Bt[i]:
                    e.tensor_tensor_scan(
                        out=b[:, 1:129], data0=self.ones_chunk[:, 0:128],
                        data1=f[:], initial=0.0, op0=ALU.mult, op1=ALU.add),
                    [g_h[i], self.cr[3], war.get(("bt", i))])
            H_h = [None] * 8
            ft_read = [None] * 8
            for i in range(8):
                h1 = P.emit("scalar", lambda e, f=Ft[i], o=eb[i]:
                            e.activation(out=o[:], in_=f[:], func=AF.Exp),
                            [g_h[i], sumf_mm])
                ft_read[i] = h1
                h2 = P.emit("scalar", lambda e, s=eb[i], o=Kt[i]:
                            e.activation(out=o[:], in_=s[:], func=AF.Ln,
                                         bias=self.ones_col[:, 0:1], scale=-1.0),
                            [h1, self.cr[0], war.get(("kt", i))])
                H_h[i] = P.emit("vector", lambda e, kk=Kt[i], b=Bt[i]:
                                e.tensor_sub(kk[:], kk[:], b[:, 1:129]),
                                [h2, B_h[i]])

            ksw_h = [None] * 8
            kh1s = [None] * 8
            for i in range(8):
                slot = i % 3
                t = wtmp[slot]
                h1s = []
                for lane, ssz in enumerate((8, 16, 32, 64)):
                    nblk = 128 // ssz
                    bb = Bt[i][:, ssz::ssz][:, 0:nblk].unsqueeze(2) \
                        .broadcast_to([128, nblk, ssz])
                    eng2 = "vector" if (i + lane) % 2 == 0 else "gpsimd"
                    h1s.append(P.emit(
                        eng2,
                        lambda e, t2=t, a=Kt[i], b=bb, ln2=lane, ss2=ssz:
                        e.tensor_tensor(
                            out=t2[:, ln2, :].rearrange("p (b s) -> p b s", s=ss2),
                            in0=a.ap().rearrange("p (b s) -> p b s", s=ss2),
                            in1=b, op=ALU.add),
                        [H_h[i], B_h[i], b_zero[i], wtmp_h[slot]]))
                h1s.append(P.emit(
                    "vector" if i % 2 else "gpsimd",
                    lambda e, t2=t, k2=Kt[i], b=Bt[i]:
                    e.tensor_scalar(out=t2[:, 4, :], in0=k2[:],
                                    scalar1=b[:, 128:129], scalar2=None,
                                    op0=ALU.add),
                    [H_h[i], B_h[i], wtmp_h[slot]]))
                ksw_h[i] = P.emit("scalar", lambda e, t2=t, o=ksw[i]:
                                  e.activation(out=o[:, 0:5, :], in_=t2[:, 0:5, :],
                                               func=AF.Exp),
                                  h1s + [war.get(("lvk", i))])
                wtmp_h[slot] = ksw_h[i]
                kh1s[i] = list(h1s)
            kh_h = ksw_h

            if STAGE < 14:
                bars = self.P.barrier()
                self.last_x = [bars["vector"]] * 8
                continue
            G_h = [None] * 8
            eq_h = [None] * 8

            def q_evac(oc, pt_, mmh):
                h1 = P.emit("vector", lambda e, p=pt_, o=Gt[oc], b=Bt[oc]:
                            e.tensor_add(o[:], p[:, 0:C], b[:, 1:129]),
                            [mmh, B_h[oc], war.get(("gt", oc))])
                h2 = P.emit("scalar", lambda e, p=pt_, o=eb[oc]:
                            e.activation(out=o[:], in_=p[:, 0:C], func=AF.Exp),
                            [mmh, H_h[oc], h1])
                G_h[oc] = h1
                eq_h[oc] = h2
                return [h1, h2]
            self.proj(wq, wq_h, xbf, xbf_h, None, C, q_evac)

            qsw_h = [None] * 8
            qh1s = [None] * 8
            for i in range(8):
                slot = i % 3
                t = wtmp[slot]
                h1s = []
                for lane, (ssz, off) in enumerate(
                        ((8, 0), (8, 8), (16, 0), (32, 0), (64, 0))):
                    nblk = 128 // ssz
                    bb = Bt[i][:, off::ssz][:, 0:nblk].unsqueeze(2) \
                        .broadcast_to([128, nblk, ssz])
                    eng2 = "vector" if (i + lane) % 2 == 0 else "gpsimd"
                    h1s.append(P.emit(
                        eng2,
                        lambda e, t2=t, a=Gt[i], b=bb, ln2=lane, ss2=ssz:
                        e.tensor_tensor(
                            out=t2[:, ln2, :].rearrange("p (b s) -> p b s", s=ss2),
                            in0=a.ap().rearrange("p (b s) -> p b s", s=ss2),
                            in1=b, op=ALU.subtract),
                        [G_h[i], B_h[i], b_zero[i], wtmp_h[slot]]))
                h1s.append(P.emit("vector" if i % 2 else "gpsimd",
                                  lambda e, t2=t, g2=Gt[i]:
                                  e.tensor_copy(t2[:, 5, :], g2[:]),
                                  [G_h[i], wtmp_h[slot]]))
                qsw_h[i] = P.emit("scalar", lambda e, t2=t, o=qsw[i]:
                                  e.activation(out=o[:], in_=t2[:], func=AF.Exp),
                                  h1s + [war.get(("lvq", i))])
                wtmp_h[slot] = qsw_h[i]
                qh1s[i] = list(h1s)
            lv_h = {}
            for i in range(8):
                for n in ("ks8e", "ks16", "ks32", "ks64"):
                    lv_h[(i, n)] = ksw_h[i]
                for n in ("qs8s", "qs8e", "qs16", "qs32", "qs64"):
                    lv_h[(i, n)] = qsw_h[i]

            pt, pd = self.pr.acquire()
            mmq = None
            for k in range(8):
                mmq = P.emit("tensor", lambda e, p=pt, r=eb[k], kk=k:
                             e.matmul(p[0:1, 0:C], lhsT=self.ones_col_bf[:, 0:1],
                                      rhs=r[:], start=(kk == 0), stop=(kk == 7)),
                             [eq_h[k], self.cr[1]] + ([pd] if k == 0 else [mmq]))
            rq1 = P.emit("vector", lambda e, p=pt:
                         e.reciprocal(rowq[0:1, :], p[0:1, 0:C]),
                         [mmq, war.get("rowq")])
            self.pr.release(pt, rq1)
            rq2 = P.emit("scalar", lambda e: e.mul(rowqb[:], rowq[:], SCALE),
                         [rq1, war.get("rowqb")])

            if STAGE < 15:
                bars = self.P.barrier()
                self.last_x = [bars["vector"]] * 8
                continue
            vtm_h = []
            for hf in range(2):
                pt, pd = self.pf.acquire()
                mm = None
                for k in range(8):
                    mm = P.emit(
                        "tensor",
                        lambda e, p=pt, kk=k, h=hf, c=cs:
                        e.matmul(p[:, 0:512], lhsT=xbf[kk][:],
                                 rhs=wi[:, kk, 512 * h:512 * (h + 1)],
                                 start=(kk == 0), stop=(kk == 7)),
                        [wi_h, xbf_h[k]] + ([pd] if k == 0 else [mm]))
                ev = P.emit("vector", lambda e, p=pt, h=hf:
                            e.tensor_copy(vtm[:, 512 * h:512 * (h + 1)],
                                          p[:, 0:512]), [mm, war.get("vtm")])
                self.pf.release(pt, ev)
                vtm_h.append(ev)

            if STAGE < 20:
                bars = self.P.barrier()
                self.last_x = [bars["vector"]] * 8
                continue
            ktm_h = []
            for i in range(8):
                ptb, pdb = self.pb.acquire()
                t1 = P.emit("tensor", lambda e, p=ptb, s=KH[i]:
                            e.transpose(p[:, 0:128], in_=s[:],
                                        identity=self.idty_bf[:]),
                            [kh_h[i], pdb, self.cr[5]])
                t2 = P.emit("vector", lambda e, p=ptb, i2=i:
                            e.tensor_copy(KHtm[:, 128 * i2:128 * (i2 + 1)],
                                          p[:, 0:128]),
                            [t1, war.get(("khtm", i))])
                self.pb.release(ptb, t2)
                ktm_h.append(t2)
            rqb_h = self.bcast_row(rowqb[0:1, :], C, rqb[:],
                                   [rq2, war.get("rqb")])

            if STAGE < 30:
                bars = self.P.barrier()
                self.last_x = [bars["vector"]] * 8
                continue
            o_h = {i: [] for i in range(8)}
            o_by_tile = {}
            for h in range(H):
                ti, half = h // 2, h % 2
                rows = slice(64 * half, 64 * (half + 1))
                tp = (64 * half, 0)
                hcols = slice(64 * h, 64 * (h + 1))
                pa = {}
                for nm2, lt, rt in (("a8d", "ks8e", "qs8e"),
                                    ("a8o", "ks8e", "qs8s"),
                                    ("a16", "ks16", "qs16")):
                    pt, pd = self.pf.acquire()
                    mm = P.emit(
                        "tensor",
                        lambda e, p=pt, l=lv[lt][ti], r=lv[rt][ti], rw=rows, tpp=tp:
                        e.matmul(p[:, 0:128], lhsT=l[rw, :], rhs=r[rw, :],
                                 start=True, stop=True, tile_position=tpp),
                        [lv_h[(ti, lt)], lv_h[(ti, rt)], pd])
                    pa[nm2] = (pt, mm)
                evs = {}
                for nm2, mk in (("a8d", m8d), ("a8o", m8o), ("a16", m16)):
                    evs[nm2] = P.emit(
                        "vector", lambda e, p=pa[nm2][0], m=mk, o=a_sb[nm2]:
                        e.tensor_tensor(out=o[:], in0=p[:, 0:128], in1=m,
                                        op=ALU.mult),
                        [pa[nm2][1]] + ar_last[nm2])
                    self.pf.release(pa[nm2][0], evs[nm2])
                pt64, pd64 = self.pf.acquire()
                mm64 = P.emit("tensor",
                              lambda e, p=pt64, l=lv["ks64"][ti], r=lv["qs64"][ti],
                              rw=rows, tpp=tp:
                              e.matmul(p[0:64, 64:128], lhsT=l[rw, 0:64],
                                       rhs=r[rw, 64:128],
                                       start=True, stop=True, tile_position=tpp),
                              [lv_h[(ti, "ks64")], lv_h[(ti, "qs64")], pd64])
                evs["a64"] = P.emit("scalar", lambda e, p=pt64:
                                    e.copy(a_sb["a64"][0:64, 64:128], p[0:64, 64:128]),
                                    [mm64] + ar_last["a64"])
                self.pf.release(pt64, evs["a64"])
                pt32, pd32 = self.pf.acquire()
                mm32a = P.emit("tensor",
                               lambda e, p=pt32, l=lv["ks32"][ti], r=lv["qs32"][ti],
                               rw=rows, tpp=tp:
                               e.matmul(p[0:32, 32:64], lhsT=l[rw, 0:32],
                                        rhs=r[rw, 32:64],
                                        start=True, stop=True, tile_position=tpp),
                               [lv_h[(ti, "ks32")], lv_h[(ti, "qs32")], pd32])
                mm32b = P.emit("tensor",
                               lambda e, p=pt32, l=lv["ks32"][ti], r=lv["qs32"][ti],
                               rw=rows, tpp=(64 * half, 64):
                               e.matmul(p[64:96, 96:128], lhsT=l[rw, 64:96],
                                        rhs=r[rw, 96:128],
                                        start=True, stop=True, tile_position=tpp),
                               [lv_h[(ti, "ks32")], lv_h[(ti, "qs32")], mm32a])
                ev32a = P.emit("scalar", lambda e, p=pt32:
                               e.copy(a_sb["a32"][0:32, 32:64], p[0:32, 32:64]),
                               [mm32a] + ar_last["a32"])
                ev32b = P.emit("scalar", lambda e, p=pt32:
                               e.copy(a_sb["a32"][64:96, 96:128], p[64:96, 96:128]),
                               [mm32b, ev32a])
                evs["a32"] = ev32b
                self.pf.release(pt32, [ev32a, ev32b])
                po, po_d = self.pf.acquire()
                mo = P.emit("tensor",
                            lambda e, p=po, s=Sbf[ti], z=Zq[ti], rw=rows,
                            tpp=(64 * half, 64 * half):
                            e.matmul(p[rw, 0:128], lhsT=s[rw, :], rhs=z[rw, :],
                                     start=True, stop=False, tile_position=tpp),
                            [sbf_h[ti], qsw_h[ti], po_d])
                intra = [("a8d", None, None), ("a8o", None, None),
                         ("a16", None, None),
                         ("a64", (0, 64), (64, 128)),
                         ("a32", (0, 32), (32, 64)),
                         ("a32b", (64, 96), (96, 128))]
                for idx2, (nm2, jr, ir) in enumerate(intra):
                    key = "a32" if nm2 == "a32b" else nm2
                    last = (idx2 == len(intra) - 1)
                    if jr is None:
                        fn = (lambda e, p=po, v=vtm, hc=hcols, rw=rows, o=a_sb[key],
                              lst=last, tpp=(0, 64 * half):
                              e.matmul(p[rw, 0:128], lhsT=v[:, hc], rhs=o[:],
                                       start=False, stop=lst, tile_position=tpp))
                    else:
                        j0, j1 = jr
                        i0, i1 = ir
                        fn = (lambda e, p=po, v=vtm, hc=hcols, rw=rows, o=a_sb[key],
                              j0=j0, j1=j1, i0=i0, i1=i1, lst=last,
                              tpp=(j0, 64 * half):
                              e.matmul(p[rw, i0:i1],
                                       lhsT=v[j0:j1, hc], rhs=o[j0:j1, i0:i1],
                                       start=False, stop=lst, tile_position=tpp))
                    mo_new = P.emit("tensor", fn, [vtm_h, evs[key]],
                                    sdeps=[mo] if mo else [])
                    mo = mo_new
                oev = P.emit("vector",
                             lambda e, p=po, o=osb[ti], rw=rows:
                             e.tensor_tensor(out=o[rw, :], in0=p[rw, 0:128],
                                             in1=rqb[rw, :], op=ALU.mult),
                             [mo, rqb_h, war.get(("osb", ti))] + o_h[ti])
                self.pf.release(po, oev)
                o_h[ti].append(oev)
                for nm2 in ("a8d", "a8o", "a16", "a64", "a32"):
                    ar_last[nm2] = [mo]
                o_by_tile.setdefault(ti, []).append(oev)

            wc_h = [None] * 8
            dec_h = [None] * 8
            for ti in range(8 if STAGE >= 40 else 0):
                wc = P.emit("scalar", lambda e, b=Bt[ti], w=wcol[ti]:
                            e.activation(out=w[:], in_=b[:, 128:129], func=AF.Exp),
                            [B_h[ti], war.get(("wcol", ti))])
                wc_h[ti] = wc
                dec = P.emit("scalar", lambda e, s=Sf[ti], w=wcol[ti]:
                             e.mul(s[:], s[:], w[:, 0:1]),
                             [wc] + sf_h[ti] + o_by_tile[ti])
                dec_h[ti] = dec
                sus = []
                for half in range(2):
                    h = 2 * ti + half
                    rows = slice(64 * half, 64 * (half + 1))
                    ps2, ps2d = self.pf.acquire()
                    msu = P.emit("tensor",
                                 lambda e, p=ps2, kt=KHtm, v=vtm, rw=rows,
                                 hc=slice(64 * h, 64 * (h + 1)),
                                 tpp=(0, 64 * half):
                                 e.matmul(p[rw, 0:64], lhsT=kt[:, hc],
                                          rhs=v[:, hc],
                                          start=True, stop=True, tile_position=tpp),
                                 [ktm_h[ti], vtm_h, ps2d])
                    su = P.emit("vector", lambda e, p=ps2, s=Sf[ti], rw=rows:
                                e.tensor_add(s[rw, :], s[rw, :], p[rw, 0:64]),
                                [msu, dec])
                    self.pf.release(ps2, su)
                    sus.append(su)
                sf_h[ti] = sus
                sbf_h[ti] = [P.emit("vector", lambda e, s=Sf[ti], d=Sbf[ti]:
                                    e.tensor_copy(d[:], s[:]), sus)]

            if STAGE < 50:
                bars = self.P.barrier()
                self.last_x = [bars["vector"]] * 8
                for i in range(8):
                    sf_h[i] = [bars["vector"]]
                    sbf_h[i] = [bars["vector"]]
                continue
            def wo_evac(oc, pt_, mmh):
                ev = P.emit("vector", lambda e, p=pt_, s=x[oc], c=cs:
                            e.tensor_add(s[:, c], s[:, c], p[:, 0:C]),
                            [mmh, self.last_x[oc]])
                self.last_x[oc] = ev
                return ev
            wo_hs = self.proj(wo, wo_h, osb, [o_h[i] for i in range(8)],
                              None, C, wo_evac)

            war = {"xbf": list(vtm_h), "eb": [mmq], "rowf": [g_h[0]],
                   "rowq": [rq2], "rowqb": [rqb_h],
                   "rqb": [o_by_tile[ti][-1] for ti in range(8)],
                   "vtm": [o_by_tile[ti][-1] for ti in range(8)]
                   + [h for ti in range(8) for h in sf_h[ti]]}
            for i in range(8):
                war[("ft", i)] = [B_h[i], ft_read[i]]
                war[("bt", i)] = kh1s[i] + qh1s[i] + [wc_h[i]] + [G_h[i]]
                war[("kt", i)] = kh1s[i]
                war[("gt", i)] = qh1s[i]
                war[("lvk", i)] = o_by_tile[i] + [ktm_h[i]]
                war[("lvq", i)] = list(o_by_tile[i])
                war[("khtm", i)] = list(sf_h[i])
                war[("osb", i)] = [wo_hs[-1]]
                war[("wcol", i)] = [dec_h[i]]
        bars = self.P.barrier()
        self.last_x = [bars["vector"]] * 8
        sc.close()

    def emit_postln(self, tag):
        P = self.P
        x = self.x
        sc = P.scope()
        xbf = [P.sbs(sc, f"ln_xbf_{tag}", [128, 512], BF16) for _ in range(8)]
        sq = [P.sbs(sc, f"ln_sq_{tag}", [128, 512], BF16) for _ in range(2)]
        srow = P.sbs(sc, f"ln_srow_{tag}", [1, L])
        ssrow = P.sbs(sc, f"ln_ssrow_{tag}", [1, L])
        trow = P.sbs(sc, f"ln_t_{tag}", [1, L])
        srow_b = P.sbs(sc, f"ln_srow_b_{tag}", [1, L], BF16)
        ssrow_b = P.sbs(sc, f"ln_ssrow_b_{tag}", [1, L], BF16)
        rbs = [P.sbs(sc, f"ln_rb_{tag}", [128, 512]) for _ in range(2)]
        mbs = [P.sbs(sc, f"ln_mb_{tag}", [128, 512]) for _ in range(2)]
        seg_h = []
        prev_cast = {i: None for i in range(8)}
        prev_sq = {}
        for ct in range(4):
            cseg = slice(512 * ct, 512 * (ct + 1))
            cast_h = []
            for i in range(8):
                eng = "vector" if i % 2 else "scalar"
                if eng == "vector":
                    fn = (lambda e, o=xbf[i], s2=x[i], c=cseg:
                          e.tensor_copy(o[:], s2[:, c]))
                else:
                    fn = (lambda e, o=xbf[i], s2=x[i], c=cseg:
                          e.copy(o[:], s2[:, c]))
                cast_h.append(P.emit(eng, fn, [self.last_x[i], prev_cast[i]]))
            pt, pd = self.pr.acquire()
            pt2, pd2 = self.pf.acquire()
            mm1 = mm2 = None
            for i in range(8):
                mm1 = P.emit("tensor",
                             lambda e, p=pt, r=xbf[i], kk=i:
                             e.matmul(p[0:1, 0:512], lhsT=self.ones_col_bf[:, 0:1],
                                      rhs=r[:], start=(kk == 0), stop=(kk == 7)),
                             [cast_h[i], self.cr[1]] + ([pd] if i == 0 else [mm1]))
                sqt = sq[i % 2]
                sh = P.emit("scalar",
                            lambda e, o=sqt, s2=xbf[i]:
                            e.activation(out=o[:], in_=s2[:], func=AF.Square),
                            [cast_h[i], prev_sq.get(i % 2)])
                mm2 = P.emit("tensor",
                             lambda e, p=pt2, r=sqt, kk=i:
                             e.matmul(p[0:1, 0:512], lhsT=self.ones_col_bf[:, 0:1],
                                      rhs=r[:], start=(kk == 0), stop=(kk == 7)),
                             [sh, self.cr[1]] + ([pd2] if i == 0 else [mm2]))
                prev_sq[i % 2] = mm2
                prev_cast[i] = [mm1, sh]
            e1 = P.emit("vector", lambda e, p=pt, c=cseg:
                        e.tensor_copy(srow[0:1, c], p[0:1, 0:512]), [mm1])
            e2 = P.emit("vector", lambda e, p=pt2, c=cseg:
                        e.tensor_copy(ssrow[0:1, c], p[0:1, 0:512]), [mm2])
            self.pr.release(pt, e1)
            self.pf.release(pt2, e2)
            seg_h.append((e1, e2))
        newx = list(self.last_x)
        war = {0: [], 1: []}
        seg_hh = {}
        for ct in range(4):
            cseg = slice(512 * ct, 512 * (ct + 1))
            e1, e2 = seg_h[ct]
            h1 = P.emit("scalar", lambda e, c=cseg:
                        e.mul(srow[0:1, c], srow[0:1, c], 1.0 / D), [e1])
            h2 = P.emit("scalar", lambda e, c=cseg:
                        e.activation(out=trow[0:1, c], in_=srow[0:1, c],
                                     func=AF.Square), [h1])
            h3 = P.emit("vector", lambda e, c=cseg: e.scalar_tensor_tensor(
                out=ssrow[0:1, c], in0=ssrow[0:1, c], scalar=1.0 / D,
                in1=trow[0:1, c], op0=ALU.mult, op1=ALU.subtract), [h2, e2])
            h5 = P.emit("scalar", lambda e, c=cseg:
                        e.activation(out=ssrow[0:1, c], in_=ssrow[0:1, c],
                                     func=AF.Ln, bias=self.eps_col[0:1, 0:1]),
                        [h3, self.cr[4]])
            h6 = P.emit("scalar", lambda e, c=cseg:
                        e.activation(out=ssrow[0:1, c], in_=ssrow[0:1, c],
                                     func=AF.Exp, scale=-0.5), [h5])
            h78 = P.emit("vector", lambda e, c=cseg: e.scalar_tensor_tensor(
                out=srow[0:1, c], in0=srow[0:1, c], scalar=-1.0,
                in1=ssrow[0:1, c], op0=ALU.mult, op1=ALU.mult), [h6, h1])
            h9 = P.emit("scalar", lambda e, c=cseg:
                        e.copy(ssrow_b[0:1, c], ssrow[0:1, c]), [h6])
            h10 = P.emit("scalar", lambda e, c=cseg:
                         e.copy(srow_b[0:1, c], srow[0:1, c]), [h78])
            seg_hh[ct] = (h9, h10)
        for ct in range(4):
            cseg = slice(512 * ct, 512 * (ct + 1))
            h9, h10 = seg_hh[ct]
            b1 = self.bcast_row(ssrow_b[0:1, cseg], 512, rbs[ct % 2][:],
                                [h9] + war[ct % 2])
            b2 = self.bcast_row(srow_b[0:1, cseg], 512, mbs[ct % 2][:],
                                [h10] + war[ct % 2])
            applies = []
            for i in range(8):
                eng = "gpsimd" if i % 4 == 3 else "vector"
                a = P.emit(eng, lambda e, s2=x[i], c=cseg, r=rbs[ct % 2]:
                           e.tensor_mul(s2[:, c], s2[:, c], r[:]),
                           [b1, newx[i]])
                n2 = P.emit(eng, lambda e, s2=x[i], c=cseg, m=mbs[ct % 2]:
                            e.tensor_add(s2[:, c], s2[:, c], m[:]), [a, b2])
                applies.append(n2)
                newx[i] = n2
            war[ct % 2] = applies
        self.last_x = newx
        bars = self.P.barrier()
        self.last_x = [bars["vector"]] * 8
        sc.close()

    def emit_l0_ffn(self):
        P = self.P
        x = self.x
        nc = P.nc
        sc = P.scope()
        E = 8
        CAP = 2048 * 8 // 8

        wg_d = self.din("l0f_wg", [D, E])
        we_d = self.din("l0f_we", [D, D])
        be_d = self.din("l0f_be", [1, D])
        ps_d = self.din("presel", [E, 1])

        wg = P.sbs(sc, "wg", [128, 8, E])
        wg_h = P.dma("sync", wg.ap(), wg_d.rearrange("(c p) n -> p c n", p=128))
        we, we_h = self.load_w_bf(we_d, D, D, scope=sc)
        be_bf = P.sbs(sc, "be", [1, D], BF16)
        be_h = P.dma("gpsimd", be_bf[:], be_d[:])
        presel = P.sbs(sc, "presel", [E, 1])
        psel_h = P.dma("sync", presel[:], ps_d[:])
        idf = P.sbs(sc, "idf", [128, 128])
        idf_h = P.dma("sync", idf[:], self.idty_d[:])
        ones512_bf = P.sbs(sc, "ones512b", [1, 512], BF16)
        o5_h = P.emit("gpsimd", lambda e: e.memset(ones512_bf[:], 1.0))
        tri = P.sbs(sc, "tri", [128, 128])
        t1 = P.emit("gpsimd", lambda e: e.memset(tri[:], 1.0))
        tri_h = P.emit("gpsimd", lambda e: e.affine_select(
            out=tri[:], in_=tri[:], compare_op=ALU.is_ge, fill=0.0,
            base=0, pattern=[[1, 128]], channel_multiplier=-1), [t1])

        mask_sb = [P.sbs(sc, "moemask", [128, E]) for _ in range(16)]
        gate_sb = P.sbs(sc, "moegate", [128, 16])
        gf_sb = P.sbs(sc, "moegf", [128, 16])
        counts = P.sbs(sc, "cnt", [E, 1])
        crow = P.sbs(sc, "crow", [1, E])
        gath = P.sbs(sc, "gath", [E, E])
        offrow = P.sbs(sc, "offrow", [1, E])
        offb = P.sbs(sc, "offb", [128, E])
        tmp8 = P.sbs(sc, "tmp8", [128, E])
        col1 = P.sbs(sc, "col1", [128, 1])
        grow = P.sbs(sc, "grow", [1, L], BF16)
        gb = P.sbs(sc, "gb", [128, 512])
        xbf = [P.sbs(sc, "moex", [128, 512], BF16) for _ in range(8)]
        ytmp = P.sbs(sc, "ytmp", [128, 512])
        gscr_d = nc.dram_tensor("moe_scratch", [16, 128], F32, kind="Internal").ap()
        cc_in = nc.dram_tensor("moe_cc_in", [1, E], F32, kind="Internal").ap()
        cc_out = nc.dram_tensor("moe_cc_out", [E, E], F32, kind="Internal").ap()

        cnt_mm = None
        pcnt, pcnt_d = self.pf.acquire()
        mask_h = [None] * 16
        gate_h = [None] * 16
        prev_sh = None
        for t in range(16):
            tc2 = slice(128 * t, 128 * (t + 1))
            pt, pd = self.pf.acquire()
            mm = None
            for k in range(8):
                mm = P.emit("tensor",
                            lambda e, p=pt, kk=k, c=tc2:
                            e.matmul(p[:, 0:E], lhsT=self.x[kk][:, c],
                                     rhs=wg[:, kk, :], start=(kk == 0), stop=(kk == 7)),
                            [wg_h, self.last_x[k]] + ([pd] if k == 0 else [mm]))
            nmx = P.emit("vector", lambda e, p=pt: e.reduce_max(
                col1[:, 0:1], p[:, 0:E], axis=AX.X, negate=True),
                [mm, gate_h[t - 1] if t > 0 else None])
            mh = P.emit("vector", lambda e, p=pt, o=mask_sb[t]:
                        e.tensor_scalar(out=o[:], in0=p[:, 0:E], scalar1=col1[:, 0:1],
                                        scalar2=None, op0=ALU.add), [nmx])
            mh2 = P.emit("vector", lambda e, o=mask_sb[t]:
                         e.tensor_scalar(out=o[:], in0=o[:], scalar1=-1e-30,
                                         scalar2=None, op0=ALU.is_ge), [mh])
            eh = P.emit("scalar", lambda e, p=pt, o=tmp8:
                        e.activation(out=o[:], in_=p[:, 0:E], func=AF.Exp,
                                     bias=col1[:, 0:1]),
                        [nmx, mh, prev_sh])
            sh = P.emit("vector", lambda e, o=tmp8: e.reduce_sum(
                col1[:, 0:1], o[:], axis=AX.X), [eh, mh2])
            gh = P.emit("vector", lambda e, t2=t: e.reciprocal(
                gate_sb[:, t2:t2 + 1], col1[:, 0:1]), [sh])
            mask_h[t] = mh2
            gate_h[t] = gh
            prev_sh = sh
            self.pf.release(pt, [mh2, eh])
            cnt_mm = P.emit("tensor",
                            lambda e, p=pcnt, m=mask_sb[t], t2=t:
                            e.matmul(p[0:E, 0:1], lhsT=m[:], rhs=self.ones_col[:, 0:1],
                                     start=(t2 == 0), stop=(t2 == 15)),
                            [mh2, self.cr[0], cnt_mm] + ([pcnt_d] if t == 0 else []))
        ev_c = P.emit("vector", lambda e, p=pcnt: e.tensor_copy(counts[:], p[0:E, 0:1]),
                      [cnt_mm])
        self.pf.release(pcnt, ev_c)
        ptc, ptc_d = self.pf.acquire()
        mmt = P.emit("tensor", lambda e, p=ptc: e.matmul(
            p[0:1, 0:E], lhsT=counts[:], rhs=idf[0:E, 0:E],
            start=True, stop=True), [ev_c, idf_h, ptc_d])
        ev_cr = P.emit("vector", lambda e, p=ptc: e.tensor_copy(crow[:], p[0:1, 0:E]),
                       [mmt])
        self.pf.release(ptc, ev_cr)
        d1 = P.dma("gpsimd", cc_in[:], crow[:], [ev_cr])
        ccs = P.emit("gpsimd", lambda e: e.collective_compute(
            "AllGather", ALU.bypass, replica_groups=[list(range(8))],
            ins=[cc_in.opt()], outs=[cc_out.opt()]), [d1])
        d2 = P.dma("gpsimd", gath[:], cc_out[:], [ccs])
        pto, pto_d = self.pf.acquire()
        mmo = P.emit("tensor", lambda e, p=pto: e.matmul(
            p[0:1, 0:E], lhsT=presel[:], rhs=gath[:],
            start=True, stop=True), [d2, psel_h, pto_d])
        ev_or = P.emit("vector", lambda e, p=pto: e.tensor_copy(offrow[:], p[0:1, 0:E]),
                       [mmo])
        self.pf.release(pto, ev_or)
        offb_h = self.bcast_row(offrow[0:1, :], E, offb[:], [ev_or], f32=True)

        gf_h = [None] * 16
        for t in range(16):
            pt, pd = self.pf.acquire()
            mm = None
            for j in range(t):
                mm = P.emit("tensor", lambda e, p=pt, m=mask_sb[j], jj=j:
                            e.matmul(p[:, 0:E], lhsT=self.ones_chunk[:, 0:128],
                                     rhs=m[:], start=(jj == 0), stop=False),
                            [mask_h[j], self.cr[3]] + ([pd] if j == 0 else [mm]))
            mm = P.emit("tensor", lambda e, p=pt, m=mask_sb[t], st=(t == 0):
                        e.matmul(p[:, 0:E], lhsT=tri[:], rhs=m[:],
                                 start=st, stop=False),
                        [mask_h[t], tri_h, mm] + ([pd] if t == 0 else []))
            mm = P.emit("tensor", lambda e, p=pt: e.matmul(
                p[:, 0:E], lhsT=idf[:], rhs=offb[:],
                start=False, stop=True), [offb_h, idf_h, mm])
            kh = P.emit("vector", lambda e, p=pt: e.tensor_scalar(
                out=tmp8[:], in0=p[:, 0:E], scalar1=float(CAP) + 0.5, scalar2=None,
                op0=ALU.is_lt), [mm, gf_h[t - 1] if t > 0 else prev_sh])
            kh2 = P.emit("vector", lambda e, m=mask_sb[t]:
                         e.tensor_mul(tmp8[:], tmp8[:], m[:]), [kh, mask_h[t]])
            kh3 = P.emit("vector", lambda e: e.reduce_sum(
                col1[:, 0:1], tmp8[:], axis=AX.X), [kh2])
            gf_h[t] = P.emit("vector", lambda e, t2=t: e.tensor_mul(
                gf_sb[:, t2:t2 + 1], gate_sb[:, t2:t2 + 1], col1[:, 0:1]),
                [kh3, gate_h[t]])
            self.pf.release(pt, [kh, kh2])
        ptg, ptg_d = self.pf.acquire()
        tg = P.emit("tensor", lambda e, p=ptg: e.transpose(
            p[0:16, 0:128], in_=gf_sb[:], identity=idf[:]), gf_h + [idf_h, ptg_d])
        ev_tg = P.emit("vector", lambda e, p=ptg: e.tensor_copy(
            gb[0:16, 0:128], p[0:16, 0:128]), [tg])
        self.pf.release(ptg, ev_tg)
        d3 = P.dma("gpsimd", gscr_d[:], gb[0:16, 0:128], [ev_tg])
        d4 = P.dma("gpsimd", grow[0:1, :], gscr_d.rearrange("a b -> (a b)").unsqueeze(0), [d3])

        for s2 in range(4):
            cs = slice(512 * s2, 512 * (s2 + 1))
            xbf_h = self.cast_bf(xbf, cs, self.last_x)
            gbh = self.bcast_row(grow[0:1, cs], 512, gb[:], [d4, ev_tg])
            prev_y = None
            for oc in range(8):
                pt, pd = self.pf.acquire()
                mm = None
                for k in range(8):
                    mm = P.emit("tensor",
                                lambda e, p=pt, kk=k, o=oc:
                                e.matmul(p[:, 0:512],
                                         lhsT=we[:, kk, 128 * o:128 * (o + 1)],
                                         rhs=xbf[kk][:], start=(kk == 0), stop=False),
                                [we_h, xbf_h[k]] + ([pd] if k == 0 else [mm]))
                mm = P.emit("tensor", lambda e, p=pt, o=oc: e.matmul(
                    p[:, 0:512], lhsT=be_bf[0:1, 128 * o:128 * (o + 1)],
                    rhs=ones512_bf[0:1, :], start=False, stop=True),
                    [be_h, o5_h, mm])
                h1 = P.emit("vector", lambda e, p=pt:
                            e.tensor_mul(ytmp[:], p[:, 0:512], gb[:]),
                            [mm, gbh, prev_y])
                h2 = P.emit("vector", lambda e, s3=x[oc], c=cs:
                            e.tensor_add(s3[:, c], s3[:, c], ytmp[:]),
                            [h1, self.last_x[oc]])
                self.last_x[oc] = h2
                prev_y = h2
                self.pf.release(pt, h1)
            bars = P.barrier()
            self.last_x = [bars["vector"]] * 8
        sc.close()

    def emit_l1_ffn(self):
        P = self.P
        x = self.x
        sc = P.scope()
        NSTATE = 4096
        w1_d = self.din("l1f_w1", [D, NSTATE])
        b1_d = self.din("l1f_b1", [1, NSTATE])
        w2_d = self.din("l1f_w2", [NSTATE, D])
        b2_d = self.din("l1f_b2", [1, D])

        w1 = P.sbs(sc, "w1", [128, 8, NSTATE], BF16)
        w1_hk = [P.dma("gpsimd", w1[:, k, :],
                       w1_d[128 * k:128 * (k + 1), :]) for k in range(8)]
        b1_bf = P.sbs(sc, "b1", [1, NSTATE], BF16)
        b1_h = P.dma("gpsimd", b1_bf[:], b1_d[:])
        b2_bf = P.sbs(sc, "b2", [1, D], BF16)
        b2_h = P.dma("gpsimd", b2_bf[:], b2_d[:])
        ones512_bf = P.sbs(sc, "ones512c", [1, 512], BF16)
        o5_h = P.emit("gpsimd", lambda e: e.memset(ones512_bf[:], 1.0))
        w2b = [P.sbs(sc, "w2b", [128, 32, 128], BF16) for _ in range(2)]
        xbf = [P.sbs(sc, "f1x", [128, 512], BF16) for _ in range(8)]
        h1t = [P.sbs(sc, "h1t", [128, 512], BF16) for _ in range(32)]
        w2_loads = {}
        w2_cons = {}

        for s2 in range(4):
            cs = slice(512 * s2, 512 * (s2 + 1))
            xbf_h = self.cast_bf(xbf, cs, self.last_x)
            h1_h = [None] * 32
            for hc in range(8):
                for pt2 in range(4):
                    ti = 4 * hc + pt2
                    ncol = slice(512 * hc + 128 * pt2, 512 * hc + 128 * (pt2 + 1))
                    pt, pd = self.pf.acquire()
                    mm = None
                    for k in range(8):
                        mm = P.emit("tensor",
                                    lambda e, p=pt, kk=k, nc2=ncol:
                                    e.matmul(p[:, 0:512], lhsT=w1[:, kk, nc2],
                                             rhs=xbf[kk][:],
                                             start=(kk == 0), stop=False),
                                    [w1_hk[k], xbf_h[k]]
                                    + ([pd] if k == 0 else [mm]))
                    mm = P.emit("tensor", lambda e, p=pt, nc2=ncol: e.matmul(
                        p[:, 0:512], lhsT=b1_bf[0:1, nc2], rhs=ones512_bf[0:1, :],
                        start=False, stop=True), [b1_h, o5_h, mm])
                    gh = P.emit("scalar", lambda e, p=pt, o=h1t[ti]:
                                e.activation(out=o[:], in_=p[:, 0:512], func=AF.Gelu),
                                [mm, h1_h[ti]])
                    h1_h[ti] = gh
                    self.pf.release(pt, gh)
            for oc in range(8):
                wb = w2b[oc % 2]
                ld = P.dma("gpsimd", wb.ap(),
                           w2_d[:, 128 * oc:128 * (oc + 1)]
                           .rearrange("(c p) n -> p c n", p=128),
                           [w2_cons.get(oc % 2)])
                w2_loads[(s2, oc)] = ld
                pt, pd = self.pf.acquire()
                mm = None
                for k in range(32):
                    mm = P.emit("tensor",
                                lambda e, p=pt, kk=k, w=wb:
                                e.matmul(p[:, 0:512], lhsT=w[:, kk, :], rhs=h1t[kk][:],
                                         start=(kk == 0), stop=False),
                                [ld, h1_h[k]] + ([pd] if k == 0 else [mm]))
                mm = P.emit("tensor", lambda e, p=pt, o=oc: e.matmul(
                    p[:, 0:512], lhsT=b2_bf[0:1, 128 * o:128 * (o + 1)],
                    rhs=ones512_bf[0:1, :], start=False, stop=True),
                    [b2_h, o5_h, mm])
                ev = P.emit("vector", lambda e, p=pt, s3=x[oc], c=cs:
                            e.tensor_add(s3[:, c], s3[:, c], p[:, 0:512]),
                            [mm, self.last_x[oc]])
                self.last_x[oc] = ev
                self.pf.release(pt, ev)
                w2_cons[oc % 2] = mm
            bars = P.barrier()
            self.last_x = [bars["vector"]] * 8
        sc.close()

    def emit_l1_attn(self):
        P = self.P
        x = self.x
        nc = P.nc
        sc = P.scope()
        H, DK, DV = 4, 256, 256
        SL = 512
        NSL = L // SL

        wq_d = self.din("l1_wq", [D, D])
        wk_d = self.din("l1_wk", [D, D])
        wv_d = self.din("l1_wv", [D, D])
        wo_d = self.din("l1_wo", [D, D])
        wb_d = self.din("l1_wb", [D, H])
        wg_d = self.din("l1_wg", [D, H])
        wl_d = self.din("l1_wl", [D, 12])
        bg_d = self.din("l1_bg", [1, H])
        bl_d = self.din("l1_bl", [1, 12])
        lt_d = self.din("l1_lt", [1, 12])
        rms_d = self.din("l1_rms", [128, 8])
        cq_d = self.din("l1_cq", [D, 4])
        ck_d = self.din("l1_ck", [D, 4])
        cv_d = self.din("l1_cv", [D, 4])
        fs_d = self.din("l1_fs", [D, 3])
        flw_d = self.din("l1_flw", [128, 8 * 31])
        dmask_d = self.din("c_dmask", [2 * 128, 128])

        wb_sb = P.sbs(sc, "l1_wbs", [128, 8, H], BF16)
        wg_sb = P.sbs(sc, "l1_wgs", [128, 8, H], BF16)
        wl_sb = P.sbs(sc, "l1_wls", [128, 8, 12], BF16)
        wb_h = P.dma("gpsimd", wb_sb.ap(), wb_d.rearrange("(c p) n -> p c n", p=128))
        wg_h = P.dma("gpsimd", wg_sb.ap(), wg_d.rearrange("(c p) n -> p c n", p=128))
        wl_h = P.dma("gpsimd", wl_sb.ap(), wl_d.rearrange("(c p) n -> p c n", p=128))
        cqc = P.sbs(sc, "l1_cqc", [128, 8, 4])
        ckc = P.sbs(sc, "l1_ckc", [128, 8, 4])
        cvc = P.sbs(sc, "l1_cvc", [128, 8, 4])
        fsc = P.sbs(sc, "l1_fsc", [128, 8, 3])
        cq_h = P.dma("sync", cqc.ap(), cq_d.rearrange("(c p) n -> p c n", p=128))
        ck_h = P.dma("sync", ckc.ap(), ck_d.rearrange("(c p) n -> p c n", p=128))
        cv_h = P.dma("sync", cvc.ap(), cv_d.rearrange("(c p) n -> p c n", p=128))
        fs_h = P.dma("sync", fsc.ap(), fs_d.rearrange("(c p) n -> p c n", p=128))
        bg_bf = P.sbs(sc, "l1_bg", [1, H], BF16)
        bl_bf = P.sbs(sc, "l1_bl", [1, 12], BF16)
        bg_h = P.dma("gpsimd", bg_bf[:], bg_d[:])
        bl_h = P.dma("gpsimd", bl_bf[:], bl_d[:])
        lt_sb = P.sbs(sc, "l1_lt", [1, 12])
        lt_h = P.dma("sync", lt_sb[:], lt_d[:])
        rms_sb = P.sbs(sc, "l1_rms", [128, 8])
        rms_h = P.dma("sync", rms_sb.ap(), rms_d[:])
        dmask = P.sbs(sc, "l1_dm", [128, 2, 128], BF16)
        dm_h = P.dma("gpsimd", dmask.ap(), dmask_d.rearrange("(m p) n -> p m n", p=128))
        mneg_low = dmask[:, 0, :]
        m_up_incl = dmask[:, 1, :]
        eps6 = P.sbs(sc, "eps6", [1, 1])
        e6_h = P.emit("gpsimd", lambda e: e.memset(eps6[:], 1e-6))
        ones1_bf = P.sbs(sc, "ones1b", [1, 128], BF16)
        o1b_h = P.emit("gpsimd", lambda e: e.memset(ones1_bf[:], 1.0))
        etrow = P.sbs(sc, "etrow", [1, 12], BF16)
        et1 = P.emit("scalar", lambda e: e.activation(
            out=etrow[:], in_=lt_sb[:], func=AF.Exp, scale=-1.0), [lt_h])
        etb = P.sbs(sc, "etb", [128, 12])
        etb_h = self.bcast_row(etrow[0:1, :], 12, etb[:], [et1])

        wbuf = [P.sbs(sc, "l1_w", [128, 8, 256], BF16) for _ in range(2)]
        flw = P.sbs(sc, "l1_flw", [128, 8, 31])
        flw_h = P.dma("sync", flw.ap(), flw_d.rearrange("p (o t) -> p o t", t=31))
        flh = [P.sbs(sc, "l1_flh", [128, n, 128], BF16) for n in (11, 10, 10)]
        pad = [P.sbs(sc, "l1_pad", [128, 3 + SL], BF16) for _ in range(8)]
        vsil = [P.sbs(sc, "l1_vsil", [128, 30 + SL], BF16) for _ in range(8)]
        tails = {t: [P.sbs(sc, f"l1_tl{t}", [128, 3], BF16) for _ in range(8)]
                 for t in ("q", "k", "v")}
        vtails = [P.sbs(sc, "l1_vtl", [128, 30], BF16) for _ in range(8)]
        qn = [P.sbs(sc, "l1_qn", [128, SL], BF16) for _ in range(8)]
        kn = [P.sbs(sc, "l1_kn", [128, SL], BF16) for _ in range(8)]
        dtile = [P.sbs(sc, "l1_dt", [128, 128], BF16) for _ in range(2)]
        vtm_sets = [P.sbs(sc, "l1_vtm", [128, DV], BF16) for _ in range(4)]
        ktm_sets = [P.sbs(sc, "l1_ktm", [128, DK], BF16) for _ in range(4)]
        kbtm_sets = [P.sbs(sc, "l1_kbtm", [128, DK], BF16) for _ in range(4)]
        beta_tm = [P.sbs(sc, "l1_btm", [128, H]) for _ in range(4)]
        gates_tm = [P.sbs(sc, "l1_gtm", [128, 16]) for _ in range(4)]
        dsc_sets = [{n: P.sbs(sc, f"l1_{n}", [128, 128], BF16)
                     for n in ("Nt", "Mt", "N2", "M2", "N4", "M4", "N8", "M8",
                               "N16", "M16", "N32", "M32", "N64", "Xc", "attn",
                               "u1", "u2", "w1", "w2", "ws1", "ws2")}
                    for _ in range(4)]
        Sf = [P.sbs(sc, "l1_S", [128, DV]) for _ in range(8)]
        Sbf = [P.sbs(sc, "l1_Sb", [128, DV], BF16) for _ in range(8)]
        macc = [P.sbs(sc, "l1_macc", [128, SL]) for _ in range(8)]
        mtmp = P.sbs(sc, "l1_mtmp", [128, SL], BF16)
        mtmp2 = P.sbs(sc, "l1_mtmp2", [128, SL], BF16)
        gbc = [P.sbs(sc, "l1_gbc", [128, SL], BF16) for _ in range(2)]
        gbp = [P.sbs(sc, "l1_gbp", [128, SL], BF16) for _ in range(3)]
        l2row = P.sbs(sc, "l1_l2r", [1, SL])
        l2rb = P.sbs(sc, "l1_l2rb", [1, SL], BF16)
        rbc = P.sbs(sc, "l1_rbc", [128, SL], BF16)
        sq_bf = P.sbs(sc, "l1_sqbf", [128, SL], BF16)
        osb = [P.sbs(sc, "l1_osb", [128, SL], BF16) for _ in range(8)]
        selg_d = self.din("c_selg", [16, 16 * 128])
        selg = P.sbs(sc, "l1_selg", [16, 16, 128], BF16)
        selg_h = P.dma("gpsimd", selg.ap(),
                       selg_d.rearrange("a (r n) -> a r n", n=128))
        gtr4 = P.sbs(sc, "l1_gtr4", [16, 4, 128], BF16)
        idf2 = P.sbs(sc, "l1_idf", [128, 128])
        idf2_h = P.dma("sync", idf2[:], self.idty_d[:])

        s_init = []
        for i in range(8):
            s_init.append(P.emit("gpsimd", lambda e, s2=Sf[i]: e.memset(s2[:], 0.0)))
            s_init.append(P.emit("gpsimd", lambda e, s2=Sbf[i]: e.memset(s2[:], 0.0)))
            for t in ("q", "k", "v"):
                s_init.append(P.emit("gpsimd", lambda e, s2=tails[t][i]:
                                     e.memset(s2[:], 0.0)))
            s_init.append(P.emit("gpsimd", lambda e, s2=vtails[i]: e.memset(s2[:], 0.0)))
        init_bar = P.barrier()
        self.last_x = [init_bar["vector"]] * 8
        sbf_h = {i: [init_bar["vector"]] for i in range(8)}
        sf_h = {i: [init_bar["vector"]] for i in range(8)}

        xbf = [P.sbs(sc, "l1_xbf", [128, SL], BF16) for _ in range(8)]
        wslot = {"i": 0, "cons": {0: None, 1: None}}

        def stream_w(dram, col0):
            bidx = wslot["i"] % 2
            buf = wbuf[bidx]
            h = P.dma("gpsimd", buf.ap(),
                      dram[:, col0:col0 + 256].rearrange("(c p) n -> p c n", p=128),
                      [wslot["cons"][bidx]])
            wslot["i"] += 1
            return buf, h, bidx

        PRB = os.environ.get("DEV_L1_PROBE")

        def probe(src_tiles, cs2, deps_fn):
            hs3 = []
            for i in range(8):
                hs3.append(P.emit("vector",
                                  lambda e, s3=x[i], o=src_tiles[i], c=cs2:
                                  e.tensor_copy(s3[:, c], o[:]),
                                  [deps_fn(i), self.last_x[i]]))
            return hs3

        for sl2 in range(NSL):
            cs = slice(SL * sl2, SL * (sl2 + 1))
            xbf_h = self.cast_bf(xbf, cs, self.last_x)
            tens_h = {}
            dt_use = {0: None, 1: None}

            if PRB == "xbf":
                for i in range(8):
                    P.emit("vector", lambda e, s3=x[i], o=xbf[i], c=cs:
                           e.tensor_copy(s3[:, c], o[:]),
                           [xbf_h[i], self.last_x[i]])
                bars = P.barrier()
                self.last_x = [bars["vector"]] * 8
                continue
            bg_done = []
            for ch in range(4 if not os.environ.get("DEV_L1_NOGATES") else 0):
                lcol = slice(128 * ch, 128 * (ch + 1))
                ptb, pdb = self.pf.acquire()
                mm = None
                for k in range(8):
                    mm = P.emit("tensor",
                                lambda e, p=ptb, kk=k, c2=lcol:
                                e.matmul(p[:, 0:H], lhsT=xbf[kk][:, c2],
                                         rhs=wb_sb[:, kk, :],
                                         start=(kk == 0), stop=(kk == 7)),
                                [wb_h, xbf_h[k]] + ([pdb] if k == 0 else [mm]))
                bh = P.emit("scalar", lambda e, p=ptb, o=beta_tm[ch]:
                            e.activation(out=o[:], in_=p[:, 0:H], func=AF.Sigmoid),
                            [mm])
                self.pf.release(ptb, bh)
                ptg, pdg = self.pf.acquire()
                mm = None
                for k in range(8):
                    mm = P.emit("tensor",
                                lambda e, p=ptg, kk=k, c2=lcol:
                                e.matmul(p[:, 0:12], lhsT=xbf[kk][:, c2],
                                         rhs=wl_sb[:, kk, :],
                                         start=(kk == 0), stop=False),
                                [wl_h, xbf_h[k]] + ([pdg] if k == 0 else [mm]))
                mm = P.emit("tensor", lambda e, p=ptg: e.matmul(
                    p[:, 0:12], lhsT=ones1_bf[0:1, :], rhs=bl_bf[0:1, :],
                    start=False, stop=True), [bl_h, o1b_h, mm])
                gt = gates_tm[ch]
                g1 = P.emit("vector", lambda e, p=ptg, o=gt:
                            e.tensor_mul(o[:, 4:16], p[:, 0:12], etb[:]),
                            [mm, etb_h])
                self.pf.release(ptg, g1)
                m1 = P.emit("vector", lambda e, o=gt: e.tensor_tensor(
                    out=o[:, 0:4], in0=o[:, 4:16:3], in1=o[:, 5:16:3], op=ALU.max),
                    [g1])
                m2 = P.emit("vector", lambda e, o=gt: e.tensor_tensor(
                    out=o[:, 0:4], in0=o[:, 0:4], in1=o[:, 6:16:3], op=ALU.max),
                    [m1])
                subs = []
                for j in range(3):
                    sj = P.emit("vector", lambda e, o=gt, jj=j: e.tensor_sub(
                        o[:, 4 + jj:16:3], o[:, 4 + jj:16:3], o[:, 0:4]),
                        [m2] + subs)
                    subs.append(sj)
                ex = P.emit("scalar", lambda e, o=gt: e.activation(
                    out=o[:, 4:16], in_=o[:, 4:16], func=AF.Exp), subs)
                sm = P.emit("vector", lambda e, o=gt: e.tensor_add(
                    o[:, 0:4], o[:, 4:16:3], o[:, 5:16:3]), [ex])
                sm2 = P.emit("vector", lambda e, o=gt: e.tensor_add(
                    o[:, 0:4], o[:, 0:4], o[:, 6:16:3]), [sm])
                rc = P.emit("vector", lambda e, o=gt: e.reciprocal(
                    o[:, 0:4], o[:, 0:4]), [sm2])
                mls = []
                for j in range(3):
                    mj = P.emit("vector", lambda e, o=gt, jj=j: e.tensor_mul(
                        o[:, 4 + jj:16:3], o[:, 4 + jj:16:3], o[:, 0:4]),
                        [rc] + mls)
                    mls.append(mj)
                ptw, pdw = self.pf.acquire()
                mm = None
                for k in range(8):
                    mm = P.emit("tensor",
                                lambda e, p=ptw, kk=k, c2=lcol:
                                e.matmul(p[:, 0:H], lhsT=xbf[kk][:, c2],
                                         rhs=wg_sb[:, kk, :],
                                         start=(kk == 0), stop=False),
                                [wg_h, xbf_h[k]] + ([pdw] if k == 0 else [mm]))
                mm = P.emit("tensor", lambda e, p=ptw: e.matmul(
                    p[:, 0:H], lhsT=ones1_bf[0:1, :], rhs=bg_bf[0:1, :],
                    start=False, stop=True), [bg_h, o1b_h, mm])
                wgh = P.emit("scalar", lambda e, p=ptw, o=gt: e.activation(
                    out=o[:, 0:4], in_=p[:, 0:H], func=AF.Sigmoid),
                    [mm, rc] + mls)
                self.pf.release(ptw, wgh)
                bg_done += [wgh, bh] + mls
            gds = []
            for ch2 in range(4 if not os.environ.get("DEV_L1_NOGATES") else 0):
                ptt, ptt_d = self.pf.acquire()
                tg2 = P.emit("tensor", lambda e, p=ptt, g2=gates_tm[ch2]:
                             e.transpose(p[0:16, 0:128], in_=g2[:], identity=idf2[:]),
                             bg_done + [idf2_h, ptt_d])
                evt = P.emit("vector", lambda e, p=ptt, c2=ch2: e.tensor_copy(
                    gtr4[:, c2, :], p[0:16, 0:128]), [tg2])
                self.pf.release(ptt, evt)
                gds.append(evt)

            for tname in ("q", "k", "v"):
                dram = {"q": wq_d, "k": wk_d, "v": wv_d}[tname]
                cw, cw_h = {"q": (cqc, cq_h), "k": (ckc, ck_h),
                            "v": (cvc, cv_h)}[tname]
                proj_h = []
                for half2 in range(4):
                    wbuf_, wh_, bidx = stream_w(dram, 256 * half2)
                    for oc in range(2 * half2, 2 * half2 + 2):
                        th = P.emit("vector",
                                    lambda e, d2=pad[oc], t2=tails[tname][oc]:
                                    e.tensor_copy(d2[:, 0:3], t2[:]),
                                    [s_init, tens_h.get(("padr", oc)),
                                     tens_h.get(("padc", oc))])
                        pt, pd = self.pf.acquire()
                        mm = None
                        for k in range(8):
                            mm = P.emit(
                                "tensor",
                                lambda e, p=pt, kk=k, o=oc - 2 * half2, w=wbuf_:
                                e.matmul(p[:, 0:SL],
                                         lhsT=w[:, kk, 128 * o:128 * (o + 1)],
                                         rhs=xbf[kk][:],
                                         start=(kk == 0), stop=(kk == 7)),
                                [wh_, xbf_h[k]] + ([pd] if k == 0 else [mm]))
                        peng = "scalar" if oc % 2 == 0 else "vector"
                        if peng == "scalar":
                            pfn = (lambda e, p=pt, d2=pad[oc]:
                                   e.copy(d2[:, 3:3 + SL], p[:, 0:SL]))
                        else:
                            pfn = (lambda e, p=pt, d2=pad[oc]:
                                   e.tensor_copy(d2[:, 3:3 + SL], p[:, 0:SL]))
                        ph = P.emit(peng, pfn, [mm, th])
                        self.pf.release(pt, ph)
                        proj_h.append((th, ph))
                        wslot["cons"][bidx] = mm
                if PRB == "praw" and tname == "q":
                    for i in range(8):
                        P.emit("vector", lambda e, s3=x[i], o=pad[i], c=cs:
                               e.tensor_copy(s3[:, c], o[:, 3:3 + SL]),
                               [proj_h[i][1], self.last_x[i]])
                    break
                for oc in range(8):
                    th, ph = proj_h[oc]
                    tsv = P.emit("vector", lambda e, d2=pad[oc],
                                 t2=tails[tname][oc]:
                                 e.tensor_copy(t2[:], d2[:, SL:SL + 3]), [ph])
                    pt, pd = self.pf.acquire()
                    mm = None
                    for tap in range(4):
                        dt2 = dtile[tap % 2]
                        bh2 = P.emit("vector",
                                     lambda e, d2=dt2, w=cw, o=oc, t2=tap:
                                     e.tensor_scalar(out=d2[:], in0=self.idty_bf[:],
                                                     scalar1=w[:, o, t2:t2 + 1],
                                                     scalar2=None, op0=ALU.mult),
                                     [cw_h, self.cr[5], dt_use[tap % 2]])
                        mm = P.emit("tensor",
                                    lambda e, p=pt, d2=dt2, s2=pad[oc], t2=tap:
                                    e.matmul(p[:, 0:SL], lhsT=d2[:],
                                             rhs=s2[:, t2:t2 + SL],
                                             start=(t2 == 0), stop=(t2 == 3)),
                                    [bh2, ph, th] + ([pd] if tap == 0 else [mm]))
                        dt_use[tap % 2] = mm
                    if tname == "v":
                        vh = P.emit("vector", lambda e, d2=vsil[oc], t2=vtails[oc]:
                                    e.tensor_copy(d2[:, 0:30], t2[:]),
                                    [s_init, tens_h.get(("vsr", oc))])
                        sh2 = P.emit("scalar", lambda e, p=pt, d2=vsil[oc]:
                                     e.activation(out=d2[:, 30:30 + SL],
                                                  in_=p[:, 0:SL], func=AF.Silu),
                                     [mm, vh])
                        tens_h[("vsave", oc)] = P.emit(
                            "vector", lambda e, d2=vsil[oc], t2=vtails[oc]:
                            e.tensor_copy(t2[:], d2[:, SL:SL + 30]), [sh2])
                    else:
                        dst2 = qn[oc] if tname == "q" else kn[oc]
                        sh2 = P.emit("scalar", lambda e, p=pt, d2=dst2:
                                     e.activation(out=d2[:], in_=p[:, 0:SL],
                                                  func=AF.Silu),
                                     [mm, tens_h.get(("sil", tname, oc))])
                    tens_h[("sil", tname, oc)] = sh2
                    tens_h[("padr", oc)] = tsv
                    tens_h[("padc", oc)] = mm
                    self.pf.release(pt, sh2)

            if PRB == "praw":
                bars = P.barrier()
                self.last_x = [bars["vector"]] * 8
                continue
            if PRB in ("qsil", "ksil"):
                tiles2 = qn if PRB == "qsil" else kn
                probe(tiles2, cs, lambda i: tens_h[("sil", PRB[0], i)])
                bars = P.barrier()
                self.last_x = [bars["vector"]] * 8
                continue
            for tname, tiles in (("q", qn), ("k", kn)):
                for h in range(H):
                    t0, t1 = 2 * h, 2 * h + 1
                    pt, pd = self.pf.acquire()
                    mm = None
                    for i2, ti in enumerate((t0, t1)):
                        sq2 = P.emit("scalar", lambda e, s2=tiles[ti]:
                                     e.activation(out=sq_bf[:], in_=s2[:],
                                                  func=AF.Square),
                                     [tens_h[("sil", tname, ti)],
                                      tens_h.get("sqd"), mm])
                        mm = P.emit("tensor", lambda e, p=pt, i3=i2:
                                    e.matmul(p[0:1, 0:SL],
                                             lhsT=self.ones_col_bf[:, 0:1],
                                             rhs=sq_bf[:], start=(i3 == 0),
                                             stop=(i3 == 1)),
                                    [sq2, self.cr[1]] + ([pd] if i2 == 0 else []))
                        tens_h["sqd"] = mm
                    r1 = P.emit("scalar", lambda e, p=pt: e.activation(
                        out=l2row[:], in_=p[0:1, 0:SL], func=AF.Ln,
                        bias=eps6[0:1, 0:1]), [mm, e6_h, tens_h.get("l2u")])
                    r2 = P.emit("scalar", lambda e: e.activation(
                        out=l2rb[:], in_=l2row[:], func=AF.Exp, scale=-0.5), [r1])
                    self.pf.release(pt, r1)
                    rb = self.bcast_row(l2rb[0:1, :], SL, rbc[:],
                                        [r2, tens_h.get("rbcu")])
                    u1 = P.emit("vector", lambda e, s2=tiles[t0]:
                                e.tensor_mul(s2[:], s2[:], rbc[:]),
                                [rb, tens_h[("sil", tname, t0)]])
                    u2 = P.emit("vector", lambda e, s2=tiles[t1]:
                                e.tensor_mul(s2[:], s2[:], rbc[:]),
                                [rb, u1, tens_h[("sil", tname, t1)]])
                    tens_h[("sil", tname, t0)] = u1
                    tens_h[("sil", tname, t1)] = u2
                    tens_h["l2u"] = [u1, u2]
                    tens_h["rbcu"] = [u1, u2]

            if PRB in ("qn", "kn"):
                tiles2 = qn if PRB == "qn" else kn
                probe(tiles2, cs, lambda i: tens_h[("sil", PRB[0], i)])
                bars = P.barrier()
                self.last_x = [bars["vector"]] * 8
                continue
            for ch in range(int(os.environ.get("DEV_L1_NCH", 4))):
                lcol = slice(128 * ch, 128 * (ch + 1))

                def delta_head(h, si):
                    t0, t1 = 2 * h, 2 * h + 1
                    dsc2 = dsc_sets[si]
                    v_tm = vtm_sets[si]
                    k_tm = ktm_sets[si]
                    kb_tm = kbtm_sets[si]
                    tkey = lambda n2: (si, n2)
                    tms = []
                    for (srcg, sname, off, dst2) in ((vsil, "v", 30, v_tm),
                                                     (kn, "k", 0, k_tm)):
                        for i2, ti in enumerate((t0, t1)):
                            ptb2, pdb2 = self.pb.acquire()
                            if srcg is vsil:
                                src_ap = srcg[ti][:, off + 128 * ch:
                                                  off + 128 * (ch + 1)]
                                sdep = tens_h[("vsave", ti)]
                            else:
                                src_ap = srcg[ti][:, lcol]
                                sdep = tens_h[("sil", "k", ti)]
                            tr = P.emit("tensor", lambda e, p=ptb2, s2=src_ap:
                                        e.transpose(p[:, 0:128], in_=s2,
                                                    identity=self.idty_bf[:]),
                                        [sdep, pdb2, self.cr[5]])
                            ev = P.emit("scalar", lambda e, p=ptb2, d2=dst2, i3=i2:
                                        e.copy(d2[:, 128 * i3:128 * (i3 + 1)],
                                               p[:, 0:128]),
                                        [tr, tens_h.get(tkey(("tmu", sname, i2)))])
                            self.pb.release(ptb2, ev)
                            tms.append(ev)
                    yield
                    bcol = beta_tm[ch]
                    vb = P.emit("vector", lambda e, b2=bcol, hh=h, vt2=v_tm:
                                e.tensor_scalar(out=vt2[:], in0=vt2[:],
                                                scalar1=b2[:, hh:hh + 1],
                                                scalar2=None, op0=ALU.mult),
                                tms + bg_done + [tens_h.get(tkey("vtmu"))])
                    kb2 = P.emit("vector", lambda e, b2=bcol, hh=h, kt2=k_tm,
                                 kb3=kb_tm:
                                 e.tensor_scalar(out=kb3[:], in0=kt2[:],
                                                 scalar1=b2[:, hh:hh + 1],
                                                 scalar2=None, op0=ALU.mult),
                                 tms + bg_done + [tens_h.get(tkey("kbu"))])
                    pa, pa_d = self.pf.acquire()
                    mm = None
                    for i2, ti in enumerate((t0, t1)):
                        mm = P.emit("tensor",
                                    lambda e, p=pa, s2=kn[ti], i3=i2, c2=lcol:
                                    e.matmul(p[:, 0:128], lhsT=s2[:, c2],
                                             rhs=s2[:, c2], start=(i3 == 0),
                                             stop=(i3 == 1)),
                                    [tens_h[("sil", "k", ti)]]
                                    + ([pa_d] if i2 == 0 else []),
                                    sdeps=[mm] if mm else [])
                    nh = P.emit("vector", lambda e, p=pa, b2=bcol, hh=h, d2=dsc2:
                                e.scalar_tensor_tensor(
                                    out=d2["Nt"][:], in0=p[:, 0:128],
                                    scalar=b2[:, hh:hh + 1], in1=mneg_low,
                                    op0=ALU.mult, op1=ALU.mult),
                                [mm, dm_h, tens_h.get(tkey("Ntu"))] + bg_done)
                    self.pf.release(pa, nh)
                    yield
                    ptb2, pdb2 = self.pb.acquire()
                    trm = P.emit("tensor", lambda e, p=ptb2, d2=dsc2: e.transpose(
                        p[:, 0:128], in_=d2["Nt"][:], identity=self.idty_bf[:]),
                        [nh, pdb2, self.cr[5]])
                    mh = P.emit("scalar", lambda e, p=ptb2, d2=dsc2: e.copy(
                        d2["Mt"][:], p[:, 0:128]), [trm, tens_h.get(tkey("Mtu"))])
                    self.pb.release(ptb2, mh)
                    yield
                    pw_h = {"Nt": nh, "Mt": mh}
                    for (dstn, lt2, rt2) in (
                        ("N2", "Mt", "Nt"), ("M2", "Nt", "Mt"),
                        ("N4", "M2", "N2"), ("M4", "N2", "M2"),
                        ("N8", "M4", "N4"), ("M8", "N4", "M4"),
                        ("N16", "M8", "N8"), ("M16", "N8", "M8"),
                        ("N32", "M16", "N16"), ("M32", "N16", "M16"),
                        ("N64", "M32", "N32"),
                    ):
                        pp, pp_d = self.pf.acquire()
                        mm2 = P.emit("tensor",
                                     lambda e, p=pp, l2=dsc2[lt2], r2=dsc2[rt2]:
                                     e.matmul(p[:, 0:128], lhsT=l2[:], rhs=r2[:],
                                              start=True, stop=True),
                                     [pw_h[lt2], pw_h[rt2], pp_d])
                        ev2 = P.emit("scalar", lambda e, p=pp, d2=dsc2[dstn]:
                                     e.copy(d2[:], p[:, 0:128]),
                                     [mm2, tens_h.get(tkey((dstn, "u")))])
                        self.pf.release(pp, ev2)
                        pw_h[dstn] = ev2
                        yield
                    xh = P.emit("vector", lambda e, d2=dsc2: e.scalar_tensor_tensor(
                        out=d2["Xc"][:], in0=d2["Mt"][:], scalar=1.0,
                        in1=self.idty_bf[:], op0=ALU.mult, op1=ALU.add),
                        [mh, self.cr[5], tens_h.get(tkey("Xcu"))])
                    for j2 in ("N2", "N4", "N8", "N16", "N32", "N64"):
                        pp, pp_d = self.pf.acquire()
                        mm2 = P.emit("tensor", lambda e, p=pp, l2=dsc2[j2], d2=dsc2:
                                     e.matmul(p[:, 0:128], lhsT=l2[:],
                                              rhs=d2["Xc"][:],
                                              start=True, stop=False),
                                     [pw_h[j2], xh, pp_d])
                        mm2 = P.emit("tensor", lambda e, p=pp, d2=dsc2: e.matmul(
                            p[:, 0:128], lhsT=self.idty_bf[:], rhs=d2["Xc"][:],
                            start=False, stop=True), [xh, self.cr[5]],
                            sdeps=[mm2])
                        xh = P.emit("scalar", lambda e, p=pp, d2=dsc2: e.copy(
                            d2["Xc"][:], p[:, 0:128]), [mm2])
                        self.pf.release(pp, xh)
                        yield
                    sbf_readers = {t0: [], t1: []}
                    pw, pw_d = self.pf.acquire()
                    mmw = P.emit("tensor", lambda e, p=pw, d2=dsc2, kb3=kb_tm:
                                 e.matmul(p[:, 0:256], lhsT=d2["Xc"][:], rhs=kb3[:],
                                          start=True, stop=True), [kb2, xh, pw_d])
                    wn1 = P.emit("scalar", lambda e, p=pw, d2=dsc2: e.copy(
                        d2["w1"][:], p[:, 0:128]),
                        [mmw, tens_h.get(tkey(("w", 0)))])
                    wn2 = P.emit("scalar", lambda e, p=pw, d2=dsc2: e.copy(
                        d2["w2"][:], p[:, 128:256]),
                        [mmw, wn1, tens_h.get(tkey(("w", 1)))])
                    self.pf.release(pw, [wn1, wn2])
                    yield
                    wts = []
                    for i2 in range(2):
                        ptb3, pdb3 = self.pb.acquire()
                        trw = P.emit("tensor", lambda e, p=ptb3, i3=i2, d2=dsc2:
                                     e.transpose(p[:, 0:128],
                                                 in_=d2["w%d" % (i3 + 1)][:],
                                                 identity=self.idty_bf[:]),
                                     [wn1 if i2 == 0 else wn2, pdb3, self.cr[5]])
                        wtv = P.emit("scalar", lambda e, p=ptb3, i3=i2, d2=dsc2:
                                     e.copy(d2["w%d" % (i3 + 1)][:], p[:, 0:128]),
                                     [trw])
                        self.pb.release(ptb3, wtv)
                        wts.append(wtv)
                    yield
                    pws, pws_d = self.pf.acquire()
                    mws = None
                    for i2 in range(2):
                        mws = P.emit("tensor", lambda e, p=pws, i3=i2, d2=dsc2,
                                     sb3=Sbf[2 * h + i2]: e.matmul(
                            p[:, 0:256], lhsT=d2["w%d" % (i3 + 1)][:],
                            rhs=sb3[:],
                            start=(i3 == 0), stop=(i3 == 1)),
                            [wts[i2], sbf_h[2 * h + i2]] +
                            ([pws_d] if i2 == 0 else []),
                            sdeps=[mws] if mws else [])
                        sbf_readers[2 * h + i2].append(mws)
                    ws1 = P.emit("scalar", lambda e, p=pws, d2=dsc2: e.copy(
                        d2["ws1"][:], p[:, 0:128]),
                        [mws, tens_h.get(tkey(("ws", 0)))])
                    ws2 = P.emit("scalar", lambda e, p=pws, d2=dsc2: e.copy(
                        d2["ws2"][:], p[:, 128:256]),
                        [mws, ws1, tens_h.get(tkey(("ws", 1)))])
                    self.pf.release(pws, [ws1, ws2])
                    yield
                    pu, pu_d = self.pf.acquire()
                    mmu = P.emit("tensor", lambda e, p=pu, d2=dsc2, vt2=v_tm:
                                 e.matmul(p[:, 0:256], lhsT=d2["Xc"][:], rhs=vt2[:],
                                          start=True, stop=True), [xh, vb, pu_d])
                    uh1 = P.emit("vector", lambda e, p=pu, d2=dsc2: e.tensor_sub(
                        d2["u1"][:], p[:, 0:128], d2["ws1"][:]),
                        [mmu, ws1, tens_h.get(tkey("u1u"))])
                    uh2 = P.emit("vector", lambda e, p=pu, d2=dsc2: e.tensor_sub(
                        d2["u2"][:], p[:, 128:256], d2["ws2"][:]),
                        [mmu, uh1, ws2, tens_h.get(tkey("u2u"))])
                    self.pf.release(pu, [uh1, uh2])
                    yield
                    pat, pat_d = self.pf.acquire()
                    mm = None
                    for i2, ti in enumerate((t0, t1)):
                        mm = P.emit("tensor",
                                    lambda e, p=pat, kk=kn[ti], q2=qn[ti], c2=lcol,
                                    i3=i2:
                                    e.matmul(p[:, 0:128], lhsT=kk[:, c2],
                                             rhs=q2[:, c2], start=(i3 == 0),
                                             stop=(i3 == 1)),
                                    [tens_h[("sil", "k", ti)],
                                     tens_h[("sil", "q", ti)]]
                                    + ([pat_d] if i2 == 0 else []),
                                    sdeps=[mm] if mm else [])
                    ath = P.emit("vector", lambda e, p=pat, d2=dsc2: e.tensor_tensor(
                        out=d2["attn"][:], in0=p[:, 0:128], in1=m_up_incl,
                        op=ALU.mult), [mm, dm_h, tens_h.get(tkey("attnu"))])
                    self.pf.release(pat, ath)
                    yield
                    mo = None
                    for i2, ti in enumerate((t0, t1)):
                        po2, po2_d = self.pf.acquire()
                        mo = None
                        for j3, tj in enumerate((t0, t1)):
                            mo = P.emit("tensor",
                                        lambda e, p=po2, tj2=tj, i3=i2, c2=lcol:
                                        e.matmul(p[:, 0:128],
                                                 lhsT=Sbf[tj2][:, 128 * i3:
                                                               128 * (i3 + 1)],
                                                 rhs=qn[tj2][:, c2],
                                                 start=(j4 == 0), stop=False)
                                        if False else
                                        e.matmul(p[:, 0:128],
                                                 lhsT=Sbf[tj2][:, 128 * i3:
                                                               128 * (i3 + 1)],
                                                 rhs=qn[tj2][:, c2],
                                                 start=(tj2 == t0), stop=False),
                                        [sbf_h[tj], tens_h[("sil", "q", tj)]]
                                        + ([po2_d] if j3 == 0 else []),
                                        sdeps=[mo] if mo else [])
                        sbf_readers[t0].append(mo)
                        sbf_readers[t1].append(mo)
                        mo = P.emit("tensor",
                                    lambda e, p=po2, i3=i2, d2=dsc2:
                                    e.matmul(p[:, 0:128],
                                             lhsT=d2["u1" if i3 == 0 else "u2"][:],
                                             rhs=d2["attn"][:],
                                             start=False, stop=True),
                                    [uh1 if i2 == 0 else uh2, ath],
                                    sdeps=[mo])
                        me = P.emit("vector", lambda e, p=po2, d2=macc[ti], c2=lcol:
                                    e.tensor_copy(d2[:, c2], p[:, 0:128]),
                                    [mo, tens_h.get(("maccu", ti))])
                        tens_h[("maccu", ti)] = me
                        self.pf.release(po2, me)
                    yield
                    m2b = None
                    for i2, ti in enumerate((t0, t1)):
                        ps2, ps2_d = self.pf.acquire()
                        m1b = P.emit("tensor", lambda e, p=ps2, i3=i2, d2=dsc2,
                                     kt2=k_tm: e.matmul(
                            p[:, 0:128], lhsT=kt2[:, 128 * i3:128 * (i3 + 1)],
                            rhs=d2["u1"][:], start=True, stop=True),
                            [uh1, tms, ps2_d])
                        m2b = P.emit("tensor", lambda e, p=ps2, i3=i2, d2=dsc2,
                                     kt2=k_tm: e.matmul(
                            p[:, 128:256], lhsT=kt2[:, 128 * i3:128 * (i3 + 1)],
                            rhs=d2["u2"][:], start=True, stop=True),
                            [uh2, tms], sdeps=[m1b])
                        su = P.emit("vector", lambda e, p=ps2, s2=Sf[ti]:
                                    e.tensor_add(s2[:], s2[:], p[:, 0:256]),
                                    [m2b] + sf_h[ti])
                        sb2 = P.emit("vector", lambda e, s2=Sf[ti], d2=Sbf[ti]:
                                     e.tensor_copy(d2[:], s2[:]),
                                     [su] + sbf_h[ti] + sbf_readers[ti])
                        self.pf.release(ps2, su)
                        sf_h[ti] = [su]
                        sbf_h[ti] = [sb2]
                    late = [m2b, uh2, sbf_h[t1][0]]
                    for key2 in ("vtmu", "kbu", "Ntu", "Mtu", "attnu",
                                 "u1u", "u2u", "Xcu"):
                        tens_h[tkey(key2)] = late
                    for dstn in ("N2", "M2", "N4", "M4", "N8", "M8", "N16",
                                 "M16", "N32", "M32", "N64"):
                        tens_h[tkey((dstn, "u"))] = late
                    for i2 in range(2):
                        tens_h[tkey(("w", i2))] = late
                        tens_h[tkey(("ws", i2))] = late
                    for sname in ("v", "k"):
                        for i2 in range(2):
                            tens_h[tkey(("tmu", sname, i2))] = late

                gens = [delta_head(hh2, hh2) for hh2 in range(4)]
                alive = list(gens)
                while alive:
                    for g in list(alive):
                        try:
                            next(g)
                        except StopIteration:
                            alive.remove(g)

            if PRB == "delta":
                probe(macc, cs, lambda i: tens_h[("maccu", i)])
                bars = P.barrier()
                self.last_x = [bars["vector"]] * 8
                continue
            if PRB == "u2":
                bars = P.barrier()
                self.last_x = [bars["vector"]] * 8
                continue
            fl_use = {j: None for j in range(3)}
            for h in range(H):
                t0, t1 = 2 * h, 2 * h + 1
                pt_, pd_ = self.pf.acquire()
                mmb = None
                for ch in range(4):
                    mmb = P.emit("tensor", lambda e, p=pt_, c2=ch, hh=h:
                                 e.matmul(p[:, 128 * c2:128 * (c2 + 1)],
                                          lhsT=selg[:, hh, :], rhs=gtr4[:, c2, :],
                                          start=True, stop=True),
                                 [gds[ch], selg_h, pd_]
                                 + ([mmb] if mmb else []))
                bc0 = P.emit("vector", lambda e, p=pt_: e.tensor_copy(
                    gbc[0][:], p[:, 0:SL]), [mmb, tens_h.get("gbc0u")])
                bc1 = P.emit("vector", lambda e, p=pt_: e.tensor_scalar(
                    out=gbc[1][:], in0=p[:, 0:SL], scalar1=-1.0, scalar2=1.0,
                    op0=ALU.mult, op1=ALU.add), [mmb, tens_h.get("gbc1u")])
                self.pf.release(pt_, [bc0, bc1])
                pbs = []
                for j in range(3):
                    ptj, pdj = self.pf.acquire()
                    mmj = None
                    for ch in range(4):
                        mmj = P.emit("tensor", lambda e, p=ptj, c2=ch, jj=j, hh=h:
                                     e.matmul(p[:, 128 * c2:128 * (c2 + 1)],
                                              lhsT=selg[:, 4 + 3 * hh + jj, :],
                                              rhs=gtr4[:, c2, :],
                                              start=True, stop=True),
                                     [gds[ch], selg_h, pdj]
                                     + ([mmj] if mmj else []))
                    bj = P.emit("scalar", lambda e, p=ptj, jj=j: e.copy(
                        gbp[jj][:], p[:, 0:SL]), [mmj, tens_h.get(("gbpu", j))])
                    self.pf.release(ptj, bj)
                    pbs.append(bj)
                for ti in (t0, t1):
                    bld = {}
                    for hb, (tb, ntap, eng2) in enumerate(
                            ((0, 11, "vector"), (11, 10, "gpsimd"),
                             (21, 10, "vector"))):
                        bld[hb] = P.emit(
                            eng2,
                            lambda e, d2=flh[hb], o=ti, t3=tb, n3=ntap:
                            e.tensor_tensor(
                                out=d2[:, 0:n3, :],
                                in0=self.idty_bf.ap().unsqueeze(1)
                                .broadcast_to([128, n3, 128]),
                                in1=flw[:, o, t3:t3 + n3].unsqueeze(2)
                                .broadcast_to([128, n3, 128]),
                                op=ALU.mult),
                            [flw_h, self.cr[5], fl_use[hb]])
                    ptl, ptl_d = self.pf.acquire()
                    mm = None
                    for tap in range(31):
                        hb = 0 if tap < 11 else (1 if tap < 21 else 2)
                        off = (0, 11, 21)[hb]
                        mm = P.emit("tensor",
                                    lambda e, p=ptl, d2=flh[hb],
                                    t2=tap, o2=off, s2=vsil[ti]:
                                    e.matmul(p[:, 0:SL], lhsT=d2[:, t2 - o2, :],
                                             rhs=s2[:, t2:t2 + SL],
                                             start=(t2 == 0), stop=(t2 == 30)),
                                    [bld[hb], tens_h[("vsave", ti)]]
                                    + ([ptl_d] if tap == 0 else [mm]))
                        fl_use[hb] = mm
                    pts, pts_d = self.pf.acquire()
                    mm2 = None
                    for tap in range(3):
                        dt2 = dtile[tap % 2]
                        bh2 = P.emit("vector",
                                     lambda e, d2=dt2, t2=tap, o=ti:
                                     e.tensor_scalar(out=d2[:], in0=self.idty_bf[:],
                                                     scalar1=fsc[:, o, t2:t2 + 1],
                                                     scalar2=None, op0=ALU.mult),
                                     [fs_h, self.cr[5], dt_use[tap % 2]])
                        mm2 = P.emit("tensor",
                                     lambda e, p=pts, d2=dt2, t2=tap, s2=vsil[ti]:
                                     e.matmul(p[:, 0:SL], lhsT=d2[:],
                                              rhs=s2[:, 28 + t2:28 + t2 + SL],
                                              start=(t2 == 0), stop=(t2 == 2)),
                                     [bh2, tens_h[("vsave", ti)]]
                                     + ([pts_d] if tap == 0 else [mm2]))
                        dt_use[tap % 2] = mm2
                    e1 = P.emit("vector", lambda e, d2=macc[ti]: e.tensor_mul(
                        d2[:], d2[:], gbc[0][:]),
                        [bc0, tens_h.get(("maccu", ti))])
                    e2 = P.emit("vector", lambda e, p=ptl: e.tensor_mul(
                        mtmp[:], p[:, 0:SL], gbp[2][:]),
                        [mm, pbs[2], tens_h.get("mtmpu")])
                    self.pf.release(ptl, e2)
                    e3 = P.emit("vector", lambda e, p=pts: e.tensor_mul(
                        mtmp2[:], p[:, 0:SL], gbp[1][:]),
                        [mm2, pbs[1], tens_h.get("mtmp2u")])
                    self.pf.release(pts, e3)
                    e4 = P.emit("vector", lambda e: e.tensor_add(
                        mtmp[:], mtmp[:], mtmp2[:]), [e2, e3])
                    e5 = P.emit("vector", lambda e, d2=vsil[ti]: e.tensor_mul(
                        mtmp2[:], d2[:, 30:30 + SL], gbp[0][:]),
                        [pbs[0], tens_h[("vsave", ti)], e4])
                    e6 = P.emit("vector", lambda e: e.tensor_add(
                        mtmp[:], mtmp[:], mtmp2[:]), [e4, e5])
                    e7 = P.emit("vector", lambda e: e.tensor_mul(
                        mtmp[:], mtmp[:], gbc[1][:]), [e6, bc1])
                    e8 = P.emit("vector", lambda e, d2=macc[ti]: e.tensor_add(
                        d2[:], d2[:], mtmp[:]), [e7, e1])
                    tens_h[("maccu", ti)] = e8
                    tens_h["mtmpu"] = e7
                    tens_h["mtmp2u"] = e6
                    tens_h[("gbpu", 0)] = e5
                    tens_h[("gbpu", 1)] = e3
                    tens_h[("gbpu", 2)] = e2
                    tens_h["gbc0u"] = e1
                    tens_h["gbc1u"] = e7
                pt, pd = self.pf.acquire()
                mm = None
                for i2, ti in enumerate((t0, t1)):
                    sq2 = P.emit("scalar", lambda e, d2=macc[ti]:
                                 e.activation(out=sq_bf[:], in_=d2[:],
                                              func=AF.Square),
                                 [tens_h[("maccu", ti)], tens_h.get("sqd"), mm])
                    mm = P.emit("tensor", lambda e, p=pt, i3=i2:
                                e.matmul(p[0:1, 0:SL],
                                         lhsT=self.ones_col_bf[:, 0:1],
                                         rhs=sq_bf[:], start=(i3 == 0),
                                         stop=(i3 == 1)),
                                [sq2, self.cr[1]] + ([pd] if i2 == 0 else []))
                    tens_h["sqd"] = mm
                r1 = P.emit("scalar", lambda e, p=pt: e.activation(
                    out=l2row[:], in_=p[0:1, 0:SL], func=AF.Ln, scale=1.0 / DV,
                    bias=self.eps_col[0:1, 0:1]), [mm, self.cr[4],
                                                   tens_h.get("l2u")])
                r2 = P.emit("scalar", lambda e: e.activation(
                    out=l2rb[:], in_=l2row[:], func=AF.Exp, scale=-0.5), [r1])
                self.pf.release(pt, r1)
                rb = self.bcast_row(l2rb[0:1, :], SL, rbc[:],
                                    [r2, tens_h.get("rbcu")])
                us = []
                for ti in (t0, t1):
                    ua = P.emit("vector", lambda e, d2=macc[ti]: e.tensor_mul(
                        d2[:], d2[:], rbc[:]), [rb, tens_h[("maccu", ti)]] + us)
                    ub = P.emit("vector", lambda e, d2=macc[ti], o=osb[ti], t2=ti:
                                e.tensor_scalar(out=o[:], in0=d2[:],
                                                scalar1=rms_sb[:, t2:t2 + 1],
                                                scalar2=None, op0=ALU.mult),
                                [ua, rms_h, tens_h.get(("osbu", ti))])
                    tens_h[("maccu", ti)] = ub
                    tens_h[("osbu", ti)] = ub
                    us.append(ub)
                tens_h["l2u"] = us
                tens_h["rbcu"] = us

            if PRB == "mix":
                probe(macc, cs, lambda i: tens_h[("maccu", i)])
                bars = P.barrier()
                self.last_x = [bars["vector"]] * 8
                continue
            for half2 in range(4):
                wo_b, wo_hh, wo_bidx = stream_w(wo_d, 256 * half2)
                for oc in range(2 * half2, 2 * half2 + 2):
                    pt, pd = self.pf.acquire()
                    mm = None
                    for k in range(8):
                        mm = P.emit(
                            "tensor",
                            lambda e, p=pt, kk=k, o=oc - 2 * half2, w=wo_b:
                            e.matmul(p[:, 0:SL],
                                     lhsT=w[:, kk, 128 * o:128 * (o + 1)],
                                     rhs=osb[kk][:],
                                     start=(kk == 0), stop=(kk == 7)),
                            [wo_hh, tens_h[("osbu", k)]]
                            + ([pd] if k == 0 else [mm]))
                    ev = P.emit("vector", lambda e, p=pt, s3=x[oc], c=cs:
                                e.tensor_add(s3[:, c], s3[:, c], p[:, 0:SL]),
                                [mm, self.last_x[oc]])
                    self.last_x[oc] = ev
                    self.pf.release(pt, ev)
                    wslot["cons"][wo_bidx] = mm

            bars = P.barrier()
            self.last_x = [bars["vector"]] * 8
            for i in range(8):
                sf_h[i] = [bars["vector"]]
                sbf_h[i] = [bars["vector"]]
        sc.close()

def host_consts():
    idx = np.arange(128)
    mneg_low = -(idx[None, :] < idx[:, None]).astype(np.float32)
    m_up_incl = (idx[:, None] <= idx[None, :]).astype(np.float32)
    dmask = np.concatenate([mneg_low, m_up_incl], 0).astype(np.float32)
    idty = np.eye(128, dtype=np.float32)
    idx = np.arange(128)
    b8 = idx // 8
    m8d = ((b8[None, :] == b8[:, None]) & (idx[None, :] >= idx[:, None]))
    m8 = (b8[None, :] == b8[:, None] + 1) & (b8[None, :] % 2 == 1)
    b16 = idx // 16
    m16 = (b16[None, :] == b16[:, None] + 1) & (b16[None, :] % 2 == 1)
    gmask = np.concatenate([m8d, m8, m16], 0).astype(np.float32)
    selg = np.zeros((16, 16 * 128), np.float32)
    for r in range(16):
        selg[r, 128 * r:128 * (r + 1)] = 1.0
    return {"c_idty": idty, "c_gmask": gmask, "c_dmask": dmask, "c_selg": selg}


def build_program(blocks):
    b = Builder()
    if "l0a" in blocks:
        b.emit_l0_attn()
        if not os.environ.get("DEV_NOLN"):
            b.emit_postln("l0a")
    if "l0f" in blocks:
        b.emit_l0_ffn()
        b.emit_postln("l0f")
    if "l1a" in blocks:
        b.emit_l1_attn()
        if not os.environ.get("DEV_L1_PROBE"):
            b.emit_postln("l1a")
    if "l1f" in blocks:
        b.emit_l1_ffn()
        b.emit_postln("l1f")
    counts = b.finish()
    return b, counts


def _fl_diag(fl):
    flr = fl.reshape(D, 31)
    out = np.zeros((31, 8, 128, 128), np.float32)
    p = np.arange(128)
    for t in range(31):
        for o in range(8):
            out[t, o, p, p] = flr[o * 128 + p, t]
    return out.reshape(31 * 8 * 128, 128)


def host_inputs(inp, core):
    f32 = lambda k: np.ascontiguousarray(np.asarray(inp[k], np.float32))
    m = {
        "l0_wq": f32("L0_attn_Wq"), "l0_wf": f32("L0_attn_Wf"),
        "l0_wi": f32("L0_attn_Wi"), "l0_wo": f32("L0_attn_Wo"),
        "l0f_wg": f32("L0_ffn_Wg"), "l0f_we": f32("L0_ffn_We"),
        "l0f_be": f32("L0_ffn_be").reshape(1, D),
        "presel": (np.arange(8) < core).astype(np.float32).reshape(8, 1),
        "l1_wq": f32("L1_attn_Wq"), "l1_wk": f32("L1_attn_Wk"),
        "l1_wv": f32("L1_attn_Wv"), "l1_wo": f32("L1_attn_Wo"),
        "l1_wb": f32("L1_attn_Wb"), "l1_wg": f32("L1_attn_Wg"),
        "l1_wl": f32("L1_attn_Wl"),
        "l1_bg": f32("L1_attn_bg").reshape(1, 4),
        "l1_bl": f32("L1_attn_bl").reshape(1, 12),
        "l1_lt": np.repeat(np.asarray(inp["L1_attn_log_temp"], np.float32), 3)
        .reshape(1, 12).copy(),
        "l1_rms": np.ascontiguousarray(
            np.tile(np.asarray(inp["L1_attn_rms_w"], np.float32), 4)
            .reshape(8, 128).T),
        "l1_cq": f32("L1_attn_conv_q"), "l1_ck": f32("L1_attn_conv_k"),
        "l1_cv": f32("L1_attn_conv_v"),
        "l1_fs": np.asarray(inp["L1_attn_fir_short"], np.float32)
        .reshape(D, 3).copy(),
        "l1_flw": np.ascontiguousarray(
            np.asarray(inp["L1_attn_fir_long"], np.float32).reshape(8, 128, 31)
            .transpose(1, 0, 2).reshape(128, 8 * 31)),
        "l1f_w1": f32("L1_ffn_w1"),
        "l1f_b1": f32("L1_ffn_b1").reshape(1, 4096),
        "l1f_w2": f32("L1_ffn_w2"),
        "l1f_b2": f32("L1_ffn_b2").reshape(1, D),
    }
    return m


_BUILD = None


def kernel(**inputs):
    global _BUILD
    if _BUILD is None:
        _BUILD = build_program(("l0a", "l0f", "l1a", "l1f"))[0]
    b = _BUILD
    x = np.ascontiguousarray(np.asarray(inputs["L0_attn_hidden_states"], np.float32))
    consts = host_consts()
    in_maps = []
    for c in range(8):
        m = {"xT": np.ascontiguousarray(x[c].T)}
        m.update(consts)
        m.update(host_inputs(inputs, c))
        m = {k2: m[k2] for k2 in b.inputs}
        in_maps.append(m)
    res = run_bass_kernel_spmd(b.P.nc, in_maps, core_ids=list(range(8)))
    out = np.stack([np.asarray(res.results[c]["outT"]).T for c in range(8)])
    return np.ascontiguousarray(out.astype(np.float32))
